# version 07
# baseline (speedup 1.0000x reference)
import contextlib
import numpy as np

import concourse.bass as bass
import concourse.mybir as mybir
from concourse.bass_utils import run_bass_kernel_spmd
from concourse import library_config
from concourse import library_overlay

F32 = mybir.dt.float32
BF16 = mybir.dt.bfloat16
AF = mybir.ActivationFunctionType
ALU = mybir.AluOpType

S = 2048
D = 1024
KT = 8
H = 16
HD = 64
NG = 8
LNC = 256
UC = 512
PC = 256


class Prog:

    def __init__(self, nc):
        self.nc = nc
        self.eng = {"t": nc.tensor, "v": nc.vector, "s": nc.scalar,
                    "g": nc.gpsimd, "d": nc.sync, "gd": nc.gpsimd}
        self.cnt = {k: 0 for k in self.eng}
        self.sem = {}
        self.waited = {k: {} for k in self.eng}
        self.res = {}
        self._es = contextlib.ExitStack()

    def init_sems(self):
        for k in self.eng:
            self.sem[k] = self._es.enter_context(self.nc.semaphore("sem_" + k))

    def _wait(self, stream, tok):
        src, val = tok
        if src == stream and src not in ("d", "gd"):
            pass
        if self.waited[stream].get(src, -1) >= val:
            return
        self.eng[stream].wait_ge(self.sem[src], val)
        self.waited[stream][src] = val

    def emit(self, stream, fn, reads=(), writes=(), inc=1):
        toks = []
        for r in reads:
            st = self.res.get(r)
            if st and st["w"] is not None:
                toks.append(st["w"])
        for w in writes:
            st = self.res.get(w)
            if st:
                if st["w"] is not None:
                    toks.append(st["w"])
                toks.extend(st["r"])
        for tok in sorted(set(toks)):
            if tok[0] == stream and tok[1] <= self.cnt[stream]:
                self._wait(stream, tok)
            else:
                self._wait(stream, tok)
        instr = fn(self.eng[stream])
        instr.then_inc(self.sem[stream], inc)
        self.cnt[stream] += inc
        tok = (stream, self.cnt[stream])
        for r in reads:
            st = self.res.setdefault(r, {"w": None, "r": []})
            st["r"].append(tok)
        for w in writes:
            self.res[w] = {"w": tok, "r": []}
        return tok

    def barrier(self):
        snap = dict(self.cnt)
        for stream in ("t", "v", "s", "g", "d"):
            for src, val in snap.items():
                if src != stream and val > 0:
                    self._wait(stream, (src, val))
        self.res = {}


def ceil_div(a, b):
    return -(-a // b)


def jlist(qc):
    out = []
    for j in range(16):
        s = 128 * j - 256 * qc
        if s <= -640 or s >= 768:
            continue
        mask = None
        if s == -512:
            mask = "ml0"
        elif s == -384:
            mask = "ml1"
        elif s == 512:
            mask = "mu0"
        elif s == 640:
            mask = "mu1"
        out.append((j, mask))
    return out


def build_program(n_sub=6, l1a_stage=99):
    nc = bass.Bass("TRN2", target_bir_lowering=False)
    p = Prog(nc)

    dram = {}

    def din(name, shape, dtype=F32):
        dram[name] = nc.dram_tensor(name, list(shape), dtype, kind="ExternalInput").ap()
        return dram[name]

    din("xT", (KT, 128, S))
    dout = nc.dram_tensor("out", [KT, 128, S], F32, kind="ExternalOutput").ap()

    for n in ("wqT", "wkT", "wvT", "woT"):
        din(n, (KT, 128, D))
    for n in ("bq", "bk", "bo"):
        din(n, (128, 8))
    din("bvrow", (1, D))
    din("cw1", (8, 128, 1024)); din("cw2", (8, 128, 512))
    din("cw3", (4, 128, 256)); din("cw4", (128, 2))
    din("gw1", (8, 128, 512)); din("gw2", (4, 128, 256)); din("gw3", (128, 2))
    din("cb1", (128, 8)); din("cb2", (128, 4)); din("cb3", (128, 2)); din("cb4", (1, 1))
    din("gb1", (128, 4)); din("gb2", (128, 2)); din("gb3", (1, 1)); din("arr", (1, 1))
    din("wrT", (8, 128, 1024)); din("wkfT", (8, 128, 1024)); din("wvfT", (8, 128, 1024))
    din("mur", (128, 8)); din("muk", (128, 8))
    din("winT", (8, 128, 256)); din("bmT", (2, 128, 256))
    din("amat", (2, 128, 256)); din("amatT", (2, 128, 256)); din("cmT", (2, 128, 1024))
    din("binb", (128, 2)); din("ddv", (128, 8))

    es = contextlib.ExitStack()

    def sb(name, shape, dtype):
        return es.enter_context(nc.sbuf_tensor(name, list(shape), dtype))

    _uid = [0]

    def uname(name):
        _uid[0] += 1
        return "%s_u%d" % (name, _uid[0])

    with es:
        p.init_sems()
        nc.gpsimd.load_library(library_config.proxy)

        resid = sb("resid", (128, KT, S), F32)
        arena = sb("arena", (128, KT, 2064), BF16)
        xstat = sb("xstat", (128, KT, LNC), BF16)
        xsqst = sb("xsqst", (128, KT, LNC), BF16)
        fsraw = sb("fsraw", (128, 2048), F32)
        mrow = sb("mrow", (1, S), F32)[:, :]
        rsrow = sb("rsrow", (1, S), F32)[:, :]
        srow = sb("srow", (1, LNC), F32)[:, :]
        srow2 = sb("srow2", (1, LNC), F32)[:, :]
        mb = sb("mb", (128, 256), F32)
        rb = sb("rb", (128, 256), F32)
        t1 = sb("t1", (128, 256), F32)
        t2 = sb("t2", (128, 256), F32)
        cpl = sb("cpl", (128, PC), F32)
        wbig = sb("wbig", (128, KT, 2048), BF16)
        onesb = sb("onesb", (128, 1), BF16)
        onesm = sb("onesm", (1, 128), BF16)
        negones = sb("negones", (128, 1), BF16)
        negpad = sb("negpad", (1, 65), BF16)
        ident = sb("ident", (128, 128), BF16)
        masks = {n: sb(n, (128, 256), BF16) for n in ("ml0", "ml1", "mu0", "mu1")}
        bias = {}
        for n, shp in (("bq", (128, 8)), ("bk", (128, 8)), ("bo", (128, 8)),
                       ("cb1", (128, 8)), ("cb2", (128, 4)), ("cb3", (128, 2)),
                       ("cb4", (1, 1)), ("gb1", (128, 4)), ("gb2", (128, 2)),
                       ("gb3", (1, 1)), ("arr", (1, 1)), ("mur", (128, 8)),
                       ("muk", (128, 8)), ("binb", (128, 2)), ("ddv", (128, 8))):
            bias[n] = sb("b_" + n, shp, F32)
        bvrow = sb("bvrow_s", (1, D), BF16)
        scal = sb("scal", (1, 16), F32)

        pb = [es.enter_context(nc.psum_tensor("pb%d" % i, [128, 512], F32))
              for i in range(8)]

        xTsrc = dram["xT"].transpose([1, 0, 2])
        for kt in range(KT):
            p.emit("d", lambda e, kt=kt: e.dma_start(resid[:, kt, :], xTsrc[:, kt, :]),
                   writes=["resid"], inc=16)
        for n in bias:
            nm = n
            p.emit("d", lambda e, nm=nm: e.dma_start(bias[nm][:, :], dram[nm][:, :]),
                   writes=["bias"], inc=16)
        p.emit("gd", lambda e: e.dma_start(bvrow[:, :], dram["bvrow"][:, :]),
               writes=["bvrow"], inc=16)

        p.emit("v", lambda e: e.memset(onesb[:, :], 1.0), writes=["onesb"])
        p.emit("v", lambda e: e.memset(onesm[:, :], 1.0), writes=["onesm"])
        p.emit("v", lambda e: e.memset(negones[:, :], -1.0), writes=["negones"])
        p.emit("v", lambda e: e.memset(negpad[:, 0:64], -512.0), writes=["negpad"])
        p.emit("v", lambda e: e.memset(negpad[:, 64:65], 512.0), writes=["negpad"])
        p.emit("v", lambda e: e.memset(t1[:, 0:128], 1.0), writes=["t1"])
        p.emit("g", lambda e: e.affine_select(t1[:, 0:128], t1[:, 0:128], pattern=[[-1, 128]],
                                              base=0, channel_multiplier=1,
                                              compare_op=ALU.is_equal, fill=0.0),
               reads=["t1"], writes=["t1"])
        p.emit("s", lambda e: e.activation(ident[:, :], t1[:, 0:128], AF.Copy),
               reads=["t1"], writes=["ident"])
        p.emit("v", lambda e: e.memset(scal[:, 10:11], 1e-5), writes=["eps"])
        mspec = {"ml0": (0, 1, -1), "ml1": (128, 1, -1), "mu0": (0, -1, 1), "mu1": (-128, -1, 1)}
        for n, (base, cm, fc) in mspec.items():
            p.emit("v", lambda e: e.memset(t1[:, 0:256], 1.0), writes=["t1"])
            p.emit("g", lambda e, base=base, cm=cm, fc=fc: e.affine_select(
                t1[:, 0:256], t1[:, 0:256], pattern=[[fc, 256]], base=base,
                channel_multiplier=cm, compare_op=ALU.is_ge, fill=0.0),
                reads=["t1"], writes=["t1"])
            p.emit("s", lambda e, n=n: e.activation(masks[n][:, :], t1[:, 0:256], AF.Copy),
                   reads=["t1"], writes=[n])

        def load_weight(dst_ap, src_name, key, kt_tiles):
            src = dram[src_name].transpose([1, 0, 2])
            for kt in range(kt_tiles):
                p.emit("gd", lambda e, kt=kt: e.dma_start(dst_ap[:, kt, :], src[:, kt, :]),
                       writes=[key], inc=16)

        def ln_stats():
            for c in range(S // LNC):
                cs = c * LNC
                p.emit("s", lambda e, cs=cs: e.activation(xstat[:, :, :], resid[:, :, cs:cs + LNC], AF.Copy),
                       reads=["resid"], writes=["xstat"])
                p.emit("s", lambda e, cs=cs: e.activation(xsqst[:, :, :], resid[:, :, cs:cs + LNC], AF.Square),
                       reads=["resid"], writes=["xsqst"])
                for kt in range(KT):
                    p.emit("t", lambda e, kt=kt: e.matmul(pb[6][0:1, 0:LNC], onesb[:, :], xstat[:, kt, :],
                                                          start=(kt == 0), stop=(kt == KT - 1)),
                           reads=["onesb", "xstat"], writes=["pb6"])
                for kt in range(KT):
                    p.emit("t", lambda e, kt=kt: e.matmul(pb[7][0:1, 0:LNC], onesb[:, :], xsqst[:, kt, :],
                                                          start=(kt == 0), stop=(kt == KT - 1)),
                           reads=["onesb", "xsqst"], writes=["pb7"])
                p.emit("v", lambda e, cs=cs: e.tensor_scalar_mul(mrow[:, cs:cs + LNC], pb[6][0:1, 0:LNC], 1.0 / D),
                       reads=["pb6"], writes=["mrow"])
                p.emit("v", lambda e: e.tensor_scalar_mul(srow[:, :], pb[7][0:1, 0:LNC], 1.0 / D),
                       reads=["pb7"], writes=["srow"])
                p.emit("v", lambda e, cs=cs: e.tensor_tensor(srow2[:, :], mrow[:, cs:cs + LNC],
                                                             mrow[:, cs:cs + LNC], ALU.mult),
                       reads=["mrow"], writes=["srow2"])
                p.emit("v", lambda e: e.tensor_tensor(srow[:, :], srow[:, :], srow2[:, :], ALU.subtract),
                       reads=["srow", "srow2"], writes=["srow"])
                p.emit("s", lambda e: e.activation(srow2[:, :], srow[:, :], AF.Ln, bias=scal[:, 10:11]),
                       reads=["srow", "eps"], writes=["srow2"])
                p.emit("s", lambda e, cs=cs: e.activation(rsrow[:, cs:cs + LNC], srow2[:, :], AF.Exp, scale=-0.5),
                       reads=["srow2"], writes=["rsrow"])

        def bcast_mr(cs, n):
            p.emit("g", lambda e: e.partition_broadcast(mb[:, 0:n], mrow[:, cs:cs + n]),
                   reads=["mrow"], writes=["mb"])
            p.emit("g", lambda e: e.partition_broadcast(rb[:, 0:n], rsrow[:, cs:cs + n]),
                   reads=["rsrow"], writes=["rb"])

        def xnorm_apply(dst_ap_fn, dst_key, cs, n):
            for kt in range(KT):
                p.emit("v", lambda e, kt=kt: e.tensor_tensor(t1[:, 0:n], resid[:, kt, cs:cs + n],
                                                             mb[:, 0:n], ALU.subtract),
                       reads=["resid", "mb"], writes=["t1"])
                p.emit("v", lambda e, kt=kt: e.tensor_tensor(dst_ap_fn(kt), t1[:, 0:n], rb[:, 0:n], ALU.mult),
                       reads=["t1", "rb"], writes=[dst_key])

        def l0ffn():
            sub = contextlib.ExitStack()
            with sub:
                ln_stats()
                load_weight(wbig[:, :, 0:1024], "cw1", "wbig", 8)
                load_weight(wbig[:, :, 1024:1536], "cw2", "wbig", 8)
                load_weight(wbig[:, :, 1536:2048], "gw1", "wbig", 8)
                wsm = sub.enter_context(nc.sbuf_tensor(uname("wsm"), [128, 4, 512], BF16))
                w4s = sub.enter_context(nc.sbuf_tensor(uname("w4s"), [128, 4], BF16))
                load_weight(wsm[:, :, 0:256], "cw3", "wsm", 4)
                load_weight(wsm[:, 0:4, 256:512], "gw2", "wsm", 4)
                p.emit("gd", lambda e: e.dma_start(w4s[:, 0:2], dram["cw4"][:, :]),
                       writes=["w4s"], inc=16)
                p.emit("gd", lambda e: e.dma_start(w4s[:, 2:4], dram["gw3"][:, :]),
                       writes=["w4s"], inc=16)
                gxa = sub.enter_context(nc.sbuf_tensor(uname("gxa"), [128, KT, 1], F32))
                gxr = sub.enter_context(nc.sbuf_tensor(uname("gxr"), [128, KT, 1], F32))
                gxbf = sub.enter_context(nc.sbuf_tensor(uname("gxbf"), [128, KT, 1], BF16))
                gh1 = sub.enter_context(nc.sbuf_tensor(uname("gh1"), [128, 4, 1], BF16))
                gh2 = sub.enter_context(nc.sbuf_tensor(uname("gh2"), [128, 2, 1], BF16))

                p.emit("v", lambda e: e.memset(gxa[:, :, :], 0.0), writes=["gxa"])
                for c in range(S // LNC):
                    cs = c * LNC
                    bcast_mr(cs, LNC)
                    xnorm_apply(lambda kt: xstat[:, kt, :], "xstat", cs, LNC)
                    p.emit("v", lambda e: e.reduce_sum(gxr[:, :, :], xstat[:, :, :], mybir.AxisListType.X),
                           reads=["xstat"], writes=["gxr"])
                    p.emit("v", lambda e: e.tensor_tensor(gxa[:, :, :], gxa[:, :, :], gxr[:, :, :], ALU.add),
                           reads=["gxa", "gxr"], writes=["gxa"])
                p.emit("s", lambda e: e.activation(gxbf[:, :, :], gxa[:, :, :], AF.Copy),
                       reads=["gxa"], writes=["gxbf"])
                for mt in range(4):
                    for kt in range(KT):
                        p.emit("t", lambda e, mt=mt, kt=kt: e.matmul(
                            pb[2][:, mt:mt + 1], wbig[:, kt, 1536 + 128 * mt:1536 + 128 * mt + 128],
                            gxbf[:, kt, :], start=(kt == 0), stop=(kt == KT - 1)),
                            reads=["wbig", "gxbf"], writes=["pb2"])
                for mt in range(4):
                    p.emit("s", lambda e, mt=mt: e.activation(gh1[:, mt, :], pb[2][:, mt:mt + 1], AF.Gelu,
                                                              bias=bias["gb1"][:, mt:mt + 1], scale=1.0 / S),
                           reads=["pb2", "bias"], writes=["gh1"])
                for mt in range(2):
                    for kt in range(4):
                        p.emit("t", lambda e, mt=mt, kt=kt: e.matmul(
                            pb[3][:, mt:mt + 1], wsm[:, kt, 256 + 128 * mt:256 + 128 * mt + 128],
                            gh1[:, kt, :], start=(kt == 0), stop=(kt == 3)),
                            reads=["wsm", "gh1"], writes=["pb3"])
                for mt in range(2):
                    p.emit("s", lambda e, mt=mt: e.activation(gh2[:, mt, :], pb[3][:, mt:mt + 1], AF.Gelu,
                                                              bias=bias["gb2"][:, mt:mt + 1]),
                           reads=["pb3", "bias"], writes=["gh2"])
                for kt in range(2):
                    p.emit("t", lambda e, kt=kt: e.matmul(pb[4][0:1, 0:1], w4s[:, 2 + kt:3 + kt],
                                                          gh2[:, kt, :], start=(kt == 0), stop=(kt == 1)),
                           reads=["w4s", "gh2"], writes=["pb4"])
                p.emit("v", lambda e: e.tensor_scalar_mul(scal[:, 1:2], bias["gb3"][:, :], -1.0),
                       reads=["bias"], writes=["scal"])
                p.emit("s", lambda e: e.activation(scal[:, 2:3], pb[4][0:1, 0:1], AF.Exp,
                                                   scale=-1.0, bias=scal[:, 1:2]),
                       reads=["pb4", "scal"], writes=["scal2"])
                p.emit("v", lambda e: e.tensor_scalar_add(scal[:, 3:4], scal[:, 2:3], 1.0),
                       reads=["scal2"], writes=["scal3"])
                p.emit("v", lambda e: e.reciprocal(scal[:, 0:1], scal[:, 3:4]),
                       reads=["scal3"], writes=["scal0"])
                p.emit("v", lambda e: e.tensor_tensor(scal[:, 4:5], bias["arr"][:, :], scal[:, 0:1], ALU.mult),
                       reads=["bias", "scal0"], writes=["scal4"])
                p.emit("v", lambda e: e.tensor_scalar_mul(scal[:, 5:6], scal[:, 4:5], 1.4),
                       reads=["scal4"], writes=["scal5"])
                p.emit("v", lambda e: e.tensor_scalar(scal[:, 6:7], bias["arr"][:, :], -1.0, 0.1, ALU.mult, ALU.add),
                       reads=["bias"], writes=["scal6"])
                p.emit("v", lambda e: e.tensor_tensor(scal[:, 7:8], scal[:, 5:6], scal[:, 6:7], ALU.add),
                       reads=["scal5", "scal6"], writes=["c0"])
                p.emit("v", lambda e: e.tensor_scalar_mul(scal[:, 8:9], bias["arr"][:, :], 0.6),
                       reads=["bias"], writes=["c1"])
                p.emit("v", lambda e: e.tensor_scalar_mul(scal[:, 9:10], bias["cb4"][:, :], -1.0),
                       reads=["bias"], writes=["scal9"])

                op = fsraw[:, :].rearrange("p (k c) -> p k c", k=KT)
                obf = arena[:, :, 0:PC]
                h1 = arena[:, :, PC:2 * PC]
                h2 = arena[:, 0:4, 2 * PC:3 * PC]
                h3 = arena[:, 0:2, 3 * PC:4 * PC]
                pairs = [(0, 4, 1.0), (1, 5, 1.0), (2, 6, -1.0), (3, 7, -1.0)]
                for c in range(S // PC):
                    cs = c * PC
                    bcast_mr(cs, PC)
                    xnorm_apply(lambda kt: op[:, kt, :], "op", cs, PC)
                    p.emit("s", lambda e: e.activation(obf, op[:, :, :], AF.Copy),
                           reads=["op"], writes=["obf"])
                    for r in range(3):
                        for mt in range(KT):
                            bk = "pb%d" % (2 + mt % 2)
                            for kt in range(KT):
                                p.emit("t", lambda e, mt=mt, kt=kt: e.matmul(
                                    pb[2 + mt % 2][:, 0:PC], wbig[:, kt, 128 * mt:128 * mt + 128],
                                    obf[:, kt, :], start=(kt == 0), stop=(kt == KT - 1)),
                                    reads=["wbig", "obf"], writes=[bk])
                            p.emit("s", lambda e, mt=mt: e.activation(
                                h1[:, mt, :], pb[2 + mt % 2][:, 0:PC], AF.Gelu,
                                bias=bias["cb1"][:, mt:mt + 1]),
                                reads=[bk, "bias"], writes=["h1"])
                        for mt in range(4):
                            bk = "pb%d" % (4 + mt % 2)
                            for kt in range(KT):
                                p.emit("t", lambda e, mt=mt, kt=kt: e.matmul(
                                    pb[4 + mt % 2][:, 0:PC], wbig[:, kt, 1024 + 128 * mt:1024 + 128 * mt + 128],
                                    h1[:, kt, :], start=(kt == 0), stop=(kt == KT - 1)),
                                    reads=["wbig", "h1"], writes=[bk])
                            p.emit("s", lambda e, mt=mt: e.activation(
                                h2[:, mt, :], pb[4 + mt % 2][:, 0:PC], AF.Gelu,
                                bias=bias["cb2"][:, mt:mt + 1]),
                                reads=[bk, "bias"], writes=["h2"])
                        for mt in range(2):
                            bk = "pb%d" % (6 + mt % 2)
                            for kt in range(4):
                                p.emit("t", lambda e, mt=mt, kt=kt: e.matmul(
                                    pb[6 + mt % 2][:, 0:PC], wsm[:, kt, 128 * mt:128 * mt + 128],
                                    h2[:, kt, :], start=(kt == 0), stop=(kt == 3)),
                                    reads=["wsm", "h2"], writes=[bk])
                            p.emit("s", lambda e, mt=mt: e.activation(
                                h3[:, mt, :], pb[6 + mt % 2][:, 0:PC], AF.Gelu,
                                bias=bias["cb3"][:, mt:mt + 1]),
                                reads=[bk, "bias"], writes=["h3"])
                        for kt in range(2):
                            p.emit("t", lambda e, kt=kt: e.matmul(
                                pb[4][0:1, 0:PC], w4s[:, kt:kt + 1], h3[:, kt, :],
                                start=(kt == 0), stop=(kt == 1)),
                                reads=["w4s", "h3"], writes=["pb4"])
                        p.emit("s", lambda e: e.activation(srow[:, 0:PC], pb[4][0:1, 0:PC], AF.Exp,
                                                           scale=-1.0, bias=scal[:, 9:10]),
                               reads=["pb4", "scal9"], writes=["srow"])
                        p.emit("v", lambda e: e.tensor_scalar_add(srow[:, 0:PC], srow[:, 0:PC], 1.0),
                               reads=["srow"], writes=["srow"])
                        p.emit("v", lambda e: e.reciprocal(srow2[:, 0:PC], srow[:, 0:PC]),
                               reads=["srow"], writes=["srow2"])
                        p.emit("v", lambda e: e.tensor_scalar(srow[:, 0:PC], srow2[:, 0:PC],
                                                              scal[:, 8:9], scal[:, 7:8], ALU.mult, ALU.add),
                               reads=["srow2", "c0", "c1"], writes=["srow"])
                        p.emit("g", lambda e: e.partition_broadcast(cpl[:, :], srow[:, 0:PC]),
                               reads=["srow"], writes=["cpl"])
                        for (a, b, sa) in pairs:
                            p.emit("v", lambda e, b=b: e.tensor_tensor(t1[:, 0:PC], cpl[:, :], op[:, b, :], ALU.mult),
                                   reads=["cpl", "op"], writes=["t1"])
                            p.emit("v", lambda e, a=a: e.tensor_tensor(t2[:, 0:PC], cpl[:, :], op[:, a, :], ALU.mult),
                                   reads=["cpl", "op"], writes=["t2"])
                            p.emit("v", lambda e, a=a, sa=sa: e.tensor_tensor(
                                op[:, a, :], op[:, a, :], t1[:, 0:PC], ALU.add if sa > 0 else ALU.subtract),
                                reads=["op", "t1"], writes=["op"])
                            p.emit("v", lambda e, b=b, sa=sa: e.tensor_tensor(
                                op[:, b, :], op[:, b, :], t2[:, 0:PC], ALU.subtract if sa > 0 else ALU.add),
                                reads=["op", "t2"], writes=["op"])
                        if r < 2:
                            p.emit("s", lambda e: e.activation(obf, op[:, :, :], AF.Copy),
                                   reads=["op"], writes=["obf"])
                    for kt in range(KT):
                        p.emit("v", lambda e, kt=kt, cs=cs: e.scalar_tensor_tensor(
                            resid[:, kt, cs:cs + PC], op[:, kt, :], 0.5, resid[:, kt, cs:cs + PC],
                            ALU.mult, ALU.add),
                            reads=["op", "resid"], writes=["resid"])
            p.barrier()

        def l0attn():
            sub = contextlib.ExitStack()
            with sub:
                ln_stats()
                load_weight(wbig[:, :, 0:1024], "wqT", "wbig", 8)
                load_weight(wbig[:, :, 1024:2048], "wkT", "wbig", 8)
                xnorm = arena[:, :, 0:2048]
                for c in range(S // LNC):
                    cs = c * LNC
                    bcast_mr(cs, LNC)
                    xnorm_apply(lambda kt, cs=cs: xnorm[:, kt, cs:cs + LNC], "xnorm", cs, LNC)
                qb = sub.enter_context(nc.sbuf_tensor(uname("qb"), [128, S], BF16))
                kb = sub.enter_context(nc.sbuf_tensor(uname("kb"), [128, S], BF16))
                vaug = sub.enter_context(nc.sbuf_tensor(uname("vaug"), [128, 16, 2, 65], BF16))
                probs = sub.enter_context(nc.sbuf_tensor(uname("probs"), [128, 10, 256], BF16))
                ao = sub.enter_context(nc.sbuf_tensor(uname("ao"), [128, 256], BF16))
                epadb = sub.enter_context(nc.sbuf_tensor(uname("epadb"), [1, 2, 1024], BF16))
                recb = sub.enter_context(nc.sbuf_tensor(uname("recb"), [1, 256], BF16))
                recf = sub.enter_context(nc.sbuf_tensor(uname("recf"), [1, 256], F32))
                wvg = sub.enter_context(nc.sbuf_tensor(uname("wvg"), [128, KT, 128], BF16))
                wog = sub.enter_context(nc.sbuf_tensor(uname("wog"), [128, 1024], BF16))

                wvT = dram["wvT"]
                woT = dram["woT"]

                def fetch_group(g):
                    srcv = wvT[:, :, 128 * g:128 * g + 128].transpose([1, 0, 2])
                    for kt in range(KT):
                        p.emit("gd", lambda e, kt=kt: e.dma_start(wvg[:, kt, :], srcv[:, kt, :]),
                               writes=["wvg"], inc=16)
                    p.emit("gd", lambda e, g=g: e.dma_start(wog[:, :], woT[g, :, :]),
                           writes=["wog"], inc=16)

                for g in range(NG):
                    fetch_group(g)
                    for dst, wofs, bn, dk in ((qb, 0, "bq", "qb"), (kb, 1024, "bk", "kb")):
                        for nb in range(4):
                            bk_ = "pb%d" % (6 + nb % 2)
                            for kt in range(KT):
                                p.emit("t", lambda e, nb=nb, kt=kt, wofs=wofs, g=g: e.matmul(
                                    pb[6 + nb % 2][:, 0:512],
                                    wbig[:, kt, wofs + 128 * g:wofs + 128 * g + 128],
                                    xnorm[:, kt, 512 * nb:512 * nb + 512],
                                    start=(kt == 0), stop=(kt == KT - 1)),
                                    reads=["wbig", "xnorm"], writes=[bk_])
                            p.emit("s", lambda e, nb=nb, dst=dst, bn=bn, g=g: e.activation(
                                dst[:, 512 * nb:512 * nb + 512], pb[6 + nb % 2][:, 0:512],
                                AF.Identity, bias=bias[bn][:, g:g + 1]),
                                reads=[bk_, "bias"], writes=[dk])
                    for hl in range(2):
                        for half, q0 in ((0, 0), (1, 1536)):
                            p.emit("t", lambda e, hl=hl, q0=q0: e.matmul(
                                pb[6][0:1, 0:512], negones[64 * hl:64 * hl + 64, :],
                                qb[64 * hl:64 * hl + 64, q0:q0 + 512], start=True, stop=True),
                                reads=["qb", "negones"], writes=["pb6"])
                            p.emit("s", lambda e, hl=hl, half=half: e.activation(
                                epadb[:, hl, 512 * half:512 * half + 512], pb[6][0:1, 0:512], AF.Exp),
                                reads=["pb6"], writes=["epadb"])
                    p.emit("v", lambda e: e.memset(vaug[:, :, :, 64:65], 1.0), writes=["vaug"])
                    for tt in range(16):
                        bk_ = "pb%d" % (6 + tt % 2)
                        for kt in range(KT):
                            p.emit("t", lambda e, tt=tt, kt=kt: e.matmul(
                                pb[6 + tt % 2][:, 0:128], xnorm[:, kt, 128 * tt:128 * tt + 128],
                                wvg[:, kt, :], start=(kt == 0), stop=False),
                                reads=["wvg", "xnorm"], writes=[bk_])
                        p.emit("t", lambda e, tt=tt, g=g: e.matmul(
                            pb[6 + tt % 2][:, 0:128], onesm[:, :],
                            bvrow[:, 128 * g:128 * g + 128], start=False, stop=True),
                            reads=["onesm", "bvrow"], writes=[bk_])
                        p.emit("s", lambda e, tt=tt: e.activation(
                            vaug[:, tt, :, 0:64],
                            pb[6 + tt % 2][:, 0:128].rearrange("p (h d) -> p h d", h=2), AF.Copy),
                            reads=[bk_], writes=["vaug"])
                    for qc in range(8):
                        jl = jlist(qc)
                        pad = qc in (0, 1, 6, 7)
                        if qc < 2:
                            pofs = 256 * qc
                        else:
                            pofs = 512 + 256 * (qc - 6)
                        for hl in range(2):
                            for idx, (j, mk) in enumerate(jl):
                                bk_ = "pb%d" % (idx // 2)
                                p.emit("t", lambda e, j=j, hl=hl, qc=qc, idx=idx: e.matmul(
                                    pb[idx // 2][:, 256 * (idx % 2):256 * (idx % 2) + 256],
                                    kb[64 * hl:64 * hl + 64, 128 * j:128 * j + 128],
                                    qb[64 * hl:64 * hl + 64, 256 * qc:256 * qc + 256],
                                    start=True, stop=True),
                                    reads=["kb", "qb"], writes=[bk_ + "_%d" % (idx % 2)])
                            for idx, (j, mk) in enumerate(jl):
                                p.emit("s", lambda e, idx=idx: e.activation(
                                    probs[:, idx, :], pb[idx // 2][:, 256 * (idx % 2):256 * (idx % 2) + 256],
                                    AF.Exp),
                                    reads=["pb%d_%d" % (idx // 2, idx % 2)], writes=["probs%d" % idx])
                            for idx, (j, mk) in enumerate(jl):
                                if mk is not None:
                                    p.emit("v", lambda e, idx=idx, mk=mk: e.tensor_tensor(
                                        probs[:, idx, :], probs[:, idx, :], masks[mk][:, :], ALU.mult),
                                        reads=["probs%d" % idx, mk], writes=["probs%d" % idx])
                            for idx, (j, mk) in enumerate(jl):
                                p.emit("t", lambda e, j=j, hl=hl, idx=idx: e.matmul(
                                    pb[5][0:65, 0:256], vaug[:, j, hl, :], probs[:, idx, :],
                                    start=(idx == 0), stop=(idx == len(jl) - 1 and not pad)),
                                    reads=["vaug", "probs%d" % idx], writes=["pb5"])
                            if pad:
                                p.emit("t", lambda e, hl=hl, pofs=pofs: e.matmul(
                                    pb[5][0:65, 0:256], negpad[:, :], epadb[:, hl, pofs:pofs + 256],
                                    start=False, stop=True),
                                    reads=["negpad", "epadb"], writes=["pb5"])
                            p.emit("v", lambda e: e.reciprocal(recf[:, :], pb[5][64:65, 0:256]),
                                   reads=["pb5"], writes=["recf"])
                            p.emit("g", lambda e: e.partition_broadcast(t2[:, 0:256], recf[:, :]),
                                   reads=["recf"], writes=["t2"])
                            p.emit("v", lambda e, hl=hl: e.tensor_tensor(
                                ao[64 * hl:64 * hl + 64, :], pb[5][0:64, 0:256], t2[0:64, 0:256], ALU.mult),
                                reads=["pb5", "t2"], writes=["ao"])
                        for mt in range(KT):
                            bk_ = "pb7"
                            p.emit("t", lambda e, mt=mt: e.matmul(
                                pb[7][:, 0:256], wog[:, 128 * mt:128 * mt + 128], ao[:, :],
                                start=True, stop=True),
                                reads=["wog", "ao"], writes=[bk_])
                            if g == 0:
                                p.emit("v", lambda e, mt=mt, qc=qc: e.scalar_tensor_tensor(
                                    resid[:, mt, 256 * qc:256 * qc + 256], pb[7][:, 0:256],
                                    bias["bo"][:, mt:mt + 1], resid[:, mt, 256 * qc:256 * qc + 256],
                                    ALU.add, ALU.add),
                                    reads=[bk_, "bias", "resid"], writes=["resid"])
                            else:
                                p.emit("v", lambda e, mt=mt, qc=qc: e.tensor_tensor(
                                    resid[:, mt, 256 * qc:256 * qc + 256], pb[7][:, 0:256],
                                    resid[:, mt, 256 * qc:256 * qc + 256], ALU.add),
                                    reads=[bk_, "resid"], writes=["resid"])
            p.barrier()

        def l1ffn():
            sub = contextlib.ExitStack()
            with sub:
                ln_stats()
                load_weight(wbig[:, :, 0:1024], "wrT", "wbig", 8)
                load_weight(wbig[:, :, 1024:2048], "wkfT", "wbig", 8)
                xnp = arena[:, :, 0:257]
                xr = arena[:, :, 257:513]
                xk = arena[:, :, 513:769]
                krel = arena[:, :, 769:1025]
                rbuf = fsraw[:, :].rearrange("p (k c) -> p k c", k=KT)
                wvg2 = sub.enter_context(nc.sbuf_tensor(uname("wvg2"), [128, 2, KT, 128], BF16))
                wvfT = dram["wvfT"]

                def fetch_wv(mt, pp):
                    srcv = wvfT[:, :, 128 * mt:128 * mt + 128].transpose([1, 0, 2])
                    for kt in range(KT):
                        p.emit("gd", lambda e, kt=kt, pp=pp: e.dma_start(
                            wvg2[:, pp, kt, :], srcv[:, kt, :]),
                            writes=["wvg2%d" % pp], inc=16)

                NCH = S // PC
                for c in range(NCH):
                    cs = c * PC
                    if c == 0:
                        p.emit("v", lambda e: e.memset(xnp[:, :, 0:1], 0.0), writes=["xnp0"])
                    else:
                        p.emit("v", lambda e: e.tensor_copy(xnp[:, :, 0:1], xnp[:, :, PC:PC + 1]),
                               reads=["xnp"], writes=["xnp0"])
                    bcast_mr(cs, PC)
                    xnorm_apply(lambda kt: xnp[:, kt, 1:1 + PC], "xnp", cs, PC)
                    for kt in range(KT):
                        p.emit("v", lambda e, kt=kt: e.tensor_tensor(
                            t1[:, 0:PC], xnp[:, kt, 1:1 + PC], xnp[:, kt, 0:PC], ALU.subtract),
                            reads=["xnp", "xnp0"], writes=["t1"])
                        p.emit("v", lambda e, kt=kt: e.scalar_tensor_tensor(
                            xr[:, kt, :], t1[:, 0:PC], bias["mur"][:, kt:kt + 1], xnp[:, kt, 0:PC],
                            ALU.mult, ALU.add),
                            reads=["t1", "bias", "xnp", "xnp0"], writes=["xr"])
                        p.emit("v", lambda e, kt=kt: e.scalar_tensor_tensor(
                            xk[:, kt, :], t1[:, 0:PC], bias["muk"][:, kt:kt + 1], xnp[:, kt, 0:PC],
                            ALU.mult, ALU.add),
                            reads=["t1", "bias", "xnp", "xnp0"], writes=["xk"])
                    for mt in range(KT):
                        bk_ = "pb%d" % (mt % 2)
                        for kt in range(KT):
                            p.emit("t", lambda e, mt=mt, kt=kt: e.matmul(
                                pb[mt % 2][:, 0:PC], wbig[:, kt, 128 * mt:128 * mt + 128],
                                xr[:, kt, :], start=(kt == 0), stop=(kt == KT - 1)),
                                reads=["wbig", "xr"], writes=[bk_])
                        p.emit("s", lambda e, mt=mt: e.activation(rbuf[:, mt, :], pb[mt % 2][:, 0:PC],
                                                                  AF.Sigmoid),
                               reads=[bk_], writes=["rbuf"])
                    for mt in range(KT):
                        bk_ = "pb%d" % (2 + mt % 2)
                        for kt in range(KT):
                            p.emit("t", lambda e, mt=mt, kt=kt: e.matmul(
                                pb[2 + mt % 2][:, 0:PC], wbig[:, kt, 1024 + 128 * mt:1024 + 128 * mt + 128],
                                xk[:, kt, :], start=(kt == 0), stop=(kt == KT - 1)),
                                reads=["wbig", "xk"], writes=[bk_])
                        p.emit("v", lambda e, mt=mt: e.tensor_scalar_max(
                            krel[:, mt, :], pb[2 + mt % 2][:, 0:PC], 0.0),
                            reads=[bk_], writes=["krel"])
                        p.emit("v", lambda e, mt=mt: e.tensor_tensor(
                            krel[:, mt, :], krel[:, mt, :], krel[:, mt, :], ALU.mult),
                            reads=["krel"], writes=["krel"])
                    fetch_wv(0, 0)
                    for mt in range(KT):
                        pp2 = mt % 2
                        if mt + 1 < KT:
                            fetch_wv(mt + 1, (mt + 1) % 2)
                        elif c + 1 < NCH:
                            pass
                        bk_ = "pb%d" % (4 + mt % 2)
                        for kt in range(KT):
                            p.emit("t", lambda e, mt=mt, kt=kt, pp2=pp2: e.matmul(
                                pb[4 + mt % 2][:, 0:PC], wvg2[:, pp2, kt, :],
                                krel[:, kt, :], start=(kt == 0), stop=(kt == KT - 1)),
                                reads=["wvg2%d" % pp2, "krel"], writes=[bk_])
                        p.emit("v", lambda e, mt=mt: e.tensor_tensor(
                            t2[:, 0:PC], pb[4 + mt % 2][:, 0:PC], rbuf[:, mt, :], ALU.mult),
                            reads=[bk_, "rbuf"], writes=["t2"])
                        p.emit("v", lambda e, mt=mt, cs=cs: e.scalar_tensor_tensor(
                            resid[:, mt, cs:cs + PC], t2[:, 0:PC], 0.5, resid[:, mt, cs:cs + PC],
                            ALU.mult, ALU.add),
                            reads=["t2", "resid"], writes=["resid"])
            p.barrier()

        def l1attn(stage=99):
            sub = contextlib.ExitStack()
            with sub:
                ln_stats()
                wmini = sub.enter_context(nc.sbuf_tensor(uname("wmini"), [128, 2, 2048], BF16))
                load_weight(wmini[:, :, 0:256], "bmT", "wmini", 2)
                load_weight(wmini[:, :, 256:512], "amat", "wmini", 2)
                load_weight(wmini[:, :, 512:768], "amatT", "wmini", 2)
                load_weight(wmini[:, :, 768:1792], "cmT", "wmini", 2)
                load_weight(wbig[:, :, 0:256], "winT", "wbig", 8)
                wpow = sub.enter_context(nc.sbuf_tensor(uname("wpow"), [128, 2, 1536], BF16))
                cb = sub.enter_context(nc.sbuf_tensor(uname("cbuf"), [128, 2, 128], F32))
                hh = sub.enter_context(nc.sbuf_tensor(uname("hhbuf"), [128, 2, 8], F32))
                hh_bf = sub.enter_context(nc.sbuf_tensor(uname("hhbf"), [128, 2, 8], BF16))

                amat = wmini[:, :, 256:512]
                amatT = wmini[:, :, 512:768]
                bmT = wmini[:, :, 0:256]
                cmT = wmini[:, :, 768:1792]
                uby = fsraw[:, :].bitcast(BF16).rearrange("p (k c) -> p k c", k=2)
                ubuf = arena[:, 0:2, 0:2048]
                kstg = arena[:, 2:4, 0:512]
                hst = arena[:, 4:6, 0:128]
                hst3 = arena[:, 6:8, 0:1]

                for c in range(S // LNC):
                    cs = c * LNC
                    bcast_mr(cs, LNC)
                    xnorm_apply(lambda kt: xstat[:, kt, :], "xstat", cs, LNC)
                    for mt in range(2):
                        bk_ = "pb%d" % (mt % 2)
                        for kt in range(KT):
                            p.emit("t", lambda e, mt=mt, kt=kt: e.matmul(
                                pb[mt % 2][:, 0:LNC], wbig[:, kt, 128 * mt:128 * mt + 128],
                                xstat[:, kt, :], start=(kt == 0), stop=(kt == KT - 1)),
                                reads=["wbig", "xstat"], writes=[bk_])
                        p.emit("s", lambda e, mt=mt, cs=cs: e.activation(
                            ubuf[:, mt, cs:cs + LNC], pb[mt % 2][:, 0:LNC], AF.Identity,
                            bias=bias["binb"][:, mt:mt + 1]),
                            reads=[bk_, "bias"], writes=["ubuf"])
                if stage < 20:
                    p.barrier()
                    return
                for c in range(4):
                    cs = c * UC
                    for mt in range(2):
                        bk_ = "pb%d" % (2 + mt % 2)
                        for kt in range(2):
                            p.emit("t", lambda e, mt=mt, kt=kt, cs=cs: e.matmul(
                                pb[2 + mt % 2][:, 0:UC], bmT[:, kt, 128 * mt:128 * mt + 128],
                                ubuf[:, kt, cs:cs + UC], start=(kt == 0), stop=(kt == 1)),
                                reads=["wmini", "ubuf"], writes=[bk_])
                        p.emit("s", lambda e, mt=mt, cs=cs: e.activation(
                            uby[:, mt, cs:cs + UC], pb[2 + mt % 2][:, 0:UC], AF.Copy),
                            reads=[bk_], writes=["uby"])

                def square2(src, srcT, skeys, dst, dkey, dstT=None, dkeyT=None):
                    for mt in range(2):
                        bk_ = "pb%d" % (mt % 2)
                        for kt in range(2):
                            p.emit("t", lambda e, mt=mt, kt=kt: e.matmul(
                                pb[mt % 2][:, 0:256], srcT[:, kt, 128 * mt:128 * mt + 128],
                                src[:, kt, :], start=(kt == 0), stop=(kt == 1)),
                                reads=list(skeys), writes=[bk_])
                        p.emit("s", lambda e, mt=mt: e.activation(dst[:, mt, :], pb[mt % 2][:, 0:256], AF.Copy),
                               reads=[bk_], writes=[dkey])
                    if dstT is None:
                        return
                    for mt in range(2):
                        bk_ = "pb%d" % (2 + mt % 2)
                        for kt in range(2):
                            p.emit("t", lambda e, mt=mt, kt=kt: e.matmul(
                                pb[2 + mt % 2][:, 0:256], src[:, kt, 128 * mt:128 * mt + 128],
                                srcT[:, kt, :], start=(kt == 0), stop=(kt == 1)),
                                reads=list(skeys), writes=[bk_])
                        p.emit("s", lambda e, mt=mt: e.activation(dstT[:, mt, :], pb[2 + mt % 2][:, 0:256], AF.Copy),
                               reads=[bk_], writes=[dkeyT])

                if stage < 30:
                    p.barrier()
                    return
                p0 = wpow[:, :, 0:256]
                p0T = wpow[:, :, 256:512]
                p1 = wpow[:, :, 512:768]
                p1T = wpow[:, :, 768:1024]
                b16 = wpow[:, :, 1024:1280]
                b256 = wpow[:, :, 1280:1536]
                square2(amat, amatT, ["wmini"], p0, "p0", p0T, "p0T")
                square2(p0, p0T, ["p0", "p0T"], p1, "p1", p1T, "p1T")
                square2(p1, p1T, ["p1", "p1T"], p0, "p0", p0T, "p0T")
                square2(p0, p0T, ["p0", "p0T"], b16, "b16", p1T, "p1T")
                square2(b16, p1T, ["b16", "p1T"], p0, "p0", p0T, "p0T")
                square2(p0, p0T, ["p0", "p0T"], p1, "p1", p1T, "p1T")
                square2(p1, p1T, ["p1", "p1T"], p0, "p0", p0T, "p0T")
                square2(p0, p0T, ["p0", "p0T"], b256, "b256")

                def scan_local(buf_ap, buf_key, Lc, Cc, m_ap, m_keys, hst_ap, hst_key, f32buf):
                    for mt in range(2):
                        p.emit("s", lambda e, mt=mt: e.activation(
                            hst_ap[:, mt, 0:Cc], buf_ap[:, mt, 0::Lc], AF.Copy),
                            reads=[buf_key], writes=[hst_key])
                    for j in range(1, Lc):
                        for mt in range(2):
                            for kt in range(2):
                                p.emit("t", lambda e, mt=mt, kt=kt: e.matmul(
                                    pb[6][:, 128 * mt:128 * mt + Cc] if Cc == 128 else pb[6][:, 8 * mt:8 * mt + Cc],
                                    m_ap[:, kt, 128 * mt:128 * mt + 128], hst_ap[:, kt, 0:Cc],
                                    start=(kt == 0), stop=(kt == 1)),
                                    reads=list(m_keys) + [hst_key], writes=["pb6"])
                        for mt in range(2):
                            src = pb[6][:, 128 * mt:128 * mt + Cc] if Cc == 128 else pb[6][:, 8 * mt:8 * mt + Cc]
                            p.emit("v", lambda e, mt=mt, j=j, src=src: e.tensor_tensor(
                                buf_ap[:, mt, j::Lc], src, buf_ap[:, mt, j::Lc], ALU.add),
                                reads=["pb6", buf_key], writes=[buf_key])
                        for mt in range(2):
                            p.emit("s", lambda e, mt=mt, j=j: e.activation(
                                hst_ap[:, mt, 0:Cc], buf_ap[:, mt, j::Lc], AF.Copy),
                                reads=[buf_key], writes=[hst_key])

                def scan_expand(buf_ap, buf_key, Lc, Cc, m_ap, m_keys, w_ap, w_key):
                    for j in range(Lc):
                        for mt in range(2):
                            for kt in range(2):
                                p.emit("t", lambda e, mt=mt, kt=kt: e.matmul(
                                    pb[7][:, 128 * mt:128 * mt + Cc] if Cc == 128 else pb[7][:, 8 * mt:8 * mt + Cc],
                                    m_ap[:, kt, 128 * mt:128 * mt + 128], w_ap[:, kt, 0:Cc],
                                    start=(kt == 0), stop=(kt == 1)),
                                    reads=list(m_keys) + [w_key], writes=["pb7"])
                        for mt in range(2):
                            src = pb[7][:, 128 * mt:128 * mt + Cc] if Cc == 128 else pb[7][:, 8 * mt:8 * mt + Cc]
                            p.emit("v", lambda e, mt=mt, j=j, src=src: e.tensor_tensor(
                                buf_ap[:, mt, j::Lc], src, buf_ap[:, mt, j::Lc], ALU.add),
                                reads=["pb7", buf_key], writes=[buf_key])
                        if j < Lc - 1:
                            for mt in range(2):
                                src = pb[7][:, 128 * mt:128 * mt + Cc] if Cc == 128 else pb[7][:, 8 * mt:8 * mt + Cc]
                                p.emit("v", lambda e, mt=mt, src=src: e.tensor_copy(
                                    w_ap[:, mt, 0:Cc], src),
                                    reads=["pb7"], writes=[w_key])

                if stage < 40:
                    p.barrier()
                    return
                scan_local(uby, "uby", 16, 128, amat, ["wmini"], hst, "hst", None)
                if stage < 50:
                    p.barrier()
                    return
                for mt in range(2):
                    p.emit("v", lambda e, mt=mt: e.tensor_copy(cb[:, mt, :], uby[:, mt, 15::16]),
                           reads=["uby"], writes=["cbuf"])
                if stage < 51:
                    p.barrier()
                    return
                scan_local(cb, "cbuf", 16, 8, b16, ["b16"], hst, "hst", None)
                if stage < 52:
                    p.barrier()
                    return
                p.emit("v", lambda e: e.memset(hh[:, :, 0:1], 0.0), writes=["hh"])
                p.emit("v", lambda e: e.tensor_copy(hh[:, :, 1:2], cb[:, :, 15:16]),
                       reads=["cbuf"], writes=["hh"])
                for g2 in range(2, 8):
                    for mt in range(2):
                        p.emit("s", lambda e, mt=mt, g2=g2: e.activation(
                            hst3[:, mt, :], hh[:, mt, g2 - 1:g2], AF.Copy),
                            reads=["hh"], writes=["hst3"])
                    for mt in range(2):
                        for kt in range(2):
                            p.emit("t", lambda e, mt=mt, kt=kt: e.matmul(
                                pb[6][:, 8 * mt:8 * mt + 1], b256[:, kt, 128 * mt:128 * mt + 128],
                                hst3[:, kt, :], start=(kt == 0), stop=(kt == 1)),
                                reads=["b256", "hst3"], writes=["pb6"])
                    for mt in range(2):
                        p.emit("v", lambda e, mt=mt, g2=g2: e.tensor_tensor(
                            hh[:, mt, g2:g2 + 1], pb[6][:, 8 * mt:8 * mt + 1],
                            cb[:, mt, 16 * (g2 - 1) + 15:16 * (g2 - 1) + 16], ALU.add),
                            reads=["pb6", "cbuf"], writes=["hh"])
                if stage < 53:
                    p.barrier()
                    return
                p.emit("s", lambda e: e.activation(hh_bf[:, :, :], hh[:, :, :], AF.Copy),
                       reads=["hh"], writes=["hhbf"])
                scan_expand(cb, "cbuf", 16, 8, b16, ["b16"], hh_bf, "hhbf")
                if stage < 60:
                    p.barrier()
                    return
                p.emit("v", lambda e: e.memset(hst[:, :, 0:1], 0.0), writes=["hst"])
                for mt in range(2):
                    p.emit("s", lambda e, mt=mt: e.activation(hst[:, mt, 1:128], cb[:, mt, 0:127], AF.Copy),
                           reads=["cbuf"], writes=["hst"])
                scan_expand(uby, "uby", 16, 128, amat, ["wmini"], hst, "hst")

                if stage < 70:
                    p.barrier()
                    return
                for c in range(4):
                    cs = c * UC
                    for mt in range(2):
                        p.emit("s", lambda e, mt=mt, cs=cs: e.activation(
                            kstg[:, mt, :], uby[:, mt, cs:cs + UC], AF.Copy),
                            reads=["uby"], writes=["kstg"])
                    for mt in range(KT):
                        bk_ = "pb%d" % (mt % 2)
                        for kt in range(2):
                            p.emit("t", lambda e, mt=mt, kt=kt: e.matmul(
                                pb[mt % 2][:, 0:UC], cmT[:, kt, 128 * mt:128 * mt + 128],
                                kstg[:, kt, :], start=(kt == 0), stop=(kt == 1)),
                                reads=["wmini", "kstg"], writes=[bk_])
                        p.emit("v", lambda e, mt=mt, cs=cs: e.scalar_tensor_tensor(
                            resid[:, mt, cs:cs + UC], pb[mt % 2][:, 0:UC],
                            bias["ddv"][:, mt:mt + 1], resid[:, mt, cs:cs + UC],
                            ALU.add, ALU.add),
                            reads=[bk_, "bias", "resid"], writes=["resid"])
            p.barrier()

        subs = [l0ffn, l0attn, l0ffn, l1ffn,
                lambda: l1attn(stage=l1a_stage), l1ffn]
        for i in range(min(n_sub, 6)):
            subs[i]()

        doutT = dout.transpose([1, 0, 2])
        for kt in range(KT):
            p.emit("d", lambda e, kt=kt: e.dma_start(doutT[:, kt, :], resid[:, kt, :]),
                   reads=["resid"], inc=16)
        p._wait("t", ("d", p.cnt["d"]))
    library_overlay.lower_extended_insts(nc)
    return nc


_CACHE = {}


def kernel(**inputs):
    n_sub = int(inputs.pop("_n_sub", 6))
    l1a_stage = int(inputs.pop("_l1a_stage", 99))
    ck = (n_sub, l1a_stage)
    if ck not in _CACHE:
        _CACHE[ck] = build_program(n_sub, l1a_stage)
    nc = _CACHE[ck]

    def f32(a):
        return np.ascontiguousarray(np.asarray(a), dtype=np.float32)

    g = {k: f32(v) for k, v in inputs.items()}
    shared = {
        "wqT": f32(g["L0_attn_wq"].T).reshape(8, 128, 1024),
        "wkT": f32(g["L0_attn_wk"].T).reshape(8, 128, 1024),
        "wvT": f32(g["L0_attn_wv"].T).reshape(8, 128, 1024),
        "woT": f32(g["L0_attn_wo"].T).reshape(8, 128, 1024),
        "bq": f32(g["L0_attn_bq"].reshape(8, 128).T),
        "bk": f32(g["L0_attn_bk"].reshape(8, 128).T),
        "bo": f32(g["L0_attn_bo"].reshape(8, 128).T),
        "bvrow": f32(g["L0_attn_bv"].reshape(1, 1024)),
        "cw1": f32(g["L0_ffn_cn_w1"]).reshape(8, 128, 1024),
        "cw2": f32(g["L0_ffn_cn_w2"]).reshape(8, 128, 512),
        "cw3": f32(g["L0_ffn_cn_w3"]).reshape(4, 128, 256),
        "cw4": f32(g["L0_ffn_cn_w4"].reshape(2, 128).T),
        "gw1": f32(g["L0_ffn_gc_w1"]).reshape(8, 128, 512),
        "gw2": f32(g["L0_ffn_gc_w2"]).reshape(4, 128, 256),
        "gw3": f32(g["L0_ffn_gc_w3"].reshape(2, 128).T),
        "cb1": f32(g["L0_ffn_cn_b1"].reshape(8, 128).T),
        "cb2": f32(g["L0_ffn_cn_b2"].reshape(4, 128).T),
        "cb3": f32(g["L0_ffn_cn_b3"].reshape(2, 128).T),
        "cb4": f32(g["L0_ffn_cn_b4"].reshape(1, 1)),
        "gb1": f32(g["L0_ffn_gc_b1"].reshape(4, 128).T),
        "gb2": f32(g["L0_ffn_gc_b2"].reshape(2, 128).T),
        "gb3": f32(g["L0_ffn_gc_b3"].reshape(1, 1)),
        "arr": f32(g["L0_ffn_adaptive_range"].reshape(1, 1)),
        "wrT": f32(g["L1_ffn_W_r"].T).reshape(8, 128, 1024),
        "wkfT": f32(g["L1_ffn_W_k"].T).reshape(8, 128, 1024),
        "wvfT": f32(g["L1_ffn_W_v"].T).reshape(8, 128, 1024),
        "mur": f32(g["L1_ffn_mu_r"].reshape(8, 128).T),
        "muk": f32(g["L1_ffn_mu_k"].reshape(8, 128).T),
        "winT": f32(g["L1_attn_W_in"].T).reshape(8, 128, 256),
        "bmT": f32(g["L1_attn_Bm"].T).reshape(2, 128, 256),
        "amat": f32(g["L1_attn_A"]).reshape(2, 128, 256),
        "amatT": f32(g["L1_attn_A"].T).reshape(2, 128, 256),
        "cmT": f32(g["L1_attn_C"].T).reshape(2, 128, 1024),
        "binb": f32(g["L1_attn_b_in"].reshape(2, 128).T),
        "ddv": f32(g["L1_attn_D"].reshape(8, 128).T),
    }
    x = g["L0_attn_x"]
    in_maps = []
    for b in range(8):
        m = dict(shared)
        m["xT"] = f32(x[b].T).reshape(8, 128, 2048)
        in_maps.append(m)
    res = run_bass_kernel_spmd(nc, in_maps, core_ids=list(range(8)))
    outs = []
    for r in res.results:
        o = np.asarray(r["out"], dtype=np.float32).reshape(1024, 2048)
        outs.append(o.T)
    return np.stack(outs, axis=0)


if __name__ == "__main__":
    import reference as _ref
    inp = {k: np.asarray(v) for k, v in _ref.setup_inputs(0).items()}
    out = kernel(**inp)
    print(out.shape, out.dtype)


# version 08
# speedup vs baseline: 1.3029x; 1.3029x over previous
import contextlib
import numpy as np

import concourse.bass as bass
import concourse.mybir as mybir
from concourse.bass_utils import run_bass_kernel_spmd
from concourse import library_config
from concourse import library_overlay

F32 = mybir.dt.float32
BF16 = mybir.dt.bfloat16
AF = mybir.ActivationFunctionType
ALU = mybir.AluOpType

S = 2048
D = 1024
KT = 8
H = 16
HD = 64
NG = 8
LNC = 256
UC = 512
PC = 512


class Prog:

    def __init__(self, nc):
        self.nc = nc
        self.eng = {"t": nc.tensor, "v": nc.vector, "s": nc.scalar,
                    "g": nc.gpsimd, "d": nc.sync, "gd": nc.gpsimd}
        self.cnt = {k: 0 for k in self.eng}
        self.sem = {}
        self.waited = {k: {} for k in self.eng}
        self.res = {}
        self._es = contextlib.ExitStack()

    def init_sems(self):
        for k in self.eng:
            self.sem[k] = self._es.enter_context(self.nc.semaphore("sem_" + k))

    def _wait(self, stream, tok):
        src, val = tok
        if src == stream and src not in ("d", "gd"):
            pass
        if self.waited[stream].get(src, -1) >= val:
            return
        self.eng[stream].wait_ge(self.sem[src], val)
        self.waited[stream][src] = val

    def emit(self, stream, fn, reads=(), writes=(), inc=1):
        toks = []
        for r in reads:
            st = self.res.get(r)
            if st and st["w"] is not None:
                toks.append(st["w"])
        for w in writes:
            st = self.res.get(w)
            if st:
                if st["w"] is not None:
                    toks.append(st["w"])
                toks.extend(st["r"])
        for tok in sorted(set(toks)):
            if tok[0] == stream and tok[1] <= self.cnt[stream]:
                self._wait(stream, tok)
            else:
                self._wait(stream, tok)
        instr = fn(self.eng[stream])
        instr.then_inc(self.sem[stream], inc)
        self.cnt[stream] += inc
        tok = (stream, self.cnt[stream])
        for r in reads:
            st = self.res.setdefault(r, {"w": None, "r": []})
            st["r"].append(tok)
        for w in writes:
            self.res[w] = {"w": tok, "r": []}
        return tok

    def barrier(self):
        snap = dict(self.cnt)
        for stream in ("t", "v", "s", "g", "d"):
            for src, val in snap.items():
                if src != stream and val > 0:
                    self._wait(stream, (src, val))
        self.res = {}


def ceil_div(a, b):
    return -(-a // b)


def jlist(qc):
    out = []
    for j in range(16):
        s = 128 * j - 256 * qc
        if s <= -640 or s >= 768:
            continue
        mask = None
        if s == -512:
            mask = "ml0"
        elif s == -384:
            mask = "ml1"
        elif s == 512:
            mask = "mu0"
        elif s == 640:
            mask = "mu1"
        out.append((j, mask))
    return out


def build_program(n_sub=6, l1a_stage=99):
    nc = bass.Bass("TRN2", target_bir_lowering=False)
    p = Prog(nc)

    dram = {}

    def din(name, shape, dtype=F32):
        dram[name] = nc.dram_tensor(name, list(shape), dtype, kind="ExternalInput").ap()
        return dram[name]

    din("xT", (KT, 128, S))
    dout = nc.dram_tensor("out", [KT, 128, S], F32, kind="ExternalOutput").ap()

    for n in ("wqT", "wkT", "wvT", "woT"):
        din(n, (KT, 128, D))
    for n in ("bq", "bk", "bo"):
        din(n, (128, 8))
    din("bvrow", (1, D))
    din("cw1", (8, 128, 1024)); din("cw2", (8, 128, 512))
    din("cw3", (4, 128, 256)); din("cw4", (128, 2))
    din("gw1", (8, 128, 512)); din("gw2", (4, 128, 256)); din("gw3", (128, 2))
    din("cb1", (128, 8)); din("cb2", (128, 4)); din("cb3", (128, 2)); din("cb4", (1, 1))
    din("gb1", (128, 4)); din("gb2", (128, 2)); din("gb3", (1, 1)); din("arr", (1, 1))
    din("wrT", (8, 128, 1024)); din("wkfT", (8, 128, 1024)); din("wvfT", (8, 128, 1024))
    din("mur", (128, 8)); din("muk", (128, 8))
    din("winT", (8, 128, 256)); din("bmT", (2, 128, 256))
    din("amat", (2, 128, 256)); din("amatT", (2, 128, 256)); din("cmT", (2, 128, 1024))
    din("binb", (128, 2)); din("ddv", (128, 8))

    es = contextlib.ExitStack()

    def sb(name, shape, dtype):
        return es.enter_context(nc.sbuf_tensor(name, list(shape), dtype))

    _uid = [0]

    def uname(name):
        _uid[0] += 1
        return "%s_u%d" % (name, _uid[0])

    with es:
        p.init_sems()
        nc.gpsimd.load_library(library_config.proxy)

        resid = sb("resid", (128, KT, S), F32)
        arena = sb("arena", (128, KT, 2064), BF16)
        xstat = sb("xstat", (128, KT, LNC), BF16)
        xsqst = sb("xsqst", (128, 2, LNC), BF16)
        fsraw = sb("fsraw", (128, 4096), F32)
        mrow = sb("mrow", (1, S), F32)[:, :]
        rsrow = sb("rsrow", (1, S), F32)[:, :]
        srow = sb("srow", (1, 512), F32)[:, :]
        srow2 = sb("srow2", (1, 512), F32)[:, :]
        mb = sb("mb", (128, 512), F32)
        rb = sb("rb", (128, 512), F32)
        t1 = sb("t1", (128, 512), F32)
        t2 = sb("t2", (128, 512), F32)
        wbig = sb("wbig", (128, KT, 2048), BF16)
        onesb = sb("onesb", (128, 1), BF16)
        onesm = sb("onesm", (1, 128), BF16)
        negones = sb("negones", (128, 1), BF16)
        negpad = sb("negpad", (1, 65), BF16)
        ident = sb("ident", (128, 128), BF16)
        masks = {n: sb(n, (128, 256), BF16) for n in ("ml0", "ml1", "mu0", "mu1")}
        bias = {}
        for n, shp in (("bq", (128, 8)), ("bk", (128, 8)), ("bo", (128, 8)),
                       ("cb1", (128, 8)), ("cb2", (128, 4)), ("cb3", (128, 2)),
                       ("cb4", (1, 1)), ("gb1", (128, 4)), ("gb2", (128, 2)),
                       ("gb3", (1, 1)), ("arr", (1, 1)), ("mur", (128, 8)),
                       ("muk", (128, 8)), ("binb", (128, 2)), ("ddv", (128, 8))):
            bias[n] = sb("b_" + n, shp, F32)
        bvrow = sb("bvrow_s", (1, D), BF16)
        scal = sb("scal", (1, 16), F32)

        pb = [es.enter_context(nc.psum_tensor("pb%d" % i, [128, 512], F32))
              for i in range(8)]

        xTsrc = dram["xT"].transpose([1, 0, 2])
        for kt in range(KT):
            p.emit("d", lambda e, kt=kt: e.dma_start(resid[:, kt, :], xTsrc[:, kt, :]),
                   writes=["resid"], inc=16)
        for n in bias:
            nm = n
            p.emit("d", lambda e, nm=nm: e.dma_start(bias[nm][:, :], dram[nm][:, :]),
                   writes=["bias"], inc=16)
        p.emit("gd", lambda e: e.dma_start(bvrow[:, :], dram["bvrow"][:, :]),
               writes=["bvrow"], inc=16)

        p.emit("v", lambda e: e.memset(onesb[:, :], 1.0), writes=["onesb"])
        p.emit("v", lambda e: e.memset(onesm[:, :], 1.0), writes=["onesm"])
        p.emit("v", lambda e: e.memset(negones[:, :], -1.0), writes=["negones"])
        p.emit("v", lambda e: e.memset(negpad[:, 0:64], -512.0), writes=["negpad"])
        p.emit("v", lambda e: e.memset(negpad[:, 64:65], 512.0), writes=["negpad"])
        p.emit("v", lambda e: e.memset(t1[:, 0:128], 1.0), writes=["t1"])
        p.emit("g", lambda e: e.affine_select(t1[:, 0:128], t1[:, 0:128], pattern=[[-1, 128]],
                                              base=0, channel_multiplier=1,
                                              compare_op=ALU.is_equal, fill=0.0),
               reads=["t1"], writes=["t1"])
        p.emit("s", lambda e: e.activation(ident[:, :], t1[:, 0:128], AF.Copy),
               reads=["t1"], writes=["ident"])
        p.emit("v", lambda e: e.memset(scal[:, 10:11], 1e-5), writes=["eps"])
        mspec = {"ml0": (0, 1, -1), "ml1": (128, 1, -1), "mu0": (0, -1, 1), "mu1": (-128, -1, 1)}
        for n, (base, cm, fc) in mspec.items():
            p.emit("v", lambda e: e.memset(t1[:, 0:256], 1.0), writes=["t1"])
            p.emit("g", lambda e, base=base, cm=cm, fc=fc: e.affine_select(
                t1[:, 0:256], t1[:, 0:256], pattern=[[fc, 256]], base=base,
                channel_multiplier=cm, compare_op=ALU.is_ge, fill=0.0),
                reads=["t1"], writes=["t1"])
            p.emit("s", lambda e, n=n: e.activation(masks[n][:, :], t1[:, 0:256], AF.Copy),
                   reads=["t1"], writes=[n])

        def load_weight(dst_ap, src_name, key, kt_tiles):
            src = dram[src_name].transpose([1, 0, 2])
            for kt in range(kt_tiles):
                p.emit("gd", lambda e, kt=kt: e.dma_start(dst_ap[:, kt, :], src[:, kt, :]),
                       writes=[key], inc=16)

        def ln_stats():
            for c in range(S // LNC):
                cs = c * LNC
                p.emit("s", lambda e, cs=cs: e.activation(xstat[:, :, :], resid[:, :, cs:cs + LNC], AF.Copy),
                       reads=["resid"], writes=["xstat"])
                for kt in range(KT):
                    p.emit("t", lambda e, kt=kt: e.matmul(pb[6][0:1, 0:LNC], onesb[:, :], xstat[:, kt, :],
                                                          start=(kt == 0), stop=(kt == KT - 1)),
                           reads=["onesb", "xstat"], writes=["pb6"])
                for kt in range(KT):
                    pp = kt % 2
                    p.emit("s", lambda e, kt=kt, pp=pp, cs=cs: e.activation(
                        xsqst[:, pp, :], resid[:, kt, cs:cs + LNC], AF.Square),
                        reads=["resid"], writes=["xsq%d" % pp])
                    p.emit("t", lambda e, kt=kt, pp=pp: e.matmul(pb[7][0:1, 0:LNC], onesb[:, :], xsqst[:, pp, :],
                                                                 start=(kt == 0), stop=(kt == KT - 1)),
                           reads=["onesb", "xsq%d" % pp], writes=["pb7"])
                p.emit("v", lambda e, cs=cs: e.tensor_scalar_mul(mrow[:, cs:cs + LNC], pb[6][0:1, 0:LNC], 1.0 / D),
                       reads=["pb6"], writes=["mrow"])
                p.emit("v", lambda e: e.tensor_scalar_mul(srow[:, 0:LNC], pb[7][0:1, 0:LNC], 1.0 / D),
                       reads=["pb7"], writes=["srow"])
                p.emit("v", lambda e, cs=cs: e.tensor_tensor(srow2[:, 0:LNC], mrow[:, cs:cs + LNC],
                                                             mrow[:, cs:cs + LNC], ALU.mult),
                       reads=["mrow"], writes=["srow2"])
                p.emit("v", lambda e: e.tensor_tensor(srow[:, 0:LNC], srow[:, 0:LNC], srow2[:, 0:LNC], ALU.subtract),
                       reads=["srow", "srow2"], writes=["srow"])
                p.emit("s", lambda e: e.activation(srow2[:, 0:LNC], srow[:, 0:LNC], AF.Ln, bias=scal[:, 10:11]),
                       reads=["srow", "eps"], writes=["srow2"])
                p.emit("s", lambda e, cs=cs: e.activation(rsrow[:, cs:cs + LNC], srow2[:, 0:LNC], AF.Exp, scale=-0.5),
                       reads=["srow2"], writes=["rsrow"])

        def bcast_mr(cs, n):
            p.emit("g", lambda e: e.partition_broadcast(mb[:, 0:n], mrow[:, cs:cs + n]),
                   reads=["mrow"], writes=["mb"])
            p.emit("g", lambda e: e.partition_broadcast(rb[:, 0:n], rsrow[:, cs:cs + n]),
                   reads=["rsrow"], writes=["rb"])

        def xnorm_apply(dst_ap_fn, dst_key, cs, n):
            for kt in range(KT):
                p.emit("v", lambda e, kt=kt: e.tensor_tensor(t1[:, 0:n], resid[:, kt, cs:cs + n],
                                                             mb[:, 0:n], ALU.subtract),
                       reads=["resid", "mb"], writes=["t1"])
                p.emit("v", lambda e, kt=kt: e.tensor_tensor(dst_ap_fn(kt), t1[:, 0:n], rb[:, 0:n], ALU.mult),
                       reads=["t1", "rb"], writes=[dst_key])

        def l0ffn():
            sub = contextlib.ExitStack()
            with sub:
                ln_stats()
                load_weight(wbig[:, :, 0:1024], "cw1", "wbig", 8)
                load_weight(wbig[:, :, 1024:1536], "cw2", "wbig", 8)
                load_weight(wbig[:, :, 1536:2048], "gw1", "wbig", 8)
                wsm = sub.enter_context(nc.sbuf_tensor(uname("wsm"), [128, 4, 512], BF16))
                w4s = sub.enter_context(nc.sbuf_tensor(uname("w4s"), [128, 4], BF16))
                load_weight(wsm[:, :, 0:256], "cw3", "wsm", 4)
                load_weight(wsm[:, 0:4, 256:512], "gw2", "wsm", 4)
                p.emit("gd", lambda e: e.dma_start(w4s[:, 0:2], dram["cw4"][:, :]),
                       writes=["w4s"], inc=16)
                p.emit("gd", lambda e: e.dma_start(w4s[:, 2:4], dram["gw3"][:, :]),
                       writes=["w4s"], inc=16)
                gxa = sub.enter_context(nc.sbuf_tensor(uname("gxa"), [128, KT, 1], F32))
                gxr = sub.enter_context(nc.sbuf_tensor(uname("gxr"), [128, KT, 1], F32))
                gxbf = sub.enter_context(nc.sbuf_tensor(uname("gxbf"), [128, KT, 1], BF16))
                gh1 = sub.enter_context(nc.sbuf_tensor(uname("gh1"), [128, 4, 1], BF16))
                gh2 = sub.enter_context(nc.sbuf_tensor(uname("gh2"), [128, 2, 1], BF16))

                p.emit("v", lambda e: e.memset(gxa[:, :, :], 0.0), writes=["gxa"])
                for c in range(S // LNC):
                    cs = c * LNC
                    bcast_mr(cs, LNC)
                    xnorm_apply(lambda kt: xstat[:, kt, :], "xstat", cs, LNC)
                    p.emit("v", lambda e: e.reduce_sum(gxr[:, :, :], xstat[:, :, :], mybir.AxisListType.X),
                           reads=["xstat"], writes=["gxr"])
                    p.emit("v", lambda e: e.tensor_tensor(gxa[:, :, :], gxa[:, :, :], gxr[:, :, :], ALU.add),
                           reads=["gxa", "gxr"], writes=["gxa"])
                p.emit("s", lambda e: e.activation(gxbf[:, :, :], gxa[:, :, :], AF.Copy),
                       reads=["gxa"], writes=["gxbf"])
                for mt in range(4):
                    for kt in range(KT):
                        p.emit("t", lambda e, mt=mt, kt=kt: e.matmul(
                            pb[2][:, mt:mt + 1], wbig[:, kt, 1536 + 128 * mt:1536 + 128 * mt + 128],
                            gxbf[:, kt, :], start=(kt == 0), stop=(kt == KT - 1)),
                            reads=["wbig", "gxbf"], writes=["pb2"])
                for mt in range(4):
                    p.emit("s", lambda e, mt=mt: e.activation(gh1[:, mt, :], pb[2][:, mt:mt + 1], AF.Gelu,
                                                              bias=bias["gb1"][:, mt:mt + 1], scale=1.0 / S),
                           reads=["pb2", "bias"], writes=["gh1"])
                for mt in range(2):
                    for kt in range(4):
                        p.emit("t", lambda e, mt=mt, kt=kt: e.matmul(
                            pb[3][:, mt:mt + 1], wsm[:, kt, 256 + 128 * mt:256 + 128 * mt + 128],
                            gh1[:, kt, :], start=(kt == 0), stop=(kt == 3)),
                            reads=["wsm", "gh1"], writes=["pb3"])
                for mt in range(2):
                    p.emit("s", lambda e, mt=mt: e.activation(gh2[:, mt, :], pb[3][:, mt:mt + 1], AF.Gelu,
                                                              bias=bias["gb2"][:, mt:mt + 1]),
                           reads=["pb3", "bias"], writes=["gh2"])
                for kt in range(2):
                    p.emit("t", lambda e, kt=kt: e.matmul(pb[4][0:1, 0:1], w4s[:, 2 + kt:3 + kt],
                                                          gh2[:, kt, :], start=(kt == 0), stop=(kt == 1)),
                           reads=["w4s", "gh2"], writes=["pb4"])
                p.emit("v", lambda e: e.tensor_scalar_mul(scal[:, 1:2], bias["gb3"][:, :], -1.0),
                       reads=["bias"], writes=["scal"])
                p.emit("s", lambda e: e.activation(scal[:, 2:3], pb[4][0:1, 0:1], AF.Exp,
                                                   scale=-1.0, bias=scal[:, 1:2]),
                       reads=["pb4", "scal"], writes=["scal2"])
                p.emit("v", lambda e: e.tensor_scalar_add(scal[:, 3:4], scal[:, 2:3], 1.0),
                       reads=["scal2"], writes=["scal3"])
                p.emit("v", lambda e: e.reciprocal(scal[:, 0:1], scal[:, 3:4]),
                       reads=["scal3"], writes=["scal0"])
                p.emit("v", lambda e: e.tensor_tensor(scal[:, 4:5], bias["arr"][:, :], scal[:, 0:1], ALU.mult),
                       reads=["bias", "scal0"], writes=["scal4"])
                p.emit("v", lambda e: e.tensor_scalar_mul(scal[:, 5:6], scal[:, 4:5], 1.4),
                       reads=["scal4"], writes=["scal5"])
                p.emit("v", lambda e: e.tensor_scalar(scal[:, 6:7], bias["arr"][:, :], -1.0, 0.1, ALU.mult, ALU.add),
                       reads=["bias"], writes=["scal6"])
                p.emit("v", lambda e: e.tensor_tensor(scal[:, 7:8], scal[:, 5:6], scal[:, 6:7], ALU.add),
                       reads=["scal5", "scal6"], writes=["c0"])
                p.emit("v", lambda e: e.tensor_scalar_mul(scal[:, 8:9], bias["arr"][:, :], 0.6),
                       reads=["bias"], writes=["c1"])
                p.emit("v", lambda e: e.tensor_scalar_mul(scal[:, 9:10], bias["cb4"][:, :], -1.0),
                       reads=["bias"], writes=["scal9"])

                op = fsraw[:, :].rearrange("p (k c) -> p k c", k=KT)
                obf = arena[:, :, 0:PC]
                h1 = arena[:, :, PC:2 * PC]
                h2 = arena[:, 0:4, 2 * PC:3 * PC]
                h3 = arena[:, 0:2, 3 * PC:4 * PC]
                pairs = [(0, 4, 1.0), (1, 5, 1.0), (2, 6, -1.0), (3, 7, -1.0)]
                for c in range(S // PC):
                    cs = c * PC
                    bcast_mr(cs, PC)
                    xnorm_apply(lambda kt: op[:, kt, :], "op", cs, PC)
                    p.emit("s", lambda e: e.activation(obf, op[:, :, :], AF.Copy),
                           reads=["op"], writes=["obf"])
                    for r in range(3):
                        for mt in range(KT):
                            bk = "pb%d" % (2 + mt % 2)
                            for kt in range(KT):
                                p.emit("t", lambda e, mt=mt, kt=kt: e.matmul(
                                    pb[2 + mt % 2][:, 0:PC], wbig[:, kt, 128 * mt:128 * mt + 128],
                                    obf[:, kt, :], start=(kt == 0), stop=(kt == KT - 1)),
                                    reads=["wbig", "obf"], writes=[bk])
                            p.emit("s", lambda e, mt=mt: e.activation(
                                h1[:, mt, :], pb[2 + mt % 2][:, 0:PC], AF.Gelu,
                                bias=bias["cb1"][:, mt:mt + 1]),
                                reads=[bk, "bias"], writes=["h1"])
                        for mt in range(4):
                            bk = "pb%d" % (4 + mt % 2)
                            for kt in range(KT):
                                p.emit("t", lambda e, mt=mt, kt=kt: e.matmul(
                                    pb[4 + mt % 2][:, 0:PC], wbig[:, kt, 1024 + 128 * mt:1024 + 128 * mt + 128],
                                    h1[:, kt, :], start=(kt == 0), stop=(kt == KT - 1)),
                                    reads=["wbig", "h1"], writes=[bk])
                            p.emit("s", lambda e, mt=mt: e.activation(
                                h2[:, mt, :], pb[4 + mt % 2][:, 0:PC], AF.Gelu,
                                bias=bias["cb2"][:, mt:mt + 1]),
                                reads=[bk, "bias"], writes=["h2"])
                        for mt in range(2):
                            bk = "pb%d" % (6 + mt % 2)
                            for kt in range(4):
                                p.emit("t", lambda e, mt=mt, kt=kt: e.matmul(
                                    pb[6 + mt % 2][:, 0:PC], wsm[:, kt, 128 * mt:128 * mt + 128],
                                    h2[:, kt, :], start=(kt == 0), stop=(kt == 3)),
                                    reads=["wsm", "h2"], writes=[bk])
                            p.emit("s", lambda e, mt=mt: e.activation(
                                h3[:, mt, :], pb[6 + mt % 2][:, 0:PC], AF.Gelu,
                                bias=bias["cb3"][:, mt:mt + 1]),
                                reads=[bk, "bias"], writes=["h3"])
                        for kt in range(2):
                            p.emit("t", lambda e, kt=kt: e.matmul(
                                pb[4][0:1, 0:PC], w4s[:, kt:kt + 1], h3[:, kt, :],
                                start=(kt == 0), stop=(kt == 1)),
                                reads=["w4s", "h3"], writes=["pb4"])
                        p.emit("s", lambda e: e.activation(srow[:, 0:PC], pb[4][0:1, 0:PC], AF.Exp,
                                                           scale=-1.0, bias=scal[:, 9:10]),
                               reads=["pb4", "scal9"], writes=["srow"])
                        p.emit("v", lambda e: e.tensor_scalar_add(srow[:, 0:PC], srow[:, 0:PC], 1.0),
                               reads=["srow"], writes=["srow"])
                        p.emit("v", lambda e: e.reciprocal(srow2[:, 0:PC], srow[:, 0:PC]),
                               reads=["srow"], writes=["srow2"])
                        p.emit("v", lambda e: e.tensor_scalar(srow[:, 0:PC], srow2[:, 0:PC],
                                                              scal[:, 8:9], scal[:, 7:8], ALU.mult, ALU.add),
                               reads=["srow2", "c0", "c1"], writes=["srow"])
                        p.emit("g", lambda e: e.partition_broadcast(mb[:, 0:PC], srow[:, 0:PC]),
                               reads=["srow"], writes=["mb"])
                        for (a, b, sa) in pairs:
                            p.emit("v", lambda e, b=b: e.tensor_tensor(t1[:, 0:PC], mb[:, 0:PC], op[:, b, :], ALU.mult),
                                   reads=["mb", "op"], writes=["t1"])
                            p.emit("v", lambda e, a=a: e.tensor_tensor(t2[:, 0:PC], mb[:, 0:PC], op[:, a, :], ALU.mult),
                                   reads=["mb", "op"], writes=["t2"])
                            p.emit("v", lambda e, a=a, sa=sa: e.tensor_tensor(
                                op[:, a, :], op[:, a, :], t1[:, 0:PC], ALU.add if sa > 0 else ALU.subtract),
                                reads=["op", "t1"], writes=["op"])
                            p.emit("v", lambda e, b=b, sa=sa: e.tensor_tensor(
                                op[:, b, :], op[:, b, :], t2[:, 0:PC], ALU.subtract if sa > 0 else ALU.add),
                                reads=["op", "t2"], writes=["op"])
                        if r < 2:
                            p.emit("s", lambda e: e.activation(obf, op[:, :, :], AF.Copy),
                                   reads=["op"], writes=["obf"])
                    for kt in range(KT):
                        p.emit("v", lambda e, kt=kt, cs=cs: e.scalar_tensor_tensor(
                            resid[:, kt, cs:cs + PC], op[:, kt, :], 0.5, resid[:, kt, cs:cs + PC],
                            ALU.mult, ALU.add),
                            reads=["op", "resid"], writes=["resid"])
            p.barrier()

        def l0attn():
            sub = contextlib.ExitStack()
            with sub:
                ln_stats()
                load_weight(wbig[:, :, 0:1024], "wqT", "wbig", 8)
                load_weight(wbig[:, :, 1024:2048], "wkT", "wbig", 8)
                xnorm = arena[:, :, 0:2048]
                for c in range(S // LNC):
                    cs = c * LNC
                    bcast_mr(cs, LNC)
                    xnorm_apply(lambda kt, cs=cs: xnorm[:, kt, cs:cs + LNC], "xnorm", cs, LNC)
                qb = sub.enter_context(nc.sbuf_tensor(uname("qb"), [128, S], BF16))
                kb = sub.enter_context(nc.sbuf_tensor(uname("kb"), [128, S], BF16))
                vaug = sub.enter_context(nc.sbuf_tensor(uname("vaug"), [128, 16, 2, 65], BF16))
                probs = sub.enter_context(nc.sbuf_tensor(uname("probs"), [128, 10, 256], BF16))
                ao = xstat[:, 0, :]
                epadb = sub.enter_context(nc.sbuf_tensor(uname("epadb"), [1, 2, 1024], BF16))
                wvg = sub.enter_context(nc.sbuf_tensor(uname("wvg"), [128, KT, 128], BF16))
                wog = sub.enter_context(nc.sbuf_tensor(uname("wog"), [128, 1024], BF16))

                wvT = dram["wvT"]
                woT = dram["woT"]

                def fetch_group(g):
                    srcv = wvT[:, :, 128 * g:128 * g + 128].transpose([1, 0, 2])
                    for kt in range(KT):
                        p.emit("gd", lambda e, kt=kt: e.dma_start(wvg[:, kt, :], srcv[:, kt, :]),
                               writes=["wvg"], inc=16)
                    p.emit("gd", lambda e, g=g: e.dma_start(wog[:, :], woT[g, :, :]),
                           writes=["wog"], inc=16)

                for g in range(NG):
                    fetch_group(g)
                    for dst, wofs, bn, dk in ((qb, 0, "bq", "qb"), (kb, 1024, "bk", "kb")):
                        for nb in range(4):
                            bk_ = "pb%d" % (6 + nb % 2)
                            for kt in range(KT):
                                p.emit("t", lambda e, nb=nb, kt=kt, wofs=wofs, g=g: e.matmul(
                                    pb[6 + nb % 2][:, 0:512],
                                    wbig[:, kt, wofs + 128 * g:wofs + 128 * g + 128],
                                    xnorm[:, kt, 512 * nb:512 * nb + 512],
                                    start=(kt == 0), stop=(kt == KT - 1)),
                                    reads=["wbig", "xnorm"], writes=[bk_])
                            p.emit("s", lambda e, nb=nb, dst=dst, bn=bn, g=g: e.activation(
                                dst[:, 512 * nb:512 * nb + 512], pb[6 + nb % 2][:, 0:512],
                                AF.Identity, bias=bias[bn][:, g:g + 1]),
                                reads=[bk_, "bias"], writes=[dk])
                    for hl in range(2):
                        for half, q0 in ((0, 0), (1, 1536)):
                            p.emit("t", lambda e, hl=hl, q0=q0: e.matmul(
                                pb[6][0:1, 0:512], negones[64 * hl:64 * hl + 64, :],
                                qb[64 * hl:64 * hl + 64, q0:q0 + 512], start=True, stop=True),
                                reads=["qb", "negones"], writes=["pb6"])
                            p.emit("s", lambda e, hl=hl, half=half: e.activation(
                                epadb[:, hl, 512 * half:512 * half + 512], pb[6][0:1, 0:512], AF.Exp),
                                reads=["pb6"], writes=["epadb"])
                    p.emit("v", lambda e: e.memset(vaug[:, :, :, 64:65], 1.0), writes=["vaug"])
                    for tt in range(16):
                        bk_ = "pb%d" % (6 + tt % 2)
                        for kt in range(KT):
                            p.emit("t", lambda e, tt=tt, kt=kt: e.matmul(
                                pb[6 + tt % 2][:, 0:128], xnorm[:, kt, 128 * tt:128 * tt + 128],
                                wvg[:, kt, :], start=(kt == 0), stop=False),
                                reads=["wvg", "xnorm"], writes=[bk_])
                        p.emit("t", lambda e, tt=tt, g=g: e.matmul(
                            pb[6 + tt % 2][:, 0:128], onesm[:, :],
                            bvrow[:, 128 * g:128 * g + 128], start=False, stop=True),
                            reads=["onesm", "bvrow"], writes=[bk_])
                        p.emit("s", lambda e, tt=tt: e.activation(
                            vaug[:, tt, :, 0:64],
                            pb[6 + tt % 2][:, 0:128].rearrange("p (h d) -> p h d", h=2), AF.Copy),
                            reads=[bk_], writes=["vaug"])
                    for qc in range(8):
                        jl = jlist(qc)
                        pad = qc in (0, 1, 6, 7)
                        if qc < 2:
                            pofs = 256 * qc
                        else:
                            pofs = 512 + 256 * (qc - 6)
                        for hl in range(2):
                            for idx, (j, mk) in enumerate(jl):
                                bk_ = "pb%d" % (idx // 2)
                                p.emit("t", lambda e, j=j, hl=hl, qc=qc, idx=idx: e.matmul(
                                    pb[idx // 2][:, 256 * (idx % 2):256 * (idx % 2) + 256],
                                    kb[64 * hl:64 * hl + 64, 128 * j:128 * j + 128],
                                    qb[64 * hl:64 * hl + 64, 256 * qc:256 * qc + 256],
                                    start=True, stop=True),
                                    reads=["kb", "qb"], writes=[bk_ + "_%d" % (idx % 2)])
                            for idx, (j, mk) in enumerate(jl):
                                p.emit("s", lambda e, idx=idx: e.activation(
                                    probs[:, idx, :], pb[idx // 2][:, 256 * (idx % 2):256 * (idx % 2) + 256],
                                    AF.Exp),
                                    reads=["pb%d_%d" % (idx // 2, idx % 2)], writes=["probs%d" % idx])
                            for idx, (j, mk) in enumerate(jl):
                                if mk is not None:
                                    p.emit("v", lambda e, idx=idx, mk=mk: e.tensor_tensor(
                                        probs[:, idx, :], probs[:, idx, :], masks[mk][:, :], ALU.mult),
                                        reads=["probs%d" % idx, mk], writes=["probs%d" % idx])
                            for idx, (j, mk) in enumerate(jl):
                                p.emit("t", lambda e, j=j, hl=hl, idx=idx: e.matmul(
                                    pb[5][0:65, 0:256], vaug[:, j, hl, :], probs[:, idx, :],
                                    start=(idx == 0), stop=(idx == len(jl) - 1 and not pad)),
                                    reads=["vaug", "probs%d" % idx], writes=["pb5"])
                            if pad:
                                p.emit("t", lambda e, hl=hl, pofs=pofs: e.matmul(
                                    pb[5][0:65, 0:256], negpad[:, :], epadb[:, hl, pofs:pofs + 256],
                                    start=False, stop=True),
                                    reads=["negpad", "epadb"], writes=["pb5"])
                            p.emit("v", lambda e: e.reciprocal(srow2[:, 0:256], pb[5][64:65, 0:256]),
                                   reads=["pb5"], writes=["srow2"])
                            p.emit("g", lambda e: e.partition_broadcast(t2[:, 0:256], srow2[:, 0:256]),
                                   reads=["srow2"], writes=["t2"])
                            p.emit("v", lambda e, hl=hl: e.tensor_tensor(
                                ao[64 * hl:64 * hl + 64, :], pb[5][0:64, 0:256], t2[0:64, 0:256], ALU.mult),
                                reads=["pb5", "t2"], writes=["xstat"])
                        for mt in range(KT):
                            bk_ = "pb7"
                            p.emit("t", lambda e, mt=mt: e.matmul(
                                pb[7][:, 0:256], wog[:, 128 * mt:128 * mt + 128], ao[:, :],
                                start=True, stop=True),
                                reads=["wog", "xstat"], writes=[bk_])
                            if g == 0:
                                p.emit("v", lambda e, mt=mt, qc=qc: e.scalar_tensor_tensor(
                                    resid[:, mt, 256 * qc:256 * qc + 256], pb[7][:, 0:256],
                                    bias["bo"][:, mt:mt + 1], resid[:, mt, 256 * qc:256 * qc + 256],
                                    ALU.add, ALU.add),
                                    reads=[bk_, "bias", "resid"], writes=["resid"])
                            else:
                                p.emit("v", lambda e, mt=mt, qc=qc: e.tensor_tensor(
                                    resid[:, mt, 256 * qc:256 * qc + 256], pb[7][:, 0:256],
                                    resid[:, mt, 256 * qc:256 * qc + 256], ALU.add),
                                    reads=[bk_, "resid"], writes=["resid"])
            p.barrier()

        def l1ffn():
            sub = contextlib.ExitStack()
            with sub:
                ln_stats()
                load_weight(wbig[:, :, 0:1024], "wrT", "wbig", 8)
                load_weight(wbig[:, :, 1024:2048], "wkfT", "wbig", 8)
                xnp = arena[:, :, 0:513]
                xr = arena[:, :, 513:1025]
                xk = arena[:, :, 1025:1537]
                krel = arena[:, :, 1537:2049]
                rbuf = fsraw[:, :].rearrange("p (k c) -> p k c", k=KT)
                wvg2 = sub.enter_context(nc.sbuf_tensor(uname("wvg2"), [128, 2, KT, 128], BF16))
                wvfT = dram["wvfT"]

                def fetch_wv(mt, pp):
                    srcv = wvfT[:, :, 128 * mt:128 * mt + 128].transpose([1, 0, 2])
                    for kt in range(KT):
                        p.emit("gd", lambda e, kt=kt, pp=pp: e.dma_start(
                            wvg2[:, pp, kt, :], srcv[:, kt, :]),
                            writes=["wvg2%d" % pp], inc=16)

                NCH = S // PC
                for c in range(NCH):
                    cs = c * PC
                    if c == 0:
                        p.emit("v", lambda e: e.memset(xnp[:, :, 0:1], 0.0), writes=["xnp0"])
                    else:
                        p.emit("v", lambda e: e.tensor_copy(xnp[:, :, 0:1], xnp[:, :, PC:PC + 1]),
                               reads=["xnp"], writes=["xnp0"])
                    bcast_mr(cs, PC)
                    xnorm_apply(lambda kt: xnp[:, kt, 1:1 + PC], "xnp", cs, PC)
                    for kt in range(KT):
                        p.emit("v", lambda e, kt=kt: e.tensor_tensor(
                            t1[:, 0:PC], xnp[:, kt, 1:1 + PC], xnp[:, kt, 0:PC], ALU.subtract),
                            reads=["xnp", "xnp0"], writes=["t1"])
                        p.emit("v", lambda e, kt=kt: e.scalar_tensor_tensor(
                            xr[:, kt, :], t1[:, 0:PC], bias["mur"][:, kt:kt + 1], xnp[:, kt, 0:PC],
                            ALU.mult, ALU.add),
                            reads=["t1", "bias", "xnp", "xnp0"], writes=["xr"])
                        p.emit("v", lambda e, kt=kt: e.scalar_tensor_tensor(
                            xk[:, kt, :], t1[:, 0:PC], bias["muk"][:, kt:kt + 1], xnp[:, kt, 0:PC],
                            ALU.mult, ALU.add),
                            reads=["t1", "bias", "xnp", "xnp0"], writes=["xk"])
                    for mt in range(KT):
                        bk_ = "pb%d" % (mt % 2)
                        for kt in range(KT):
                            p.emit("t", lambda e, mt=mt, kt=kt: e.matmul(
                                pb[mt % 2][:, 0:PC], wbig[:, kt, 128 * mt:128 * mt + 128],
                                xr[:, kt, :], start=(kt == 0), stop=(kt == KT - 1)),
                                reads=["wbig", "xr"], writes=[bk_])
                        p.emit("s", lambda e, mt=mt: e.activation(rbuf[:, mt, :], pb[mt % 2][:, 0:PC],
                                                                  AF.Sigmoid),
                               reads=[bk_], writes=["rbuf"])
                    for mt in range(KT):
                        bk_ = "pb%d" % (2 + mt % 2)
                        for kt in range(KT):
                            p.emit("t", lambda e, mt=mt, kt=kt: e.matmul(
                                pb[2 + mt % 2][:, 0:PC], wbig[:, kt, 1024 + 128 * mt:1024 + 128 * mt + 128],
                                xk[:, kt, :], start=(kt == 0), stop=(kt == KT - 1)),
                                reads=["wbig", "xk"], writes=[bk_])
                        p.emit("v", lambda e, mt=mt: e.tensor_scalar_max(
                            krel[:, mt, :], pb[2 + mt % 2][:, 0:PC], 0.0),
                            reads=[bk_], writes=["krel"])
                        p.emit("v", lambda e, mt=mt: e.tensor_tensor(
                            krel[:, mt, :], krel[:, mt, :], krel[:, mt, :], ALU.mult),
                            reads=["krel"], writes=["krel"])
                    fetch_wv(0, 0)
                    for mt in range(KT):
                        pp2 = mt % 2
                        if mt + 1 < KT:
                            fetch_wv(mt + 1, (mt + 1) % 2)
                        elif c + 1 < NCH:
                            pass
                        bk_ = "pb%d" % (4 + mt % 2)
                        for kt in range(KT):
                            p.emit("t", lambda e, mt=mt, kt=kt, pp2=pp2: e.matmul(
                                pb[4 + mt % 2][:, 0:PC], wvg2[:, pp2, kt, :],
                                krel[:, kt, :], start=(kt == 0), stop=(kt == KT - 1)),
                                reads=["wvg2%d" % pp2, "krel"], writes=[bk_])
                        p.emit("v", lambda e, mt=mt: e.tensor_tensor(
                            t2[:, 0:PC], pb[4 + mt % 2][:, 0:PC], rbuf[:, mt, :], ALU.mult),
                            reads=[bk_, "rbuf"], writes=["t2"])
                        p.emit("v", lambda e, mt=mt, cs=cs: e.scalar_tensor_tensor(
                            resid[:, mt, cs:cs + PC], t2[:, 0:PC], 0.5, resid[:, mt, cs:cs + PC],
                            ALU.mult, ALU.add),
                            reads=["t2", "resid"], writes=["resid"])
            p.barrier()

        def l1attn(stage=99):
            sub = contextlib.ExitStack()
            with sub:
                ln_stats()
                wmini = sub.enter_context(nc.sbuf_tensor(uname("wmini"), [128, 2, 2048], BF16))
                load_weight(wmini[:, :, 0:256], "bmT", "wmini", 2)
                load_weight(wmini[:, :, 256:512], "amat", "wmini", 2)
                load_weight(wmini[:, :, 512:768], "amatT", "wmini", 2)
                load_weight(wmini[:, :, 768:1792], "cmT", "wmini", 2)
                load_weight(wbig[:, :, 0:256], "winT", "wbig", 8)
                wpow = sub.enter_context(nc.sbuf_tensor(uname("wpow"), [128, 2, 1536], BF16))
                cb = sub.enter_context(nc.sbuf_tensor(uname("cbuf"), [128, 2, 128], F32))
                hh = sub.enter_context(nc.sbuf_tensor(uname("hhbuf"), [128, 2, 8], F32))
                hh_bf = sub.enter_context(nc.sbuf_tensor(uname("hhbf"), [128, 2, 8], BF16))

                amat = wmini[:, :, 256:512]
                amatT = wmini[:, :, 512:768]
                bmT = wmini[:, :, 0:256]
                cmT = wmini[:, :, 768:1792]
                uby = fsraw[:, 0:2048].bitcast(BF16).rearrange("p (k c) -> p k c", k=2)
                ubuf = arena[:, 0:2, 0:2048]
                kstg = arena[:, 2:4, 0:512]
                hst = arena[:, 4:6, 0:128]
                hst3 = arena[:, 6:8, 0:1]

                for c in range(S // LNC):
                    cs = c * LNC
                    bcast_mr(cs, LNC)
                    xnorm_apply(lambda kt: xstat[:, kt, :], "xstat", cs, LNC)
                    for mt in range(2):
                        bk_ = "pb%d" % (mt % 2)
                        for kt in range(KT):
                            p.emit("t", lambda e, mt=mt, kt=kt: e.matmul(
                                pb[mt % 2][:, 0:LNC], wbig[:, kt, 128 * mt:128 * mt + 128],
                                xstat[:, kt, :], start=(kt == 0), stop=(kt == KT - 1)),
                                reads=["wbig", "xstat"], writes=[bk_])
                        p.emit("s", lambda e, mt=mt, cs=cs: e.activation(
                            ubuf[:, mt, cs:cs + LNC], pb[mt % 2][:, 0:LNC], AF.Identity,
                            bias=bias["binb"][:, mt:mt + 1]),
                            reads=[bk_, "bias"], writes=["ubuf"])
                if stage < 20:
                    p.barrier()
                    return
                for c in range(4):
                    cs = c * UC
                    for mt in range(2):
                        bk_ = "pb%d" % (2 + mt % 2)
                        for kt in range(2):
                            p.emit("t", lambda e, mt=mt, kt=kt, cs=cs: e.matmul(
                                pb[2 + mt % 2][:, 0:UC], bmT[:, kt, 128 * mt:128 * mt + 128],
                                ubuf[:, kt, cs:cs + UC], start=(kt == 0), stop=(kt == 1)),
                                reads=["wmini", "ubuf"], writes=[bk_])
                        p.emit("s", lambda e, mt=mt, cs=cs: e.activation(
                            uby[:, mt, cs:cs + UC], pb[2 + mt % 2][:, 0:UC], AF.Copy),
                            reads=[bk_], writes=["uby"])

                def square2(src, srcT, skeys, dst, dkey, dstT=None, dkeyT=None):
                    for mt in range(2):
                        bk_ = "pb%d" % (mt % 2)
                        for kt in range(2):
                            p.emit("t", lambda e, mt=mt, kt=kt: e.matmul(
                                pb[mt % 2][:, 0:256], srcT[:, kt, 128 * mt:128 * mt + 128],
                                src[:, kt, :], start=(kt == 0), stop=(kt == 1)),
                                reads=list(skeys), writes=[bk_])
                        p.emit("s", lambda e, mt=mt: e.activation(dst[:, mt, :], pb[mt % 2][:, 0:256], AF.Copy),
                               reads=[bk_], writes=[dkey])
                    if dstT is None:
                        return
                    for mt in range(2):
                        bk_ = "pb%d" % (2 + mt % 2)
                        for kt in range(2):
                            p.emit("t", lambda e, mt=mt, kt=kt: e.matmul(
                                pb[2 + mt % 2][:, 0:256], src[:, kt, 128 * mt:128 * mt + 128],
                                srcT[:, kt, :], start=(kt == 0), stop=(kt == 1)),
                                reads=list(skeys), writes=[bk_])
                        p.emit("s", lambda e, mt=mt: e.activation(dstT[:, mt, :], pb[2 + mt % 2][:, 0:256], AF.Copy),
                               reads=[bk_], writes=[dkeyT])

                if stage < 30:
                    p.barrier()
                    return
                p0 = wpow[:, :, 0:256]
                p0T = wpow[:, :, 256:512]
                p1 = wpow[:, :, 512:768]
                p1T = wpow[:, :, 768:1024]
                b16 = wpow[:, :, 1024:1280]
                b256 = wpow[:, :, 1280:1536]
                square2(amat, amatT, ["wmini"], p0, "p0", p0T, "p0T")
                square2(p0, p0T, ["p0", "p0T"], p1, "p1", p1T, "p1T")
                square2(p1, p1T, ["p1", "p1T"], p0, "p0", p0T, "p0T")
                square2(p0, p0T, ["p0", "p0T"], b16, "b16", p1T, "p1T")
                square2(b16, p1T, ["b16", "p1T"], p0, "p0", p0T, "p0T")
                square2(p0, p0T, ["p0", "p0T"], p1, "p1", p1T, "p1T")
                square2(p1, p1T, ["p1", "p1T"], p0, "p0", p0T, "p0T")
                square2(p0, p0T, ["p0", "p0T"], b256, "b256")

                def scan_local(buf_ap, buf_key, Lc, Cc, m_ap, m_keys, hst_ap, hst_key, f32buf):
                    for mt in range(2):
                        p.emit("s", lambda e, mt=mt: e.activation(
                            hst_ap[:, mt, 0:Cc], buf_ap[:, mt, 0::Lc], AF.Copy),
                            reads=[buf_key], writes=[hst_key])
                    for j in range(1, Lc):
                        for mt in range(2):
                            for kt in range(2):
                                p.emit("t", lambda e, mt=mt, kt=kt: e.matmul(
                                    pb[6][:, 128 * mt:128 * mt + Cc] if Cc == 128 else pb[6][:, 8 * mt:8 * mt + Cc],
                                    m_ap[:, kt, 128 * mt:128 * mt + 128], hst_ap[:, kt, 0:Cc],
                                    start=(kt == 0), stop=(kt == 1)),
                                    reads=list(m_keys) + [hst_key], writes=["pb6"])
                        for mt in range(2):
                            src = pb[6][:, 128 * mt:128 * mt + Cc] if Cc == 128 else pb[6][:, 8 * mt:8 * mt + Cc]
                            p.emit("v", lambda e, mt=mt, j=j, src=src: e.tensor_tensor(
                                buf_ap[:, mt, j::Lc], src, buf_ap[:, mt, j::Lc], ALU.add),
                                reads=["pb6", buf_key], writes=[buf_key])
                        for mt in range(2):
                            p.emit("s", lambda e, mt=mt, j=j: e.activation(
                                hst_ap[:, mt, 0:Cc], buf_ap[:, mt, j::Lc], AF.Copy),
                                reads=[buf_key], writes=[hst_key])

                def scan_expand(buf_ap, buf_key, Lc, Cc, m_ap, m_keys, w_ap, w_key):
                    for j in range(Lc):
                        for mt in range(2):
                            for kt in range(2):
                                p.emit("t", lambda e, mt=mt, kt=kt: e.matmul(
                                    pb[7][:, 128 * mt:128 * mt + Cc] if Cc == 128 else pb[7][:, 8 * mt:8 * mt + Cc],
                                    m_ap[:, kt, 128 * mt:128 * mt + 128], w_ap[:, kt, 0:Cc],
                                    start=(kt == 0), stop=(kt == 1)),
                                    reads=list(m_keys) + [w_key], writes=["pb7"])
                        for mt in range(2):
                            src = pb[7][:, 128 * mt:128 * mt + Cc] if Cc == 128 else pb[7][:, 8 * mt:8 * mt + Cc]
                            p.emit("v", lambda e, mt=mt, j=j, src=src: e.tensor_tensor(
                                buf_ap[:, mt, j::Lc], src, buf_ap[:, mt, j::Lc], ALU.add),
                                reads=["pb7", buf_key], writes=[buf_key])
                        if j < Lc - 1:
                            for mt in range(2):
                                src = pb[7][:, 128 * mt:128 * mt + Cc] if Cc == 128 else pb[7][:, 8 * mt:8 * mt + Cc]
                                p.emit("v", lambda e, mt=mt, src=src: e.tensor_copy(
                                    w_ap[:, mt, 0:Cc], src),
                                    reads=["pb7"], writes=[w_key])

                if stage < 40:
                    p.barrier()
                    return
                scan_local(uby, "uby", 16, 128, amat, ["wmini"], hst, "hst", None)
                if stage < 50:
                    p.barrier()
                    return
                for mt in range(2):
                    p.emit("v", lambda e, mt=mt: e.tensor_copy(cb[:, mt, :], uby[:, mt, 15::16]),
                           reads=["uby"], writes=["cbuf"])
                if stage < 51:
                    p.barrier()
                    return
                scan_local(cb, "cbuf", 16, 8, b16, ["b16"], hst, "hst", None)
                if stage < 52:
                    p.barrier()
                    return
                p.emit("v", lambda e: e.memset(hh[:, :, 0:1], 0.0), writes=["hh"])
                p.emit("v", lambda e: e.tensor_copy(hh[:, :, 1:2], cb[:, :, 15:16]),
                       reads=["cbuf"], writes=["hh"])
                for g2 in range(2, 8):
                    for mt in range(2):
                        p.emit("s", lambda e, mt=mt, g2=g2: e.activation(
                            hst3[:, mt, :], hh[:, mt, g2 - 1:g2], AF.Copy),
                            reads=["hh"], writes=["hst3"])
                    for mt in range(2):
                        for kt in range(2):
                            p.emit("t", lambda e, mt=mt, kt=kt: e.matmul(
                                pb[6][:, 8 * mt:8 * mt + 1], b256[:, kt, 128 * mt:128 * mt + 128],
                                hst3[:, kt, :], start=(kt == 0), stop=(kt == 1)),
                                reads=["b256", "hst3"], writes=["pb6"])
                    for mt in range(2):
                        p.emit("v", lambda e, mt=mt, g2=g2: e.tensor_tensor(
                            hh[:, mt, g2:g2 + 1], pb[6][:, 8 * mt:8 * mt + 1],
                            cb[:, mt, 16 * (g2 - 1) + 15:16 * (g2 - 1) + 16], ALU.add),
                            reads=["pb6", "cbuf"], writes=["hh"])
                if stage < 53:
                    p.barrier()
                    return
                p.emit("s", lambda e: e.activation(hh_bf[:, :, :], hh[:, :, :], AF.Copy),
                       reads=["hh"], writes=["hhbf"])
                scan_expand(cb, "cbuf", 16, 8, b16, ["b16"], hh_bf, "hhbf")
                if stage < 60:
                    p.barrier()
                    return
                p.emit("v", lambda e: e.memset(hst[:, :, 0:1], 0.0), writes=["hst"])
                for mt in range(2):
                    p.emit("s", lambda e, mt=mt: e.activation(hst[:, mt, 1:128], cb[:, mt, 0:127], AF.Copy),
                           reads=["cbuf"], writes=["hst"])
                scan_expand(uby, "uby", 16, 128, amat, ["wmini"], hst, "hst")

                if stage < 70:
                    p.barrier()
                    return
                for c in range(4):
                    cs = c * UC
                    for mt in range(2):
                        p.emit("s", lambda e, mt=mt, cs=cs: e.activation(
                            kstg[:, mt, :], uby[:, mt, cs:cs + UC], AF.Copy),
                            reads=["uby"], writes=["kstg"])
                    for mt in range(KT):
                        bk_ = "pb%d" % (mt % 2)
                        for kt in range(2):
                            p.emit("t", lambda e, mt=mt, kt=kt: e.matmul(
                                pb[mt % 2][:, 0:UC], cmT[:, kt, 128 * mt:128 * mt + 128],
                                kstg[:, kt, :], start=(kt == 0), stop=(kt == 1)),
                                reads=["wmini", "kstg"], writes=[bk_])
                        p.emit("v", lambda e, mt=mt, cs=cs: e.scalar_tensor_tensor(
                            resid[:, mt, cs:cs + UC], pb[mt % 2][:, 0:UC],
                            bias["ddv"][:, mt:mt + 1], resid[:, mt, cs:cs + UC],
                            ALU.add, ALU.add),
                            reads=[bk_, "bias", "resid"], writes=["resid"])
            p.barrier()

        subs = [l0ffn, l0attn, l0ffn, l1ffn,
                lambda: l1attn(stage=l1a_stage), l1ffn]
        for i in range(min(n_sub, 6)):
            subs[i]()

        doutT = dout.transpose([1, 0, 2])
        for kt in range(KT):
            p.emit("d", lambda e, kt=kt: e.dma_start(doutT[:, kt, :], resid[:, kt, :]),
                   reads=["resid"], inc=16)
        p._wait("t", ("d", p.cnt["d"]))
    library_overlay.lower_extended_insts(nc)
    return nc


_CACHE = {}


def kernel(**inputs):
    n_sub = int(inputs.pop("_n_sub", 6))
    l1a_stage = int(inputs.pop("_l1a_stage", 99))
    ck = (n_sub, l1a_stage)
    if ck not in _CACHE:
        _CACHE[ck] = build_program(n_sub, l1a_stage)
    nc = _CACHE[ck]

    def f32(a):
        return np.ascontiguousarray(np.asarray(a), dtype=np.float32)

    g = {k: f32(v) for k, v in inputs.items()}
    shared = {
        "wqT": f32(g["L0_attn_wq"].T).reshape(8, 128, 1024),
        "wkT": f32(g["L0_attn_wk"].T).reshape(8, 128, 1024),
        "wvT": f32(g["L0_attn_wv"].T).reshape(8, 128, 1024),
        "woT": f32(g["L0_attn_wo"].T).reshape(8, 128, 1024),
        "bq": f32(g["L0_attn_bq"].reshape(8, 128).T),
        "bk": f32(g["L0_attn_bk"].reshape(8, 128).T),
        "bo": f32(g["L0_attn_bo"].reshape(8, 128).T),
        "bvrow": f32(g["L0_attn_bv"].reshape(1, 1024)),
        "cw1": f32(g["L0_ffn_cn_w1"]).reshape(8, 128, 1024),
        "cw2": f32(g["L0_ffn_cn_w2"]).reshape(8, 128, 512),
        "cw3": f32(g["L0_ffn_cn_w3"]).reshape(4, 128, 256),
        "cw4": f32(g["L0_ffn_cn_w4"].reshape(2, 128).T),
        "gw1": f32(g["L0_ffn_gc_w1"]).reshape(8, 128, 512),
        "gw2": f32(g["L0_ffn_gc_w2"]).reshape(4, 128, 256),
        "gw3": f32(g["L0_ffn_gc_w3"].reshape(2, 128).T),
        "cb1": f32(g["L0_ffn_cn_b1"].reshape(8, 128).T),
        "cb2": f32(g["L0_ffn_cn_b2"].reshape(4, 128).T),
        "cb3": f32(g["L0_ffn_cn_b3"].reshape(2, 128).T),
        "cb4": f32(g["L0_ffn_cn_b4"].reshape(1, 1)),
        "gb1": f32(g["L0_ffn_gc_b1"].reshape(4, 128).T),
        "gb2": f32(g["L0_ffn_gc_b2"].reshape(2, 128).T),
        "gb3": f32(g["L0_ffn_gc_b3"].reshape(1, 1)),
        "arr": f32(g["L0_ffn_adaptive_range"].reshape(1, 1)),
        "wrT": f32(g["L1_ffn_W_r"].T).reshape(8, 128, 1024),
        "wkfT": f32(g["L1_ffn_W_k"].T).reshape(8, 128, 1024),
        "wvfT": f32(g["L1_ffn_W_v"].T).reshape(8, 128, 1024),
        "mur": f32(g["L1_ffn_mu_r"].reshape(8, 128).T),
        "muk": f32(g["L1_ffn_mu_k"].reshape(8, 128).T),
        "winT": f32(g["L1_attn_W_in"].T).reshape(8, 128, 256),
        "bmT": f32(g["L1_attn_Bm"].T).reshape(2, 128, 256),
        "amat": f32(g["L1_attn_A"]).reshape(2, 128, 256),
        "amatT": f32(g["L1_attn_A"].T).reshape(2, 128, 256),
        "cmT": f32(g["L1_attn_C"].T).reshape(2, 128, 1024),
        "binb": f32(g["L1_attn_b_in"].reshape(2, 128).T),
        "ddv": f32(g["L1_attn_D"].reshape(8, 128).T),
    }
    x = g["L0_attn_x"]
    in_maps = []
    for b in range(8):
        m = dict(shared)
        m["xT"] = f32(x[b].T).reshape(8, 128, 2048)
        in_maps.append(m)
    res = run_bass_kernel_spmd(nc, in_maps, core_ids=list(range(8)))
    outs = []
    for r in res.results:
        o = np.asarray(r["out"], dtype=np.float32).reshape(1024, 2048)
        outs.append(o.T)
    return np.stack(outs, axis=0)


if __name__ == "__main__":
    import reference as _ref
    inp = {k: np.asarray(v) for k, v in _ref.setup_inputs(0).items()}
    out = kernel(**inp)
    print(out.shape, out.dtype)


# version 09
# speedup vs baseline: 1.8966x; 1.4557x over previous
import contextlib
import numpy as np

import concourse.bass as bass
import concourse.mybir as mybir
from concourse.bass_utils import run_bass_kernel_spmd
from concourse import library_config
from concourse import library_overlay

F32 = mybir.dt.float32
BF16 = mybir.dt.bfloat16
AF = mybir.ActivationFunctionType
ALU = mybir.AluOpType

S = 2048
D = 1024
KT = 8
H = 16
HD = 64
NG = 8
LNC = 256
UC = 512
PC = 512


class Prog:

    def __init__(self, nc):
        self.nc = nc
        self.eng = {"t": nc.tensor, "v": nc.vector, "s": nc.scalar,
                    "g": nc.gpsimd, "d": nc.sync, "gd": nc.gpsimd}
        self.cnt = {k: 0 for k in self.eng}
        self.sem = {}
        self.waited = {k: {} for k in self.eng}
        self.res = {}
        self._es = contextlib.ExitStack()

    def init_sems(self):
        for k in self.eng:
            self.sem[k] = self._es.enter_context(self.nc.semaphore("sem_" + k))

    def _wait(self, stream, tok):
        src, val = tok
        if src == stream and src not in ("d", "gd"):
            pass
        if self.waited[stream].get(src, -1) >= val:
            return
        self.eng[stream].wait_ge(self.sem[src], val)
        self.waited[stream][src] = val

    def emit(self, stream, fn, reads=(), writes=(), inc=1):
        toks = []
        for r in reads:
            st = self.res.get(r)
            if st and st["w"] is not None:
                toks.append(st["w"])
        for w in writes:
            st = self.res.get(w)
            if st:
                if st["w"] is not None:
                    if not (stream == "t" and st["w"][0] == "t" and w.startswith("pb")):
                        toks.append(st["w"])
                toks.extend(st["r"])
        for tok in sorted(set(toks)):
            if tok[0] == stream and tok[1] <= self.cnt[stream]:
                self._wait(stream, tok)
            else:
                self._wait(stream, tok)
        instr = fn(self.eng[stream])
        instr.then_inc(self.sem[stream], inc)
        self.cnt[stream] += inc
        tok = (stream, self.cnt[stream])
        for r in reads:
            st = self.res.setdefault(r, {"w": None, "r": []})
            st["r"].append(tok)
        for w in writes:
            self.res[w] = {"w": tok, "r": []}
        return tok

    def barrier(self):
        snap = dict(self.cnt)
        for stream in ("t", "v", "s", "g", "d"):
            for src, val in snap.items():
                if src != stream and val > 0:
                    self._wait(stream, (src, val))
        self.res = {}


def ceil_div(a, b):
    return -(-a // b)


def jlist(qc):
    out = []
    for j in range(16):
        s = 128 * j - 256 * qc
        if s <= -640 or s >= 768:
            continue
        mask = None
        if s == -512:
            mask = "ml0"
        elif s == -384:
            mask = "ml1"
        elif s == 512:
            mask = "mu0"
        elif s == 640:
            mask = "mu1"
        out.append((j, mask))
    return out


def build_program(n_sub=6, l1a_stage=99):
    nc = bass.Bass("TRN2", target_bir_lowering=False)
    p = Prog(nc)

    dram = {}

    def din(name, shape, dtype=F32):
        dram[name] = nc.dram_tensor(name, list(shape), dtype, kind="ExternalInput").ap()
        return dram[name]

    din("xT", (KT, 128, S))
    dout = nc.dram_tensor("out", [KT, 128, S], F32, kind="ExternalOutput").ap()

    for n in ("wqT", "wkT", "wvT", "woT"):
        din(n, (KT, 128, D))
    for n in ("bq", "bk", "bo"):
        din(n, (128, 8))
    din("bvrow", (1, D))
    din("cw1", (8, 128, 1024)); din("cw2", (8, 128, 512))
    din("cw3", (4, 128, 256)); din("cw4", (128, 2))
    din("gw1", (8, 128, 512)); din("gw2", (4, 128, 256)); din("gw3", (128, 2))
    din("cb1", (128, 8)); din("cb2", (128, 4)); din("cb3", (128, 2)); din("cb4", (1, 1))
    din("gb1", (128, 4)); din("gb2", (128, 2)); din("gb3", (1, 1)); din("arr", (1, 1))
    din("wrT", (8, 128, 1024)); din("wkfT", (8, 128, 1024)); din("wvfT", (8, 128, 1024))
    din("mur", (128, 8)); din("muk", (128, 8))
    din("winT", (8, 128, 256)); din("bmT", (2, 128, 256))
    din("amat", (2, 128, 256)); din("amatT", (2, 128, 256)); din("cmT", (2, 128, 1024))
    din("binb", (128, 2)); din("ddv", (128, 8))

    es = contextlib.ExitStack()

    def sb(name, shape, dtype):
        return es.enter_context(nc.sbuf_tensor(name, list(shape), dtype))

    _uid = [0]

    def uname(name):
        _uid[0] += 1
        return "%s_u%d" % (name, _uid[0])

    with es:
        p.init_sems()
        nc.gpsimd.load_library(library_config.proxy)

        resid = sb("resid", (128, KT, S), F32)
        arena = sb("arena", (128, KT, 2064), BF16)
        xstat = sb("xstat", (128, KT, LNC), BF16)
        xsqst = sb("xsqst", (128, 2, LNC), BF16)
        fsraw = sb("fsraw", (128, 4096), F32)
        mrow = sb("mrow", (1, S), F32)[:, :]
        rsrow = sb("rsrow", (1, S), F32)[:, :]
        srow = sb("srow", (1, 512), F32)[:, :]
        srow2 = sb("srow2", (1, 512), F32)[:, :]
        mb = sb("mb", (128, 512), F32)
        rb = sb("rb", (128, 512), F32)
        t1 = sb("t1", (128, 512), F32)
        t2 = sb("t2", (128, 512), F32)
        wbig = sb("wbig", (128, KT, 2048), BF16)
        onesb = sb("onesb", (128, 1), BF16)
        onesm = sb("onesm", (1, 128), BF16)
        negones = sb("negones", (128, 1), BF16)
        negpad = sb("negpad", (1, 65), BF16)
        ident = sb("ident", (128, 128), BF16)
        masks = {n: sb(n, (128, 256), BF16) for n in ("ml0", "ml1", "mu0", "mu1")}
        bias = {}
        for n, shp in (("bq", (128, 8)), ("bk", (128, 8)), ("bo", (128, 8)),
                       ("cb1", (128, 8)), ("cb2", (128, 4)), ("cb3", (128, 2)),
                       ("cb4", (1, 1)), ("gb1", (128, 4)), ("gb2", (128, 2)),
                       ("gb3", (1, 1)), ("arr", (1, 1)), ("mur", (128, 8)),
                       ("muk", (128, 8)), ("binb", (128, 2)), ("ddv", (128, 8))):
            bias[n] = sb("b_" + n, shp, F32)
        bvrow = sb("bvrow_s", (1, D), BF16)
        scal = sb("scal", (1, 16), F32)

        pb = [es.enter_context(nc.psum_tensor("pb%d" % i, [128, 512], F32))
              for i in range(8)]

        xTsrc = dram["xT"].transpose([1, 0, 2])
        for kt in range(KT):
            p.emit("d", lambda e, kt=kt: e.dma_start(resid[:, kt, :], xTsrc[:, kt, :]),
                   writes=["resid"], inc=16)
        for n in bias:
            nm = n
            p.emit("d", lambda e, nm=nm: e.dma_start(bias[nm][:, :], dram[nm][:, :]),
                   writes=["bias"], inc=16)
        p.emit("gd", lambda e: e.dma_start(bvrow[:, :], dram["bvrow"][:, :]),
               writes=["bvrow"], inc=16)

        p.emit("v", lambda e: e.memset(onesb[:, :], 1.0), writes=["onesb"])
        p.emit("v", lambda e: e.memset(onesm[:, :], 1.0), writes=["onesm"])
        p.emit("v", lambda e: e.memset(negones[:, :], -1.0), writes=["negones"])
        p.emit("v", lambda e: e.memset(negpad[:, 0:64], -512.0), writes=["negpad"])
        p.emit("v", lambda e: e.memset(negpad[:, 64:65], 512.0), writes=["negpad"])
        p.emit("v", lambda e: e.memset(t1[:, 0:128], 1.0), writes=["t1"])
        p.emit("g", lambda e: e.affine_select(t1[:, 0:128], t1[:, 0:128], pattern=[[-1, 128]],
                                              base=0, channel_multiplier=1,
                                              compare_op=ALU.is_equal, fill=0.0),
               reads=["t1"], writes=["t1"])
        p.emit("s", lambda e: e.activation(ident[:, :], t1[:, 0:128], AF.Copy),
               reads=["t1"], writes=["ident"])
        p.emit("v", lambda e: e.memset(scal[:, 10:11], 1e-5), writes=["eps"])
        mspec = {"ml0": (0, 1, -1), "ml1": (128, 1, -1), "mu0": (0, -1, 1), "mu1": (-128, -1, 1)}
        for n, (base, cm, fc) in mspec.items():
            p.emit("v", lambda e: e.memset(t1[:, 0:256], 1.0), writes=["t1"])
            p.emit("g", lambda e, base=base, cm=cm, fc=fc: e.affine_select(
                t1[:, 0:256], t1[:, 0:256], pattern=[[fc, 256]], base=base,
                channel_multiplier=cm, compare_op=ALU.is_ge, fill=0.0),
                reads=["t1"], writes=["t1"])
            p.emit("s", lambda e, n=n: e.activation(masks[n][:, :], t1[:, 0:256], AF.Copy),
                   reads=["t1"], writes=[n])

        def load_weight(dst_ap, src_name, key, kt_tiles):
            src = dram[src_name].transpose([1, 0, 2])
            for kt in range(kt_tiles):
                p.emit("gd", lambda e, kt=kt: e.dma_start(dst_ap[:, kt, :], src[:, kt, :]),
                       writes=[key], inc=16)

        def ln_stats():
            for c in range(S // LNC):
                cs = c * LNC
                p.emit("s", lambda e, cs=cs: e.activation(xstat[:, :, :], resid[:, :, cs:cs + LNC], AF.Copy),
                       reads=["resid"], writes=["xstat"])
                for kt in range(KT):
                    p.emit("t", lambda e, kt=kt: e.matmul(pb[6][0:1, 0:LNC], onesb[:, :], xstat[:, kt, :],
                                                          start=(kt == 0), stop=(kt == KT - 1)),
                           reads=["onesb", "xstat"], writes=["pb6"])
                for kt in range(KT):
                    pp = kt % 2
                    p.emit("s", lambda e, kt=kt, pp=pp, cs=cs: e.activation(
                        xsqst[:, pp, :], resid[:, kt, cs:cs + LNC], AF.Square),
                        reads=["resid"], writes=["xsq%d" % pp])
                    p.emit("t", lambda e, kt=kt, pp=pp: e.matmul(pb[7][0:1, 0:LNC], onesb[:, :], xsqst[:, pp, :],
                                                                 start=(kt == 0), stop=(kt == KT - 1)),
                           reads=["onesb", "xsq%d" % pp], writes=["pb7"])
                p.emit("v", lambda e, cs=cs: e.tensor_scalar_mul(mrow[:, cs:cs + LNC], pb[6][0:1, 0:LNC], 1.0 / D),
                       reads=["pb6"], writes=["mrow"])
                p.emit("v", lambda e: e.tensor_scalar_mul(srow[:, 0:LNC], pb[7][0:1, 0:LNC], 1.0 / D),
                       reads=["pb7"], writes=["srow"])
                p.emit("v", lambda e, cs=cs: e.tensor_tensor(srow2[:, 0:LNC], mrow[:, cs:cs + LNC],
                                                             mrow[:, cs:cs + LNC], ALU.mult),
                       reads=["mrow"], writes=["srow2"])
                p.emit("v", lambda e: e.tensor_tensor(srow[:, 0:LNC], srow[:, 0:LNC], srow2[:, 0:LNC], ALU.subtract),
                       reads=["srow", "srow2"], writes=["srow"])
                p.emit("s", lambda e: e.activation(srow2[:, 0:LNC], srow[:, 0:LNC], AF.Ln, bias=scal[:, 10:11]),
                       reads=["srow", "eps"], writes=["srow2"])
                p.emit("s", lambda e, cs=cs: e.activation(rsrow[:, cs:cs + LNC], srow2[:, 0:LNC], AF.Exp, scale=-0.5),
                       reads=["srow2"], writes=["rsrow"])

        def bcast_mr(cs, n):
            p.emit("g", lambda e: e.partition_broadcast(mb[:, 0:n], mrow[:, cs:cs + n]),
                   reads=["mrow"], writes=["mb"])
            p.emit("g", lambda e: e.partition_broadcast(rb[:, 0:n], rsrow[:, cs:cs + n]),
                   reads=["rsrow"], writes=["rb"])

        def xnorm_apply(dst_ap_fn, dst_key, cs, n):
            for kt in range(KT):
                p.emit("v", lambda e, kt=kt: e.tensor_tensor(t1[:, 0:n], resid[:, kt, cs:cs + n],
                                                             mb[:, 0:n], ALU.subtract),
                       reads=["resid", "mb"], writes=["t1"])
                p.emit("v", lambda e, kt=kt: e.tensor_tensor(dst_ap_fn(kt), t1[:, 0:n], rb[:, 0:n], ALU.mult),
                       reads=["t1", "rb"], writes=[dst_key])

        def l0ffn():
            sub = contextlib.ExitStack()
            with sub:
                ln_stats()
                load_weight(wbig[:, :, 0:1024], "cw1", "wbig", 8)
                load_weight(wbig[:, :, 1024:1536], "cw2", "wbig", 8)
                load_weight(wbig[:, :, 1536:2048], "gw1", "wbig", 8)
                wsm = sub.enter_context(nc.sbuf_tensor(uname("wsm"), [128, 4, 512], BF16))
                w4s = sub.enter_context(nc.sbuf_tensor(uname("w4s"), [128, 4], BF16))
                load_weight(wsm[:, :, 0:256], "cw3", "wsm", 4)
                load_weight(wsm[:, 0:4, 256:512], "gw2", "wsm", 4)
                p.emit("gd", lambda e: e.dma_start(w4s[:, 0:2], dram["cw4"][:, :]),
                       writes=["w4s"], inc=16)
                p.emit("gd", lambda e: e.dma_start(w4s[:, 2:4], dram["gw3"][:, :]),
                       writes=["w4s"], inc=16)
                gxa = sub.enter_context(nc.sbuf_tensor(uname("gxa"), [128, KT, 1], F32))
                gxr = sub.enter_context(nc.sbuf_tensor(uname("gxr"), [128, KT, 1], F32))
                gxbf = sub.enter_context(nc.sbuf_tensor(uname("gxbf"), [128, KT, 1], BF16))
                gh1 = sub.enter_context(nc.sbuf_tensor(uname("gh1"), [128, 4, 1], BF16))
                gh2 = sub.enter_context(nc.sbuf_tensor(uname("gh2"), [128, 2, 1], BF16))

                p.emit("v", lambda e: e.memset(gxa[:, :, :], 0.0), writes=["gxa"])
                for c in range(S // LNC):
                    cs = c * LNC
                    bcast_mr(cs, LNC)
                    xnorm_apply(lambda kt: xstat[:, kt, :], "xstat", cs, LNC)
                    p.emit("v", lambda e: e.reduce_sum(gxr[:, :, :], xstat[:, :, :], mybir.AxisListType.X),
                           reads=["xstat"], writes=["gxr"])
                    p.emit("v", lambda e: e.tensor_tensor(gxa[:, :, :], gxa[:, :, :], gxr[:, :, :], ALU.add),
                           reads=["gxa", "gxr"], writes=["gxa"])
                p.emit("s", lambda e: e.activation(gxbf[:, :, :], gxa[:, :, :], AF.Copy),
                       reads=["gxa"], writes=["gxbf"])
                for mt in range(4):
                    for kt in range(KT):
                        p.emit("t", lambda e, mt=mt, kt=kt: e.matmul(
                            pb[2][:, mt:mt + 1], wbig[:, kt, 1536 + 128 * mt:1536 + 128 * mt + 128],
                            gxbf[:, kt, :], start=(kt == 0), stop=(kt == KT - 1)),
                            reads=["wbig", "gxbf"], writes=["pb2"])
                for mt in range(4):
                    p.emit("s", lambda e, mt=mt: e.activation(gh1[:, mt, :], pb[2][:, mt:mt + 1], AF.Gelu,
                                                              bias=bias["gb1"][:, mt:mt + 1], scale=1.0 / S),
                           reads=["pb2", "bias"], writes=["gh1"])
                for mt in range(2):
                    for kt in range(4):
                        p.emit("t", lambda e, mt=mt, kt=kt: e.matmul(
                            pb[3][:, mt:mt + 1], wsm[:, kt, 256 + 128 * mt:256 + 128 * mt + 128],
                            gh1[:, kt, :], start=(kt == 0), stop=(kt == 3)),
                            reads=["wsm", "gh1"], writes=["pb3"])
                for mt in range(2):
                    p.emit("s", lambda e, mt=mt: e.activation(gh2[:, mt, :], pb[3][:, mt:mt + 1], AF.Gelu,
                                                              bias=bias["gb2"][:, mt:mt + 1]),
                           reads=["pb3", "bias"], writes=["gh2"])
                for kt in range(2):
                    p.emit("t", lambda e, kt=kt: e.matmul(pb[4][0:1, 0:1], w4s[:, 2 + kt:3 + kt],
                                                          gh2[:, kt, :], start=(kt == 0), stop=(kt == 1)),
                           reads=["w4s", "gh2"], writes=["pb4"])
                p.emit("v", lambda e: e.tensor_scalar_mul(scal[:, 1:2], bias["gb3"][:, :], -1.0),
                       reads=["bias"], writes=["scal"])
                p.emit("s", lambda e: e.activation(scal[:, 2:3], pb[4][0:1, 0:1], AF.Exp,
                                                   scale=-1.0, bias=scal[:, 1:2]),
                       reads=["pb4", "scal"], writes=["scal2"])
                p.emit("v", lambda e: e.tensor_scalar_add(scal[:, 3:4], scal[:, 2:3], 1.0),
                       reads=["scal2"], writes=["scal3"])
                p.emit("v", lambda e: e.reciprocal(scal[:, 0:1], scal[:, 3:4]),
                       reads=["scal3"], writes=["scal0"])
                p.emit("v", lambda e: e.tensor_tensor(scal[:, 4:5], bias["arr"][:, :], scal[:, 0:1], ALU.mult),
                       reads=["bias", "scal0"], writes=["scal4"])
                p.emit("v", lambda e: e.tensor_scalar_mul(scal[:, 5:6], scal[:, 4:5], 1.4),
                       reads=["scal4"], writes=["scal5"])
                p.emit("v", lambda e: e.tensor_scalar(scal[:, 6:7], bias["arr"][:, :], -1.0, 0.1, ALU.mult, ALU.add),
                       reads=["bias"], writes=["scal6"])
                p.emit("v", lambda e: e.tensor_tensor(scal[:, 7:8], scal[:, 5:6], scal[:, 6:7], ALU.add),
                       reads=["scal5", "scal6"], writes=["c0"])
                p.emit("v", lambda e: e.tensor_scalar_mul(scal[:, 8:9], bias["arr"][:, :], 0.6),
                       reads=["bias"], writes=["c1"])
                p.emit("v", lambda e: e.tensor_scalar_mul(scal[:, 9:10], bias["cb4"][:, :], -1.0),
                       reads=["bias"], writes=["scal9"])

                op = fsraw[:, :].rearrange("p (k c) -> p k c", k=KT)
                obf = arena[:, :, 0:PC]
                h1 = arena[:, :, PC:2 * PC]
                h2 = arena[:, 0:4, 2 * PC:3 * PC]
                h3 = arena[:, 0:2, 3 * PC:4 * PC]
                pairs = [(0, 4, 1.0), (1, 5, 1.0), (2, 6, -1.0), (3, 7, -1.0)]
                for c in range(S // PC):
                    cs = c * PC
                    bcast_mr(cs, PC)
                    xnorm_apply(lambda kt: op[:, kt, :], "op", cs, PC)
                    p.emit("s", lambda e: e.activation(obf, op[:, :, :], AF.Copy),
                           reads=["op"], writes=["obf"])
                    for r in range(3):
                        for mt in range(KT):
                            bk = "pb%d" % (2 + mt % 2)
                            for kt in range(KT):
                                p.emit("t", lambda e, mt=mt, kt=kt: e.matmul(
                                    pb[2 + mt % 2][:, 0:PC], wbig[:, kt, 128 * mt:128 * mt + 128],
                                    obf[:, kt, :], start=(kt == 0), stop=(kt == KT - 1)),
                                    reads=["wbig", "obf"], writes=[bk])
                            p.emit("s", lambda e, mt=mt: e.activation(
                                h1[:, mt, :], pb[2 + mt % 2][:, 0:PC], AF.Gelu,
                                bias=bias["cb1"][:, mt:mt + 1]),
                                reads=[bk, "bias"], writes=["h1"])
                        for mt in range(4):
                            bk = "pb%d" % (4 + mt % 2)
                            for kt in range(KT):
                                p.emit("t", lambda e, mt=mt, kt=kt: e.matmul(
                                    pb[4 + mt % 2][:, 0:PC], wbig[:, kt, 1024 + 128 * mt:1024 + 128 * mt + 128],
                                    h1[:, kt, :], start=(kt == 0), stop=(kt == KT - 1)),
                                    reads=["wbig", "h1"], writes=[bk])
                            p.emit("s", lambda e, mt=mt: e.activation(
                                h2[:, mt, :], pb[4 + mt % 2][:, 0:PC], AF.Gelu,
                                bias=bias["cb2"][:, mt:mt + 1]),
                                reads=[bk, "bias"], writes=["h2"])
                        for mt in range(2):
                            bk = "pb%d" % (6 + mt % 2)
                            for kt in range(4):
                                p.emit("t", lambda e, mt=mt, kt=kt: e.matmul(
                                    pb[6 + mt % 2][:, 0:PC], wsm[:, kt, 128 * mt:128 * mt + 128],
                                    h2[:, kt, :], start=(kt == 0), stop=(kt == 3)),
                                    reads=["wsm", "h2"], writes=[bk])
                            p.emit("s", lambda e, mt=mt: e.activation(
                                h3[:, mt, :], pb[6 + mt % 2][:, 0:PC], AF.Gelu,
                                bias=bias["cb3"][:, mt:mt + 1]),
                                reads=[bk, "bias"], writes=["h3"])
                        for kt in range(2):
                            p.emit("t", lambda e, kt=kt: e.matmul(
                                pb[4][0:1, 0:PC], w4s[:, kt:kt + 1], h3[:, kt, :],
                                start=(kt == 0), stop=(kt == 1)),
                                reads=["w4s", "h3"], writes=["pb4"])
                        p.emit("s", lambda e: e.activation(srow[:, 0:PC], pb[4][0:1, 0:PC], AF.Exp,
                                                           scale=-1.0, bias=scal[:, 9:10]),
                               reads=["pb4", "scal9"], writes=["srow"])
                        p.emit("v", lambda e: e.tensor_scalar_add(srow[:, 0:PC], srow[:, 0:PC], 1.0),
                               reads=["srow"], writes=["srow"])
                        p.emit("v", lambda e: e.reciprocal(srow2[:, 0:PC], srow[:, 0:PC]),
                               reads=["srow"], writes=["srow2"])
                        p.emit("v", lambda e: e.tensor_scalar(srow[:, 0:PC], srow2[:, 0:PC],
                                                              scal[:, 8:9], scal[:, 7:8], ALU.mult, ALU.add),
                               reads=["srow2", "c0", "c1"], writes=["srow"])
                        p.emit("g", lambda e: e.partition_broadcast(mb[:, 0:PC], srow[:, 0:PC]),
                               reads=["srow"], writes=["mb"])
                        for (a, b, sa) in pairs:
                            p.emit("v", lambda e, b=b: e.tensor_tensor(t1[:, 0:PC], mb[:, 0:PC], op[:, b, :], ALU.mult),
                                   reads=["mb", "op"], writes=["t1"])
                            p.emit("v", lambda e, a=a: e.tensor_tensor(t2[:, 0:PC], mb[:, 0:PC], op[:, a, :], ALU.mult),
                                   reads=["mb", "op"], writes=["t2"])
                            p.emit("v", lambda e, a=a, sa=sa: e.tensor_tensor(
                                op[:, a, :], op[:, a, :], t1[:, 0:PC], ALU.add if sa > 0 else ALU.subtract),
                                reads=["op", "t1"], writes=["op"])
                            p.emit("v", lambda e, b=b, sa=sa: e.tensor_tensor(
                                op[:, b, :], op[:, b, :], t2[:, 0:PC], ALU.subtract if sa > 0 else ALU.add),
                                reads=["op", "t2"], writes=["op"])
                        if r < 2:
                            p.emit("s", lambda e: e.activation(obf, op[:, :, :], AF.Copy),
                                   reads=["op"], writes=["obf"])
                    for kt in range(KT):
                        p.emit("v", lambda e, kt=kt, cs=cs: e.scalar_tensor_tensor(
                            resid[:, kt, cs:cs + PC], op[:, kt, :], 0.5, resid[:, kt, cs:cs + PC],
                            ALU.mult, ALU.add),
                            reads=["op", "resid"], writes=["resid"])
            p.barrier()

        def l0attn():
            sub = contextlib.ExitStack()
            with sub:
                ln_stats()
                load_weight(wbig[:, :, 0:1024], "wqT", "wbig", 8)
                load_weight(wbig[:, :, 1024:2048], "wkT", "wbig", 8)
                xnorm = arena[:, :, 0:2048]
                for c in range(S // LNC):
                    cs = c * LNC
                    bcast_mr(cs, LNC)
                    xnorm_apply(lambda kt, cs=cs: xnorm[:, kt, cs:cs + LNC], "xnorm", cs, LNC)
                qb = sub.enter_context(nc.sbuf_tensor(uname("qb"), [128, S], BF16))
                kb = sub.enter_context(nc.sbuf_tensor(uname("kb"), [128, S], BF16))
                vaug = sub.enter_context(nc.sbuf_tensor(uname("vaug"), [128, 16, 2, 65], BF16))
                probs = sub.enter_context(nc.sbuf_tensor(uname("probs"), [128, 10, 256], BF16))
                ao = xstat[:, 0, :]
                epadb = sub.enter_context(nc.sbuf_tensor(uname("epadb"), [1, 2, 1024], BF16))
                wvg = sub.enter_context(nc.sbuf_tensor(uname("wvg"), [128, KT, 128], BF16))
                wog = sub.enter_context(nc.sbuf_tensor(uname("wog"), [128, 1024], BF16))

                wvT = dram["wvT"]
                woT = dram["woT"]

                def fetch_group(g):
                    srcv = wvT[:, :, 128 * g:128 * g + 128].transpose([1, 0, 2])
                    for kt in range(KT):
                        p.emit("gd", lambda e, kt=kt: e.dma_start(wvg[:, kt, :], srcv[:, kt, :]),
                               writes=["wvg"], inc=16)
                    p.emit("gd", lambda e, g=g: e.dma_start(wog[:, :], woT[g, :, :]),
                           writes=["wog"], inc=16)

                for g in range(NG):
                    fetch_group(g)
                    for dst, wofs, bn, dk in ((qb, 0, "bq", "qb"), (kb, 1024, "bk", "kb")):
                        for nb in range(4):
                            bk_ = "pb%d" % (6 + nb % 2)
                            for kt in range(KT):
                                p.emit("t", lambda e, nb=nb, kt=kt, wofs=wofs, g=g: e.matmul(
                                    pb[6 + nb % 2][:, 0:512],
                                    wbig[:, kt, wofs + 128 * g:wofs + 128 * g + 128],
                                    xnorm[:, kt, 512 * nb:512 * nb + 512],
                                    start=(kt == 0), stop=(kt == KT - 1)),
                                    reads=["wbig", "xnorm"], writes=[bk_])
                            p.emit("s", lambda e, nb=nb, dst=dst, bn=bn, g=g: e.activation(
                                dst[:, 512 * nb:512 * nb + 512], pb[6 + nb % 2][:, 0:512],
                                AF.Identity, bias=bias[bn][:, g:g + 1]),
                                reads=[bk_, "bias"], writes=[dk])
                    for hl in range(2):
                        for half, q0 in ((0, 0), (1, 1536)):
                            p.emit("t", lambda e, hl=hl, q0=q0: e.matmul(
                                pb[6][0:1, 0:512], negones[64 * hl:64 * hl + 64, :],
                                qb[64 * hl:64 * hl + 64, q0:q0 + 512], start=True, stop=True),
                                reads=["qb", "negones"], writes=["pb6"])
                            p.emit("s", lambda e, hl=hl, half=half: e.activation(
                                epadb[:, hl, 512 * half:512 * half + 512], pb[6][0:1, 0:512], AF.Exp),
                                reads=["pb6"], writes=["epadb"])
                    p.emit("v", lambda e: e.memset(vaug[:, :, :, 64:65], 1.0), writes=["vaug"])
                    for tt in range(16):
                        bk_ = "pb%d" % (6 + tt % 2)
                        for kt in range(KT):
                            p.emit("t", lambda e, tt=tt, kt=kt: e.matmul(
                                pb[6 + tt % 2][:, 0:128], xnorm[:, kt, 128 * tt:128 * tt + 128],
                                wvg[:, kt, :], start=(kt == 0), stop=False),
                                reads=["wvg", "xnorm"], writes=[bk_])
                        p.emit("t", lambda e, tt=tt, g=g: e.matmul(
                            pb[6 + tt % 2][:, 0:128], onesm[:, :],
                            bvrow[:, 128 * g:128 * g + 128], start=False, stop=True),
                            reads=["onesm", "bvrow"], writes=[bk_])
                        p.emit("s", lambda e, tt=tt: e.activation(
                            vaug[:, tt, :, 0:64],
                            pb[6 + tt % 2][:, 0:128].rearrange("p (h d) -> p h d", h=2), AF.Copy),
                            reads=[bk_], writes=["vaug"])
                    for qc in range(8):
                        jl = jlist(qc)
                        pad = qc in (0, 1, 6, 7)
                        if qc < 2:
                            pofs = 256 * qc
                        else:
                            pofs = 512 + 256 * (qc - 6)
                        for hl in range(2):
                            for idx, (j, mk) in enumerate(jl):
                                bk_ = "pb%d" % (idx // 2)
                                p.emit("t", lambda e, j=j, hl=hl, qc=qc, idx=idx: e.matmul(
                                    pb[idx // 2][:, 256 * (idx % 2):256 * (idx % 2) + 256],
                                    kb[64 * hl:64 * hl + 64, 128 * j:128 * j + 128],
                                    qb[64 * hl:64 * hl + 64, 256 * qc:256 * qc + 256],
                                    start=True, stop=True),
                                    reads=["kb", "qb"], writes=[bk_ + "_%d" % (idx % 2)])
                            for idx, (j, mk) in enumerate(jl):
                                p.emit("s", lambda e, idx=idx: e.activation(
                                    probs[:, idx, :], pb[idx // 2][:, 256 * (idx % 2):256 * (idx % 2) + 256],
                                    AF.Exp),
                                    reads=["pb%d_%d" % (idx // 2, idx % 2)], writes=["probs%d" % idx])
                            for idx, (j, mk) in enumerate(jl):
                                if mk is not None:
                                    p.emit("v", lambda e, idx=idx, mk=mk: e.tensor_tensor(
                                        probs[:, idx, :], probs[:, idx, :], masks[mk][:, :], ALU.mult),
                                        reads=["probs%d" % idx, mk], writes=["probs%d" % idx])
                            for idx, (j, mk) in enumerate(jl):
                                p.emit("t", lambda e, j=j, hl=hl, idx=idx: e.matmul(
                                    pb[5][0:65, 0:256], vaug[:, j, hl, :], probs[:, idx, :],
                                    start=(idx == 0), stop=(idx == len(jl) - 1 and not pad)),
                                    reads=["vaug", "probs%d" % idx], writes=["pb5"])
                            if pad:
                                p.emit("t", lambda e, hl=hl, pofs=pofs: e.matmul(
                                    pb[5][0:65, 0:256], negpad[:, :], epadb[:, hl, pofs:pofs + 256],
                                    start=False, stop=True),
                                    reads=["negpad", "epadb"], writes=["pb5"])
                            p.emit("v", lambda e: e.reciprocal(srow2[:, 0:256], pb[5][64:65, 0:256]),
                                   reads=["pb5"], writes=["srow2"])
                            p.emit("g", lambda e: e.partition_broadcast(t2[:, 0:256], srow2[:, 0:256]),
                                   reads=["srow2"], writes=["t2"])
                            p.emit("v", lambda e, hl=hl: e.tensor_tensor(
                                ao[64 * hl:64 * hl + 64, :], pb[5][0:64, 0:256], t2[0:64, 0:256], ALU.mult),
                                reads=["pb5", "t2"], writes=["xstat"])
                        for mt in range(KT):
                            bk_ = "pb7"
                            p.emit("t", lambda e, mt=mt: e.matmul(
                                pb[7][:, 0:256], wog[:, 128 * mt:128 * mt + 128], ao[:, :],
                                start=True, stop=True),
                                reads=["wog", "xstat"], writes=[bk_])
                            if g == 0:
                                p.emit("v", lambda e, mt=mt, qc=qc: e.scalar_tensor_tensor(
                                    resid[:, mt, 256 * qc:256 * qc + 256], pb[7][:, 0:256],
                                    bias["bo"][:, mt:mt + 1], resid[:, mt, 256 * qc:256 * qc + 256],
                                    ALU.add, ALU.add),
                                    reads=[bk_, "bias", "resid"], writes=["resid"])
                            else:
                                p.emit("v", lambda e, mt=mt, qc=qc: e.tensor_tensor(
                                    resid[:, mt, 256 * qc:256 * qc + 256], pb[7][:, 0:256],
                                    resid[:, mt, 256 * qc:256 * qc + 256], ALU.add),
                                    reads=[bk_, "resid"], writes=["resid"])
            p.barrier()

        def l1ffn():
            sub = contextlib.ExitStack()
            with sub:
                ln_stats()
                load_weight(wbig[:, :, 0:1024], "wrT", "wbig", 8)
                load_weight(wbig[:, :, 1024:2048], "wkfT", "wbig", 8)
                xnp = arena[:, :, 0:513]
                xr = arena[:, :, 513:1025]
                xk = arena[:, :, 1025:1537]
                krel = arena[:, :, 1537:2049]
                rbuf = fsraw[:, :].rearrange("p (k c) -> p k c", k=KT)
                wvg2 = sub.enter_context(nc.sbuf_tensor(uname("wvg2"), [128, 2, KT, 128], BF16))
                wvfT = dram["wvfT"]

                def fetch_wv(mt, pp):
                    srcv = wvfT[:, :, 128 * mt:128 * mt + 128].transpose([1, 0, 2])
                    for kt in range(KT):
                        p.emit("gd", lambda e, kt=kt, pp=pp: e.dma_start(
                            wvg2[:, pp, kt, :], srcv[:, kt, :]),
                            writes=["wvg2%d" % pp], inc=16)

                NCH = S // PC
                for c in range(NCH):
                    cs = c * PC
                    if c == 0:
                        p.emit("v", lambda e: e.memset(xnp[:, :, 0:1], 0.0), writes=["xnp0"])
                    else:
                        p.emit("v", lambda e: e.tensor_copy(xnp[:, :, 0:1], xnp[:, :, PC:PC + 1]),
                               reads=["xnp"], writes=["xnp0"])
                    bcast_mr(cs, PC)
                    xnorm_apply(lambda kt: xnp[:, kt, 1:1 + PC], "xnp", cs, PC)
                    for kt in range(KT):
                        p.emit("v", lambda e, kt=kt: e.tensor_tensor(
                            t1[:, 0:PC], xnp[:, kt, 1:1 + PC], xnp[:, kt, 0:PC], ALU.subtract),
                            reads=["xnp", "xnp0"], writes=["t1"])
                        p.emit("v", lambda e, kt=kt: e.scalar_tensor_tensor(
                            xr[:, kt, :], t1[:, 0:PC], bias["mur"][:, kt:kt + 1], xnp[:, kt, 0:PC],
                            ALU.mult, ALU.add),
                            reads=["t1", "bias", "xnp", "xnp0"], writes=["xr"])
                        p.emit("v", lambda e, kt=kt: e.scalar_tensor_tensor(
                            xk[:, kt, :], t1[:, 0:PC], bias["muk"][:, kt:kt + 1], xnp[:, kt, 0:PC],
                            ALU.mult, ALU.add),
                            reads=["t1", "bias", "xnp", "xnp0"], writes=["xk"])
                    for mt in range(KT):
                        bk_ = "pb%d" % (mt % 2)
                        for kt in range(KT):
                            p.emit("t", lambda e, mt=mt, kt=kt: e.matmul(
                                pb[mt % 2][:, 0:PC], wbig[:, kt, 128 * mt:128 * mt + 128],
                                xr[:, kt, :], start=(kt == 0), stop=(kt == KT - 1)),
                                reads=["wbig", "xr"], writes=[bk_])
                        p.emit("s", lambda e, mt=mt: e.activation(rbuf[:, mt, :], pb[mt % 2][:, 0:PC],
                                                                  AF.Sigmoid),
                               reads=[bk_], writes=["rbuf"])
                    for mt in range(KT):
                        bk_ = "pb%d" % (2 + mt % 2)
                        for kt in range(KT):
                            p.emit("t", lambda e, mt=mt, kt=kt: e.matmul(
                                pb[2 + mt % 2][:, 0:PC], wbig[:, kt, 1024 + 128 * mt:1024 + 128 * mt + 128],
                                xk[:, kt, :], start=(kt == 0), stop=(kt == KT - 1)),
                                reads=["wbig", "xk"], writes=[bk_])
                        p.emit("v", lambda e, mt=mt: e.tensor_scalar_max(
                            krel[:, mt, :], pb[2 + mt % 2][:, 0:PC], 0.0),
                            reads=[bk_], writes=["krel"])
                        p.emit("v", lambda e, mt=mt: e.tensor_tensor(
                            krel[:, mt, :], krel[:, mt, :], krel[:, mt, :], ALU.mult),
                            reads=["krel"], writes=["krel"])
                    fetch_wv(0, 0)
                    for mt in range(KT):
                        pp2 = mt % 2
                        if mt + 1 < KT:
                            fetch_wv(mt + 1, (mt + 1) % 2)
                        elif c + 1 < NCH:
                            pass
                        bk_ = "pb%d" % (4 + mt % 2)
                        for kt in range(KT):
                            p.emit("t", lambda e, mt=mt, kt=kt, pp2=pp2: e.matmul(
                                pb[4 + mt % 2][:, 0:PC], wvg2[:, pp2, kt, :],
                                krel[:, kt, :], start=(kt == 0), stop=(kt == KT - 1)),
                                reads=["wvg2%d" % pp2, "krel"], writes=[bk_])
                        p.emit("v", lambda e, mt=mt: e.tensor_tensor(
                            t2[:, 0:PC], pb[4 + mt % 2][:, 0:PC], rbuf[:, mt, :], ALU.mult),
                            reads=[bk_, "rbuf"], writes=["t2"])
                        p.emit("v", lambda e, mt=mt, cs=cs: e.scalar_tensor_tensor(
                            resid[:, mt, cs:cs + PC], t2[:, 0:PC], 0.5, resid[:, mt, cs:cs + PC],
                            ALU.mult, ALU.add),
                            reads=["t2", "resid"], writes=["resid"])
            p.barrier()

        def l1attn(stage=99):
            sub = contextlib.ExitStack()
            with sub:
                ln_stats()
                wmini = sub.enter_context(nc.sbuf_tensor(uname("wmini"), [128, 2, 2048], BF16))
                load_weight(wmini[:, :, 0:256], "bmT", "wmini", 2)
                load_weight(wmini[:, :, 256:512], "amat", "wmini", 2)
                load_weight(wmini[:, :, 512:768], "amatT", "wmini", 2)
                load_weight(wmini[:, :, 768:1792], "cmT", "wmini", 2)
                load_weight(wbig[:, :, 0:256], "winT", "wbig", 8)
                wpow = sub.enter_context(nc.sbuf_tensor(uname("wpow"), [128, 2, 1536], BF16))
                cb = sub.enter_context(nc.sbuf_tensor(uname("cbuf"), [128, 2, 128], F32))
                hh = sub.enter_context(nc.sbuf_tensor(uname("hhbuf"), [128, 2, 8], F32))
                hh_bf = sub.enter_context(nc.sbuf_tensor(uname("hhbf"), [128, 2, 8], BF16))

                amat = wmini[:, :, 256:512]
                amatT = wmini[:, :, 512:768]
                bmT = wmini[:, :, 0:256]
                cmT = wmini[:, :, 768:1792]
                uby = fsraw[:, 0:2048].bitcast(BF16).rearrange("p (k c) -> p k c", k=2)
                ubuf = arena[:, 0:2, 0:2048]
                kstg = arena[:, 2:4, 0:512]
                hst = arena[:, 4:6, 0:128]
                hst3 = arena[:, 6:8, 0:1]

                for c in range(S // LNC):
                    cs = c * LNC
                    bcast_mr(cs, LNC)
                    xnorm_apply(lambda kt: xstat[:, kt, :], "xstat", cs, LNC)
                    for mt in range(2):
                        bk_ = "pb%d" % (mt % 2)
                        for kt in range(KT):
                            p.emit("t", lambda e, mt=mt, kt=kt: e.matmul(
                                pb[mt % 2][:, 0:LNC], wbig[:, kt, 128 * mt:128 * mt + 128],
                                xstat[:, kt, :], start=(kt == 0), stop=(kt == KT - 1)),
                                reads=["wbig", "xstat"], writes=[bk_])
                        p.emit("s", lambda e, mt=mt, cs=cs: e.activation(
                            ubuf[:, mt, cs:cs + LNC], pb[mt % 2][:, 0:LNC], AF.Identity,
                            bias=bias["binb"][:, mt:mt + 1]),
                            reads=[bk_, "bias"], writes=["ubuf"])
                if stage < 20:
                    p.barrier()
                    return
                for c in range(4):
                    cs = c * UC
                    for mt in range(2):
                        bk_ = "pb%d" % (2 + mt % 2)
                        for kt in range(2):
                            p.emit("t", lambda e, mt=mt, kt=kt, cs=cs: e.matmul(
                                pb[2 + mt % 2][:, 0:UC], bmT[:, kt, 128 * mt:128 * mt + 128],
                                ubuf[:, kt, cs:cs + UC], start=(kt == 0), stop=(kt == 1)),
                                reads=["wmini", "ubuf"], writes=[bk_])
                        p.emit("s", lambda e, mt=mt, cs=cs: e.activation(
                            uby[:, mt, cs:cs + UC], pb[2 + mt % 2][:, 0:UC], AF.Copy),
                            reads=[bk_], writes=["uby"])

                def square2(src, srcT, skeys, dst, dkey, dstT=None, dkeyT=None):
                    for mt in range(2):
                        bk_ = "pb%d" % (mt % 2)
                        for kt in range(2):
                            p.emit("t", lambda e, mt=mt, kt=kt: e.matmul(
                                pb[mt % 2][:, 0:256], srcT[:, kt, 128 * mt:128 * mt + 128],
                                src[:, kt, :], start=(kt == 0), stop=(kt == 1)),
                                reads=list(skeys), writes=[bk_])
                        p.emit("s", lambda e, mt=mt: e.activation(dst[:, mt, :], pb[mt % 2][:, 0:256], AF.Copy),
                               reads=[bk_], writes=[dkey])
                    if dstT is None:
                        return
                    for mt in range(2):
                        bk_ = "pb%d" % (2 + mt % 2)
                        for kt in range(2):
                            p.emit("t", lambda e, mt=mt, kt=kt: e.matmul(
                                pb[2 + mt % 2][:, 0:256], src[:, kt, 128 * mt:128 * mt + 128],
                                srcT[:, kt, :], start=(kt == 0), stop=(kt == 1)),
                                reads=list(skeys), writes=[bk_])
                        p.emit("s", lambda e, mt=mt: e.activation(dstT[:, mt, :], pb[2 + mt % 2][:, 0:256], AF.Copy),
                               reads=[bk_], writes=[dkeyT])

                if stage < 30:
                    p.barrier()
                    return
                p0 = wpow[:, :, 0:256]
                p0T = wpow[:, :, 256:512]
                p1 = wpow[:, :, 512:768]
                p1T = wpow[:, :, 768:1024]
                b16 = wpow[:, :, 1024:1280]
                b256 = wpow[:, :, 1280:1536]
                square2(amat, amatT, ["wmini"], p0, "p0", p0T, "p0T")
                square2(p0, p0T, ["p0", "p0T"], p1, "p1", p1T, "p1T")
                square2(p1, p1T, ["p1", "p1T"], p0, "p0", p0T, "p0T")
                square2(p0, p0T, ["p0", "p0T"], b16, "b16", p1T, "p1T")
                square2(b16, p1T, ["b16", "p1T"], p0, "p0", p0T, "p0T")
                square2(p0, p0T, ["p0", "p0T"], p1, "p1", p1T, "p1T")
                square2(p1, p1T, ["p1", "p1T"], p0, "p0", p0T, "p0T")
                square2(p0, p0T, ["p0", "p0T"], b256, "b256")

                def scan_local(buf_ap, buf_key, Lc, Cc, m_ap, m_keys, hst_ap, hst_key, f32buf):
                    for mt in range(2):
                        p.emit("s", lambda e, mt=mt: e.activation(
                            hst_ap[:, mt, 0:Cc], buf_ap[:, mt, 0::Lc], AF.Copy),
                            reads=[buf_key], writes=[hst_key])
                    for j in range(1, Lc):
                        for mt in range(2):
                            for kt in range(2):
                                p.emit("t", lambda e, mt=mt, kt=kt: e.matmul(
                                    pb[6][:, 128 * mt:128 * mt + Cc] if Cc == 128 else pb[6][:, 8 * mt:8 * mt + Cc],
                                    m_ap[:, kt, 128 * mt:128 * mt + 128], hst_ap[:, kt, 0:Cc],
                                    start=(kt == 0), stop=(kt == 1)),
                                    reads=list(m_keys) + [hst_key], writes=["pb6"])
                        for mt in range(2):
                            src = pb[6][:, 128 * mt:128 * mt + Cc] if Cc == 128 else pb[6][:, 8 * mt:8 * mt + Cc]
                            p.emit("v", lambda e, mt=mt, j=j, src=src: e.tensor_tensor(
                                buf_ap[:, mt, j::Lc], src, buf_ap[:, mt, j::Lc], ALU.add),
                                reads=["pb6", buf_key], writes=[buf_key])
                        for mt in range(2):
                            p.emit("s", lambda e, mt=mt, j=j: e.activation(
                                hst_ap[:, mt, 0:Cc], buf_ap[:, mt, j::Lc], AF.Copy),
                                reads=[buf_key], writes=[hst_key])

                def scan_expand(buf_ap, buf_key, Lc, Cc, m_ap, m_keys, w_ap, w_key):
                    for j in range(Lc):
                        for mt in range(2):
                            for kt in range(2):
                                p.emit("t", lambda e, mt=mt, kt=kt: e.matmul(
                                    pb[7][:, 128 * mt:128 * mt + Cc] if Cc == 128 else pb[7][:, 8 * mt:8 * mt + Cc],
                                    m_ap[:, kt, 128 * mt:128 * mt + 128], w_ap[:, kt, 0:Cc],
                                    start=(kt == 0), stop=(kt == 1)),
                                    reads=list(m_keys) + [w_key], writes=["pb7"])
                        for mt in range(2):
                            src = pb[7][:, 128 * mt:128 * mt + Cc] if Cc == 128 else pb[7][:, 8 * mt:8 * mt + Cc]
                            p.emit("v", lambda e, mt=mt, j=j, src=src: e.tensor_tensor(
                                buf_ap[:, mt, j::Lc], src, buf_ap[:, mt, j::Lc], ALU.add),
                                reads=["pb7", buf_key], writes=[buf_key])
                        if j < Lc - 1:
                            for mt in range(2):
                                src = pb[7][:, 128 * mt:128 * mt + Cc] if Cc == 128 else pb[7][:, 8 * mt:8 * mt + Cc]
                                p.emit("v", lambda e, mt=mt, src=src: e.tensor_copy(
                                    w_ap[:, mt, 0:Cc], src),
                                    reads=["pb7"], writes=[w_key])

                if stage < 40:
                    p.barrier()
                    return
                scan_local(uby, "uby", 16, 128, amat, ["wmini"], hst, "hst", None)
                if stage < 50:
                    p.barrier()
                    return
                for mt in range(2):
                    p.emit("v", lambda e, mt=mt: e.tensor_copy(cb[:, mt, :], uby[:, mt, 15::16]),
                           reads=["uby"], writes=["cbuf"])
                if stage < 51:
                    p.barrier()
                    return
                scan_local(cb, "cbuf", 16, 8, b16, ["b16"], hst, "hst", None)
                if stage < 52:
                    p.barrier()
                    return
                p.emit("v", lambda e: e.memset(hh[:, :, 0:1], 0.0), writes=["hh"])
                p.emit("v", lambda e: e.tensor_copy(hh[:, :, 1:2], cb[:, :, 15:16]),
                       reads=["cbuf"], writes=["hh"])
                for g2 in range(2, 8):
                    for mt in range(2):
                        p.emit("s", lambda e, mt=mt, g2=g2: e.activation(
                            hst3[:, mt, :], hh[:, mt, g2 - 1:g2], AF.Copy),
                            reads=["hh"], writes=["hst3"])
                    for mt in range(2):
                        for kt in range(2):
                            p.emit("t", lambda e, mt=mt, kt=kt: e.matmul(
                                pb[6][:, 8 * mt:8 * mt + 1], b256[:, kt, 128 * mt:128 * mt + 128],
                                hst3[:, kt, :], start=(kt == 0), stop=(kt == 1)),
                                reads=["b256", "hst3"], writes=["pb6"])
                    for mt in range(2):
                        p.emit("v", lambda e, mt=mt, g2=g2: e.tensor_tensor(
                            hh[:, mt, g2:g2 + 1], pb[6][:, 8 * mt:8 * mt + 1],
                            cb[:, mt, 16 * (g2 - 1) + 15:16 * (g2 - 1) + 16], ALU.add),
                            reads=["pb6", "cbuf"], writes=["hh"])
                if stage < 53:
                    p.barrier()
                    return
                p.emit("s", lambda e: e.activation(hh_bf[:, :, :], hh[:, :, :], AF.Copy),
                       reads=["hh"], writes=["hhbf"])
                scan_expand(cb, "cbuf", 16, 8, b16, ["b16"], hh_bf, "hhbf")
                if stage < 60:
                    p.barrier()
                    return
                p.emit("v", lambda e: e.memset(hst[:, :, 0:1], 0.0), writes=["hst"])
                for mt in range(2):
                    p.emit("s", lambda e, mt=mt: e.activation(hst[:, mt, 1:128], cb[:, mt, 0:127], AF.Copy),
                           reads=["cbuf"], writes=["hst"])
                scan_expand(uby, "uby", 16, 128, amat, ["wmini"], hst, "hst")

                if stage < 70:
                    p.barrier()
                    return
                for c in range(4):
                    cs = c * UC
                    for mt in range(2):
                        p.emit("s", lambda e, mt=mt, cs=cs: e.activation(
                            kstg[:, mt, :], uby[:, mt, cs:cs + UC], AF.Copy),
                            reads=["uby"], writes=["kstg"])
                    for mt in range(KT):
                        bk_ = "pb%d" % (mt % 2)
                        for kt in range(2):
                            p.emit("t", lambda e, mt=mt, kt=kt: e.matmul(
                                pb[mt % 2][:, 0:UC], cmT[:, kt, 128 * mt:128 * mt + 128],
                                kstg[:, kt, :], start=(kt == 0), stop=(kt == 1)),
                                reads=["wmini", "kstg"], writes=[bk_])
                        p.emit("v", lambda e, mt=mt, cs=cs: e.scalar_tensor_tensor(
                            resid[:, mt, cs:cs + UC], pb[mt % 2][:, 0:UC],
                            bias["ddv"][:, mt:mt + 1], resid[:, mt, cs:cs + UC],
                            ALU.add, ALU.add),
                            reads=[bk_, "bias", "resid"], writes=["resid"])
            p.barrier()

        subs = [l0ffn, l0attn, l0ffn, l1ffn,
                lambda: l1attn(stage=l1a_stage), l1ffn]
        for i in range(min(n_sub, 6)):
            subs[i]()

        doutT = dout.transpose([1, 0, 2])
        for kt in range(KT):
            p.emit("d", lambda e, kt=kt: e.dma_start(doutT[:, kt, :], resid[:, kt, :]),
                   reads=["resid"], inc=16)
        p._wait("t", ("d", p.cnt["d"]))
    library_overlay.lower_extended_insts(nc)
    return nc


_CACHE = {}


def kernel(**inputs):
    n_sub = int(inputs.pop("_n_sub", 6))
    l1a_stage = int(inputs.pop("_l1a_stage", 99))
    ck = (n_sub, l1a_stage)
    if ck not in _CACHE:
        _CACHE[ck] = build_program(n_sub, l1a_stage)
    nc = _CACHE[ck]

    def f32(a):
        return np.ascontiguousarray(np.asarray(a), dtype=np.float32)

    g = {k: f32(v) for k, v in inputs.items()}
    shared = {
        "wqT": f32(g["L0_attn_wq"].T).reshape(8, 128, 1024),
        "wkT": f32(g["L0_attn_wk"].T).reshape(8, 128, 1024),
        "wvT": f32(g["L0_attn_wv"].T).reshape(8, 128, 1024),
        "woT": f32(g["L0_attn_wo"].T).reshape(8, 128, 1024),
        "bq": f32(g["L0_attn_bq"].reshape(8, 128).T),
        "bk": f32(g["L0_attn_bk"].reshape(8, 128).T),
        "bo": f32(g["L0_attn_bo"].reshape(8, 128).T),
        "bvrow": f32(g["L0_attn_bv"].reshape(1, 1024)),
        "cw1": f32(g["L0_ffn_cn_w1"]).reshape(8, 128, 1024),
        "cw2": f32(g["L0_ffn_cn_w2"]).reshape(8, 128, 512),
        "cw3": f32(g["L0_ffn_cn_w3"]).reshape(4, 128, 256),
        "cw4": f32(g["L0_ffn_cn_w4"].reshape(2, 128).T),
        "gw1": f32(g["L0_ffn_gc_w1"]).reshape(8, 128, 512),
        "gw2": f32(g["L0_ffn_gc_w2"]).reshape(4, 128, 256),
        "gw3": f32(g["L0_ffn_gc_w3"].reshape(2, 128).T),
        "cb1": f32(g["L0_ffn_cn_b1"].reshape(8, 128).T),
        "cb2": f32(g["L0_ffn_cn_b2"].reshape(4, 128).T),
        "cb3": f32(g["L0_ffn_cn_b3"].reshape(2, 128).T),
        "cb4": f32(g["L0_ffn_cn_b4"].reshape(1, 1)),
        "gb1": f32(g["L0_ffn_gc_b1"].reshape(4, 128).T),
        "gb2": f32(g["L0_ffn_gc_b2"].reshape(2, 128).T),
        "gb3": f32(g["L0_ffn_gc_b3"].reshape(1, 1)),
        "arr": f32(g["L0_ffn_adaptive_range"].reshape(1, 1)),
        "wrT": f32(g["L1_ffn_W_r"].T).reshape(8, 128, 1024),
        "wkfT": f32(g["L1_ffn_W_k"].T).reshape(8, 128, 1024),
        "wvfT": f32(g["L1_ffn_W_v"].T).reshape(8, 128, 1024),
        "mur": f32(g["L1_ffn_mu_r"].reshape(8, 128).T),
        "muk": f32(g["L1_ffn_mu_k"].reshape(8, 128).T),
        "winT": f32(g["L1_attn_W_in"].T).reshape(8, 128, 256),
        "bmT": f32(g["L1_attn_Bm"].T).reshape(2, 128, 256),
        "amat": f32(g["L1_attn_A"]).reshape(2, 128, 256),
        "amatT": f32(g["L1_attn_A"].T).reshape(2, 128, 256),
        "cmT": f32(g["L1_attn_C"].T).reshape(2, 128, 1024),
        "binb": f32(g["L1_attn_b_in"].reshape(2, 128).T),
        "ddv": f32(g["L1_attn_D"].reshape(8, 128).T),
    }
    x = g["L0_attn_x"]
    in_maps = []
    for b in range(8):
        m = dict(shared)
        m["xT"] = f32(x[b].T).reshape(8, 128, 2048)
        in_maps.append(m)
    res = run_bass_kernel_spmd(nc, in_maps, core_ids=list(range(8)))
    outs = []
    for r in res.results:
        o = np.asarray(r["out"], dtype=np.float32).reshape(1024, 2048)
        outs.append(o.T)
    return np.stack(outs, axis=0)


if __name__ == "__main__":
    import reference as _ref
    inp = {k: np.asarray(v) for k, v in _ref.setup_inputs(0).items()}
    out = kernel(**inp)
    print(out.shape, out.dtype)


# version 11
# speedup vs baseline: 1.8978x; 1.0006x over previous
import contextlib
import numpy as np

import concourse.bass as bass
import concourse.mybir as mybir
from concourse.bass_utils import run_bass_kernel_spmd
from concourse import library_config
from concourse import library_overlay

F32 = mybir.dt.float32
BF16 = mybir.dt.bfloat16
AF = mybir.ActivationFunctionType
ALU = mybir.AluOpType

S = 2048
D = 1024
KT = 8
H = 16
HD = 64
NG = 8
LNC = 256
UC = 512
PC = 512


class Prog:

    def __init__(self, nc):
        self.nc = nc
        self.eng = {"t": nc.tensor, "v": nc.vector, "s": nc.scalar,
                    "g": nc.gpsimd, "d": nc.sync, "gd": nc.gpsimd}
        self.cnt = {k: 0 for k in self.eng}
        self.sem = {}
        self.waited = {k: {} for k in self.eng}
        self.res = {}
        self._es = contextlib.ExitStack()

    def init_sems(self):
        for k in self.eng:
            self.sem[k] = self._es.enter_context(self.nc.semaphore("sem_" + k))

    def _wait(self, stream, tok):
        src, val = tok
        if src == "d":
            val = self.cnt[src]
        if self.waited[stream].get(src, -1) >= val:
            return
        self.eng[stream].wait_ge(self.sem[src], val)
        self.waited[stream][src] = val

    def emit(self, stream, fn, reads=(), writes=(), inc=1):
        toks = []
        for r in reads:
            st = self.res.get(r)
            if st and st["w"] is not None:
                toks.append(st["w"])
        for w in writes:
            st = self.res.get(w)
            if st:
                if st["w"] is not None:
                    if not (stream == "t" and st["w"][0] == "t" and w.startswith("pb")):
                        toks.append(st["w"])
                toks.extend(st["r"])
        for tok in sorted(set(toks)):
            if tok[0] == stream and tok[1] <= self.cnt[stream]:
                self._wait(stream, tok)
            else:
                self._wait(stream, tok)
        instr = fn(self.eng[stream])
        instr.then_inc(self.sem[stream], inc)
        self.cnt[stream] += inc
        tok = (stream, self.cnt[stream])
        for r in reads:
            st = self.res.setdefault(r, {"w": None, "r": []})
            st["r"].append(tok)
        for w in writes:
            self.res[w] = {"w": tok, "r": []}
        return tok

    def barrier(self):
        snap = dict(self.cnt)
        for stream in ("t", "v", "s", "g", "d"):
            for src, val in snap.items():
                if src != stream and val > 0:
                    self._wait(stream, (src, val))
        self.res = {}


def ceil_div(a, b):
    return -(-a // b)


def jlist(qc):
    out = []
    for j in range(16):
        s = 128 * j - 256 * qc
        if s <= -640 or s >= 768:
            continue
        mask = None
        if s == -512:
            mask = "ml0"
        elif s == -384:
            mask = "ml1"
        elif s == 512:
            mask = "mu0"
        elif s == 640:
            mask = "mu1"
        out.append((j, mask))
    return out


def build_program(n_sub=6, l1a_stage=99):
    nc = bass.Bass("TRN2", target_bir_lowering=False)
    p = Prog(nc)

    dram = {}

    def din(name, shape, dtype=F32):
        dram[name] = nc.dram_tensor(name, list(shape), dtype, kind="ExternalInput").ap()
        return dram[name]

    din("xT", (KT, 128, S))
    dout = nc.dram_tensor("out", [KT, 128, S], F32, kind="ExternalOutput").ap()

    for n in ("wqT", "wkT", "wvT", "woT"):
        din(n, (KT, 128, D))
    for n in ("bq", "bk", "bo"):
        din(n, (128, 8))
    din("bvrow", (1, D))
    din("cw1", (8, 128, 1024)); din("cw2", (8, 128, 512))
    din("cw3", (4, 128, 256)); din("cw4", (128, 2))
    din("gw1", (8, 128, 512)); din("gw2", (4, 128, 256)); din("gw3", (128, 2))
    din("cb1", (128, 8)); din("cb2", (128, 4)); din("cb3", (128, 2)); din("cb4", (1, 1))
    din("gb1", (128, 4)); din("gb2", (128, 2)); din("gb3", (1, 1)); din("arr", (1, 1))
    din("wrT", (8, 128, 1024)); din("wkfT", (8, 128, 1024)); din("wvfT", (8, 128, 1024))
    din("mur", (128, 8)); din("muk", (128, 8))
    din("winT", (8, 128, 256)); din("bmT", (2, 128, 256))
    din("amat", (2, 128, 256)); din("amatT", (2, 128, 256)); din("cmT", (2, 128, 1024))
    din("binb", (128, 2)); din("ddv", (128, 8))

    es = contextlib.ExitStack()

    def sb(name, shape, dtype):
        return es.enter_context(nc.sbuf_tensor(name, list(shape), dtype))

    _uid = [0]

    def uname(name):
        _uid[0] += 1
        return "%s_u%d" % (name, _uid[0])

    with es:
        p.init_sems()
        nc.gpsimd.load_library(library_config.proxy)

        resid = sb("resid", (128, KT, S), F32)
        arena = sb("arena", (128, KT, 2064), BF16)
        xstat = sb("xstat", (128, KT, LNC), BF16)
        xsqst = sb("xsqst", (128, 2, LNC), BF16)
        fsraw = sb("fsraw", (128, 4096), F32)
        mrow = sb("mrow", (1, S), F32)[:, :]
        rsrow = sb("rsrow", (1, S), F32)[:, :]
        srow = sb("srow", (1, 512), F32)[:, :]
        srow2 = sb("srow2", (1, 512), F32)[:, :]
        mb = sb("mb", (128, 512), F32)
        rb = sb("rb", (128, 512), F32)
        t1 = sb("t1", (128, 512), F32)
        t2 = sb("t2", (128, 512), F32)
        wbig = sb("wbig", (128, KT, 2048), BF16)
        onesb = sb("onesb", (128, 1), BF16)
        onesm = sb("onesm", (1, 128), BF16)
        negones = sb("negones", (128, 1), BF16)
        negpad = sb("negpad", (1, 65), BF16)
        ident = sb("ident", (128, 128), BF16)
        masks = {n: sb(n, (128, 256), BF16) for n in ("ml0", "ml1", "mu0", "mu1")}
        bias = {}
        for n, shp in (("bq", (128, 8)), ("bk", (128, 8)), ("bo", (128, 8)),
                       ("cb1", (128, 8)), ("cb2", (128, 4)), ("cb3", (128, 2)),
                       ("cb4", (1, 1)), ("gb1", (128, 4)), ("gb2", (128, 2)),
                       ("gb3", (1, 1)), ("arr", (1, 1)), ("mur", (128, 8)),
                       ("muk", (128, 8)), ("binb", (128, 2)), ("ddv", (128, 8))):
            bias[n] = sb("b_" + n, shp, F32)
        bvrow = sb("bvrow_s", (1, D), BF16)
        scal = sb("scal", (1, 16), F32)

        pb = [es.enter_context(nc.psum_tensor("pb%d" % i, [128, 512], F32))
              for i in range(8)]

        xTsrc = dram["xT"].transpose([1, 0, 2])
        for kt in range(KT):
            p.emit("d", lambda e, kt=kt: e.dma_start(resid[:, kt, :], xTsrc[:, kt, :]),
                   writes=["resid"], inc=16)
        for n in bias:
            nm = n
            p.emit("d", lambda e, nm=nm: e.dma_start(bias[nm][:, :], dram[nm][:, :]),
                   writes=["bias"], inc=16)
        p.emit("gd", lambda e: e.dma_start(bvrow[:, :], dram["bvrow"][:, :]),
               writes=["bvrow"], inc=16)

        p.emit("v", lambda e: e.memset(onesb[:, :], 1.0), writes=["onesb"])
        p.emit("v", lambda e: e.memset(onesm[:, :], 1.0), writes=["onesm"])
        p.emit("v", lambda e: e.memset(negones[:, :], -1.0), writes=["negones"])
        p.emit("v", lambda e: e.memset(negpad[:, 0:64], -512.0), writes=["negpad"])
        p.emit("v", lambda e: e.memset(negpad[:, 64:65], 512.0), writes=["negpad"])
        p.emit("v", lambda e: e.memset(t1[:, 0:128], 1.0), writes=["t1"])
        p.emit("g", lambda e: e.affine_select(t1[:, 0:128], t1[:, 0:128], pattern=[[-1, 128]],
                                              base=0, channel_multiplier=1,
                                              compare_op=ALU.is_equal, fill=0.0),
               reads=["t1"], writes=["t1"])
        p.emit("s", lambda e: e.activation(ident[:, :], t1[:, 0:128], AF.Copy),
               reads=["t1"], writes=["ident"])
        p.emit("v", lambda e: e.memset(scal[:, 10:11], 1e-5), writes=["eps"])
        mspec = {"ml0": (0, 1, -1), "ml1": (128, 1, -1), "mu0": (0, -1, 1), "mu1": (-128, -1, 1)}
        for n, (base, cm, fc) in mspec.items():
            p.emit("v", lambda e: e.memset(t1[:, 0:256], 1.0), writes=["t1"])
            p.emit("g", lambda e, base=base, cm=cm, fc=fc: e.affine_select(
                t1[:, 0:256], t1[:, 0:256], pattern=[[fc, 256]], base=base,
                channel_multiplier=cm, compare_op=ALU.is_ge, fill=0.0),
                reads=["t1"], writes=["t1"])
            p.emit("s", lambda e, n=n: e.activation(masks[n][:, :], t1[:, 0:256], AF.Copy),
                   reads=["t1"], writes=[n])

        def load_weight(dst_ap, src_name, key, kt_tiles):
            src = dram[src_name].transpose([1, 0, 2])
            for kt in range(kt_tiles):
                p.emit("gd", lambda e, kt=kt: e.dma_start(dst_ap[:, kt, :], src[:, kt, :]),
                       writes=[key], inc=16)

        def ln_stats():
            for c in range(S // LNC):
                cs = c * LNC
                p.emit("s", lambda e, cs=cs: e.activation(xstat[:, :, :], resid[:, :, cs:cs + LNC], AF.Copy),
                       reads=["resid"], writes=["xstat"])
                for kt in range(KT):
                    p.emit("t", lambda e, kt=kt: e.matmul(pb[6][0:1, 0:LNC], onesb[:, :], xstat[:, kt, :],
                                                          start=(kt == 0), stop=(kt == KT - 1)),
                           reads=["onesb", "xstat"], writes=["pb6"])
                for kt in range(KT):
                    pp = kt % 2
                    p.emit("s", lambda e, kt=kt, pp=pp, cs=cs: e.activation(
                        xsqst[:, pp, :], resid[:, kt, cs:cs + LNC], AF.Square),
                        reads=["resid"], writes=["xsq%d" % pp])
                    p.emit("t", lambda e, kt=kt, pp=pp: e.matmul(pb[7][0:1, 0:LNC], onesb[:, :], xsqst[:, pp, :],
                                                                 start=(kt == 0), stop=(kt == KT - 1)),
                           reads=["onesb", "xsq%d" % pp], writes=["pb7"])
                p.emit("v", lambda e, cs=cs: e.tensor_scalar_mul(mrow[:, cs:cs + LNC], pb[6][0:1, 0:LNC], 1.0 / D),
                       reads=["pb6"], writes=["mrow"])
                p.emit("v", lambda e: e.tensor_scalar_mul(srow[:, 0:LNC], pb[7][0:1, 0:LNC], 1.0 / D),
                       reads=["pb7"], writes=["srow"])
                p.emit("v", lambda e, cs=cs: e.tensor_tensor(srow2[:, 0:LNC], mrow[:, cs:cs + LNC],
                                                             mrow[:, cs:cs + LNC], ALU.mult),
                       reads=["mrow"], writes=["srow2"])
                p.emit("v", lambda e: e.tensor_tensor(srow[:, 0:LNC], srow[:, 0:LNC], srow2[:, 0:LNC], ALU.subtract),
                       reads=["srow", "srow2"], writes=["srow"])
                p.emit("s", lambda e: e.activation(srow2[:, 0:LNC], srow[:, 0:LNC], AF.Ln, bias=scal[:, 10:11]),
                       reads=["srow", "eps"], writes=["srow2"])
                p.emit("s", lambda e, cs=cs: e.activation(rsrow[:, cs:cs + LNC], srow2[:, 0:LNC], AF.Exp, scale=-0.5),
                       reads=["srow2"], writes=["rsrow"])

        def bcast_mr(cs, n):
            p.emit("g", lambda e: e.partition_broadcast(mb[:, 0:n], mrow[:, cs:cs + n]),
                   reads=["mrow"], writes=["mb"])
            p.emit("g", lambda e: e.partition_broadcast(rb[:, 0:n], rsrow[:, cs:cs + n]),
                   reads=["rsrow"], writes=["rb"])

        def xnorm_apply(dst_ap_fn, dst_key, cs, n):
            for kt in range(KT):
                p.emit("v", lambda e, kt=kt: e.tensor_tensor(t1[:, 0:n], resid[:, kt, cs:cs + n],
                                                             mb[:, 0:n], ALU.subtract),
                       reads=["resid", "mb"], writes=["t1"])
                p.emit("v", lambda e, kt=kt: e.tensor_tensor(dst_ap_fn(kt), t1[:, 0:n], rb[:, 0:n], ALU.mult),
                       reads=["t1", "rb"], writes=[dst_key])

        def l0ffn():
            sub = contextlib.ExitStack()
            with sub:
                ln_stats()
                load_weight(wbig[:, :, 0:1024], "cw1", "wbig", 8)
                load_weight(wbig[:, :, 1024:1536], "cw2", "wbig", 8)
                load_weight(wbig[:, :, 1536:2048], "gw1", "wbig", 8)
                wsm = sub.enter_context(nc.sbuf_tensor(uname("wsm"), [128, 4, 512], BF16))
                w4s = sub.enter_context(nc.sbuf_tensor(uname("w4s"), [128, 4], BF16))
                load_weight(wsm[:, :, 0:256], "cw3", "wsm", 4)
                load_weight(wsm[:, 0:4, 256:512], "gw2", "wsm", 4)
                p.emit("gd", lambda e: e.dma_start(w4s[:, 0:2], dram["cw4"][:, :]),
                       writes=["w4s"], inc=16)
                p.emit("gd", lambda e: e.dma_start(w4s[:, 2:4], dram["gw3"][:, :]),
                       writes=["w4s"], inc=16)
                gxa = sub.enter_context(nc.sbuf_tensor(uname("gxa"), [128, KT, 1], F32))
                gxr = sub.enter_context(nc.sbuf_tensor(uname("gxr"), [128, KT, 1], F32))
                gxbf = sub.enter_context(nc.sbuf_tensor(uname("gxbf"), [128, KT, 1], BF16))
                gh1 = sub.enter_context(nc.sbuf_tensor(uname("gh1"), [128, 4, 1], BF16))
                gh2 = sub.enter_context(nc.sbuf_tensor(uname("gh2"), [128, 2, 1], BF16))

                p.emit("v", lambda e: e.memset(gxa[:, :, :], 0.0), writes=["gxa"])
                for c in range(S // LNC):
                    cs = c * LNC
                    bcast_mr(cs, LNC)
                    xnorm_apply(lambda kt: xstat[:, kt, :], "xstat", cs, LNC)
                    p.emit("v", lambda e: e.reduce_sum(gxr[:, :, :], xstat[:, :, :], mybir.AxisListType.X),
                           reads=["xstat"], writes=["gxr"])
                    p.emit("v", lambda e: e.tensor_tensor(gxa[:, :, :], gxa[:, :, :], gxr[:, :, :], ALU.add),
                           reads=["gxa", "gxr"], writes=["gxa"])
                p.emit("s", lambda e: e.activation(gxbf[:, :, :], gxa[:, :, :], AF.Copy),
                       reads=["gxa"], writes=["gxbf"])
                for mt in range(4):
                    for kt in range(KT):
                        p.emit("t", lambda e, mt=mt, kt=kt: e.matmul(
                            pb[2][:, mt:mt + 1], wbig[:, kt, 1536 + 128 * mt:1536 + 128 * mt + 128],
                            gxbf[:, kt, :], start=(kt == 0), stop=(kt == KT - 1)),
                            reads=["wbig", "gxbf"], writes=["pb2"])
                for mt in range(4):
                    p.emit("s", lambda e, mt=mt: e.activation(gh1[:, mt, :], pb[2][:, mt:mt + 1], AF.Gelu,
                                                              bias=bias["gb1"][:, mt:mt + 1], scale=1.0 / S),
                           reads=["pb2", "bias"], writes=["gh1"])
                for mt in range(2):
                    for kt in range(4):
                        p.emit("t", lambda e, mt=mt, kt=kt: e.matmul(
                            pb[3][:, mt:mt + 1], wsm[:, kt, 256 + 128 * mt:256 + 128 * mt + 128],
                            gh1[:, kt, :], start=(kt == 0), stop=(kt == 3)),
                            reads=["wsm", "gh1"], writes=["pb3"])
                for mt in range(2):
                    p.emit("s", lambda e, mt=mt: e.activation(gh2[:, mt, :], pb[3][:, mt:mt + 1], AF.Gelu,
                                                              bias=bias["gb2"][:, mt:mt + 1]),
                           reads=["pb3", "bias"], writes=["gh2"])
                for kt in range(2):
                    p.emit("t", lambda e, kt=kt: e.matmul(pb[4][0:1, 0:1], w4s[:, 2 + kt:3 + kt],
                                                          gh2[:, kt, :], start=(kt == 0), stop=(kt == 1)),
                           reads=["w4s", "gh2"], writes=["pb4"])
                p.emit("v", lambda e: e.tensor_scalar_mul(scal[:, 1:2], bias["gb3"][:, :], -1.0),
                       reads=["bias"], writes=["scal"])
                p.emit("s", lambda e: e.activation(scal[:, 2:3], pb[4][0:1, 0:1], AF.Exp,
                                                   scale=-1.0, bias=scal[:, 1:2]),
                       reads=["pb4", "scal"], writes=["scal2"])
                p.emit("v", lambda e: e.tensor_scalar_add(scal[:, 3:4], scal[:, 2:3], 1.0),
                       reads=["scal2"], writes=["scal3"])
                p.emit("v", lambda e: e.reciprocal(scal[:, 0:1], scal[:, 3:4]),
                       reads=["scal3"], writes=["scal0"])
                p.emit("v", lambda e: e.tensor_tensor(scal[:, 4:5], bias["arr"][:, :], scal[:, 0:1], ALU.mult),
                       reads=["bias", "scal0"], writes=["scal4"])
                p.emit("v", lambda e: e.tensor_scalar_mul(scal[:, 5:6], scal[:, 4:5], 1.4),
                       reads=["scal4"], writes=["scal5"])
                p.emit("v", lambda e: e.tensor_scalar(scal[:, 6:7], bias["arr"][:, :], -1.0, 0.1, ALU.mult, ALU.add),
                       reads=["bias"], writes=["scal6"])
                p.emit("v", lambda e: e.tensor_tensor(scal[:, 7:8], scal[:, 5:6], scal[:, 6:7], ALU.add),
                       reads=["scal5", "scal6"], writes=["c0"])
                p.emit("v", lambda e: e.tensor_scalar_mul(scal[:, 8:9], bias["arr"][:, :], 0.6),
                       reads=["bias"], writes=["c1"])
                p.emit("v", lambda e: e.tensor_scalar_mul(scal[:, 9:10], bias["cb4"][:, :], -1.0),
                       reads=["bias"], writes=["scal9"])

                op = fsraw[:, :].rearrange("p (k c) -> p k c", k=KT)
                obf = arena[:, :, 0:PC]
                h1 = arena[:, :, PC:2 * PC]
                h2 = arena[:, 0:4, 2 * PC:3 * PC]
                h3 = arena[:, 0:2, 3 * PC:4 * PC]
                pairs = [(0, 4, 1.0), (1, 5, 1.0), (2, 6, -1.0), (3, 7, -1.0)]
                for c in range(S // PC):
                    cs = c * PC
                    bcast_mr(cs, PC)
                    xnorm_apply(lambda kt: op[:, kt, :], "op", cs, PC)
                    p.emit("s", lambda e: e.activation(obf, op[:, :, :], AF.Copy),
                           reads=["op"], writes=["obf"])
                    for r in range(3):
                        for mt in range(KT):
                            bk = "pb%d" % (2 + mt % 2)
                            for kt in range(KT):
                                p.emit("t", lambda e, mt=mt, kt=kt: e.matmul(
                                    pb[2 + mt % 2][:, 0:PC], wbig[:, kt, 128 * mt:128 * mt + 128],
                                    obf[:, kt, :], start=(kt == 0), stop=(kt == KT - 1)),
                                    reads=["wbig", "obf"], writes=[bk])
                            p.emit("s", lambda e, mt=mt: e.activation(
                                h1[:, mt, :], pb[2 + mt % 2][:, 0:PC], AF.Gelu,
                                bias=bias["cb1"][:, mt:mt + 1]),
                                reads=[bk, "bias"], writes=["h1"])
                        for mt in range(4):
                            bk = "pb%d" % (4 + mt % 2)
                            for kt in range(KT):
                                p.emit("t", lambda e, mt=mt, kt=kt: e.matmul(
                                    pb[4 + mt % 2][:, 0:PC], wbig[:, kt, 1024 + 128 * mt:1024 + 128 * mt + 128],
                                    h1[:, kt, :], start=(kt == 0), stop=(kt == KT - 1)),
                                    reads=["wbig", "h1"], writes=[bk])
                            p.emit("s", lambda e, mt=mt: e.activation(
                                h2[:, mt, :], pb[4 + mt % 2][:, 0:PC], AF.Gelu,
                                bias=bias["cb2"][:, mt:mt + 1]),
                                reads=[bk, "bias"], writes=["h2"])
                        for mt in range(2):
                            bk = "pb%d" % (6 + mt % 2)
                            for kt in range(4):
                                p.emit("t", lambda e, mt=mt, kt=kt: e.matmul(
                                    pb[6 + mt % 2][:, 0:PC], wsm[:, kt, 128 * mt:128 * mt + 128],
                                    h2[:, kt, :], start=(kt == 0), stop=(kt == 3)),
                                    reads=["wsm", "h2"], writes=[bk])
                            p.emit("s", lambda e, mt=mt: e.activation(
                                h3[:, mt, :], pb[6 + mt % 2][:, 0:PC], AF.Gelu,
                                bias=bias["cb3"][:, mt:mt + 1]),
                                reads=[bk, "bias"], writes=["h3"])
                        for kt in range(2):
                            p.emit("t", lambda e, kt=kt: e.matmul(
                                pb[4][0:1, 0:PC], w4s[:, kt:kt + 1], h3[:, kt, :],
                                start=(kt == 0), stop=(kt == 1)),
                                reads=["w4s", "h3"], writes=["pb4"])
                        p.emit("s", lambda e: e.activation(srow[:, 0:PC], pb[4][0:1, 0:PC], AF.Exp,
                                                           scale=-1.0, bias=scal[:, 9:10]),
                               reads=["pb4", "scal9"], writes=["srow"])
                        p.emit("v", lambda e: e.tensor_scalar_add(srow[:, 0:PC], srow[:, 0:PC], 1.0),
                               reads=["srow"], writes=["srow"])
                        p.emit("v", lambda e: e.reciprocal(srow2[:, 0:PC], srow[:, 0:PC]),
                               reads=["srow"], writes=["srow2"])
                        p.emit("v", lambda e: e.tensor_scalar(srow[:, 0:PC], srow2[:, 0:PC],
                                                              scal[:, 8:9], scal[:, 7:8], ALU.mult, ALU.add),
                               reads=["srow2", "c0", "c1"], writes=["srow"])
                        p.emit("g", lambda e: e.partition_broadcast(mb[:, 0:PC], srow[:, 0:PC]),
                               reads=["srow"], writes=["mb"])
                        for (a, b, sa) in pairs:
                            p.emit("v", lambda e, b=b: e.tensor_tensor(t1[:, 0:PC], mb[:, 0:PC], op[:, b, :], ALU.mult),
                                   reads=["mb", "op"], writes=["t1"])
                            p.emit("v", lambda e, a=a: e.tensor_tensor(t2[:, 0:PC], mb[:, 0:PC], op[:, a, :], ALU.mult),
                                   reads=["mb", "op"], writes=["t2"])
                            p.emit("v", lambda e, a=a, sa=sa: e.tensor_tensor(
                                op[:, a, :], op[:, a, :], t1[:, 0:PC], ALU.add if sa > 0 else ALU.subtract),
                                reads=["op", "t1"], writes=["op"])
                            p.emit("v", lambda e, b=b, sa=sa: e.tensor_tensor(
                                op[:, b, :], op[:, b, :], t2[:, 0:PC], ALU.subtract if sa > 0 else ALU.add),
                                reads=["op", "t2"], writes=["op"])
                        if r < 2:
                            p.emit("s", lambda e: e.activation(obf, op[:, :, :], AF.Copy),
                                   reads=["op"], writes=["obf"])
                    for kt in range(KT):
                        p.emit("v", lambda e, kt=kt, cs=cs: e.scalar_tensor_tensor(
                            resid[:, kt, cs:cs + PC], op[:, kt, :], 0.5, resid[:, kt, cs:cs + PC],
                            ALU.mult, ALU.add),
                            reads=["op", "resid"], writes=["resid"])
            p.barrier()

        def l0attn():
            sub = contextlib.ExitStack()
            with sub:
                ln_stats()
                load_weight(wbig[:, :, 0:1024], "wqT", "wbig", 8)
                load_weight(wbig[:, :, 1024:2048], "wkT", "wbig", 8)
                xnorm = arena[:, :, 0:2048]
                for c in range(S // LNC):
                    cs = c * LNC
                    bcast_mr(cs, LNC)
                    xnorm_apply(lambda kt, cs=cs: xnorm[:, kt, cs:cs + LNC], "xnorm", cs, LNC)
                qb = sub.enter_context(nc.sbuf_tensor(uname("qb"), [128, S], BF16))
                kb = sub.enter_context(nc.sbuf_tensor(uname("kb"), [128, S], BF16))
                vaug = sub.enter_context(nc.sbuf_tensor(uname("vaug"), [128, 16, 2, 65], BF16))
                probs = sub.enter_context(nc.sbuf_tensor(uname("probs"), [128, 10, 256], BF16))
                ao = xstat[:, 0, :]
                epadb = sub.enter_context(nc.sbuf_tensor(uname("epadb"), [1, 2, 1024], BF16))
                wvg = sub.enter_context(nc.sbuf_tensor(uname("wvg"), [128, KT, 128], BF16))
                wog = sub.enter_context(nc.sbuf_tensor(uname("wog"), [128, 1024], BF16))

                wvT = dram["wvT"]
                woT = dram["woT"]

                def fetch_group(g):
                    srcv = wvT[:, :, 128 * g:128 * g + 128].transpose([1, 0, 2])
                    for kt in range(KT):
                        p.emit("gd", lambda e, kt=kt: e.dma_start(wvg[:, kt, :], srcv[:, kt, :]),
                               writes=["wvg"], inc=16)
                    p.emit("gd", lambda e, g=g: e.dma_start(wog[:, :], woT[g, :, :]),
                           writes=["wog"], inc=16)

                for g in range(NG):
                    fetch_group(g)
                    for dst, wofs, bn, dk in ((qb, 0, "bq", "qb"), (kb, 1024, "bk", "kb")):
                        for nb in range(4):
                            bk_ = "pb%d" % (6 + nb % 2)
                            for kt in range(KT):
                                p.emit("t", lambda e, nb=nb, kt=kt, wofs=wofs, g=g: e.matmul(
                                    pb[6 + nb % 2][:, 0:512],
                                    wbig[:, kt, wofs + 128 * g:wofs + 128 * g + 128],
                                    xnorm[:, kt, 512 * nb:512 * nb + 512],
                                    start=(kt == 0), stop=(kt == KT - 1)),
                                    reads=["wbig", "xnorm"], writes=[bk_])
                            p.emit("s", lambda e, nb=nb, dst=dst, bn=bn, g=g: e.activation(
                                dst[:, 512 * nb:512 * nb + 512], pb[6 + nb % 2][:, 0:512],
                                AF.Identity, bias=bias[bn][:, g:g + 1]),
                                reads=[bk_, "bias"], writes=[dk])
                    for hl in range(2):
                        for half, q0 in ((0, 0), (1, 1536)):
                            p.emit("t", lambda e, hl=hl, q0=q0: e.matmul(
                                pb[6][0:1, 0:512], negones[64 * hl:64 * hl + 64, :],
                                qb[64 * hl:64 * hl + 64, q0:q0 + 512], start=True, stop=True),
                                reads=["qb", "negones"], writes=["pb6"])
                            p.emit("s", lambda e, hl=hl, half=half: e.activation(
                                epadb[:, hl, 512 * half:512 * half + 512], pb[6][0:1, 0:512], AF.Exp),
                                reads=["pb6"], writes=["epadb"])
                    p.emit("v", lambda e: e.memset(vaug[:, :, :, 64:65], 1.0), writes=["vaug"])
                    for tt in range(16):
                        bk_ = "pb%d" % (6 + tt % 2)
                        for kt in range(KT):
                            p.emit("t", lambda e, tt=tt, kt=kt: e.matmul(
                                pb[6 + tt % 2][:, 0:128], xnorm[:, kt, 128 * tt:128 * tt + 128],
                                wvg[:, kt, :], start=(kt == 0), stop=False),
                                reads=["wvg", "xnorm"], writes=[bk_])
                        p.emit("t", lambda e, tt=tt, g=g: e.matmul(
                            pb[6 + tt % 2][:, 0:128], onesm[:, :],
                            bvrow[:, 128 * g:128 * g + 128], start=False, stop=True),
                            reads=["onesm", "bvrow"], writes=[bk_])
                        p.emit("s", lambda e, tt=tt: e.activation(
                            vaug[:, tt, :, 0:64],
                            pb[6 + tt % 2][:, 0:128].rearrange("p (h d) -> p h d", h=2), AF.Copy),
                            reads=[bk_], writes=["vaug"])
                    for qc in range(8):
                        jl = jlist(qc)
                        pad = qc in (0, 1, 6, 7)
                        if qc < 2:
                            pofs = 256 * qc
                        else:
                            pofs = 512 + 256 * (qc - 6)
                        for hl in range(2):
                            for idx, (j, mk) in enumerate(jl):
                                bk_ = "pb%d" % (idx // 2)
                                p.emit("t", lambda e, j=j, hl=hl, qc=qc, idx=idx: e.matmul(
                                    pb[idx // 2][:, 256 * (idx % 2):256 * (idx % 2) + 256],
                                    kb[64 * hl:64 * hl + 64, 128 * j:128 * j + 128],
                                    qb[64 * hl:64 * hl + 64, 256 * qc:256 * qc + 256],
                                    start=True, stop=True),
                                    reads=["kb", "qb"], writes=[bk_ + "_%d" % (idx % 2)])
                            for idx, (j, mk) in enumerate(jl):
                                p.emit("s", lambda e, idx=idx: e.activation(
                                    probs[:, idx, :], pb[idx // 2][:, 256 * (idx % 2):256 * (idx % 2) + 256],
                                    AF.Exp),
                                    reads=["pb%d_%d" % (idx // 2, idx % 2)], writes=["probs%d" % idx])
                            for idx, (j, mk) in enumerate(jl):
                                if mk is not None:
                                    p.emit("v", lambda e, idx=idx, mk=mk: e.tensor_tensor(
                                        probs[:, idx, :], probs[:, idx, :], masks[mk][:, :], ALU.mult),
                                        reads=["probs%d" % idx, mk], writes=["probs%d" % idx])
                            for idx, (j, mk) in enumerate(jl):
                                p.emit("t", lambda e, j=j, hl=hl, idx=idx: e.matmul(
                                    pb[5][0:65, 0:256], vaug[:, j, hl, :], probs[:, idx, :],
                                    start=(idx == 0), stop=(idx == len(jl) - 1 and not pad)),
                                    reads=["vaug", "probs%d" % idx], writes=["pb5"])
                            if pad:
                                p.emit("t", lambda e, hl=hl, pofs=pofs: e.matmul(
                                    pb[5][0:65, 0:256], negpad[:, :], epadb[:, hl, pofs:pofs + 256],
                                    start=False, stop=True),
                                    reads=["negpad", "epadb"], writes=["pb5"])
                            p.emit("v", lambda e: e.reciprocal(srow2[:, 0:256], pb[5][64:65, 0:256]),
                                   reads=["pb5"], writes=["srow2"])
                            p.emit("g", lambda e: e.partition_broadcast(t2[:, 0:256], srow2[:, 0:256]),
                                   reads=["srow2"], writes=["t2"])
                            p.emit("v", lambda e, hl=hl: e.tensor_tensor(
                                ao[64 * hl:64 * hl + 64, :], pb[5][0:64, 0:256], t2[0:64, 0:256], ALU.mult),
                                reads=["pb5", "t2"], writes=["xstat"])
                        for mt in range(KT):
                            bk_ = "pb7"
                            p.emit("t", lambda e, mt=mt: e.matmul(
                                pb[7][:, 0:256], wog[:, 128 * mt:128 * mt + 128], ao[:, :],
                                start=True, stop=True),
                                reads=["wog", "xstat"], writes=[bk_])
                            if g == 0:
                                p.emit("v", lambda e, mt=mt, qc=qc: e.scalar_tensor_tensor(
                                    resid[:, mt, 256 * qc:256 * qc + 256], pb[7][:, 0:256],
                                    bias["bo"][:, mt:mt + 1], resid[:, mt, 256 * qc:256 * qc + 256],
                                    ALU.add, ALU.add),
                                    reads=[bk_, "bias", "resid"], writes=["resid"])
                            else:
                                p.emit("v", lambda e, mt=mt, qc=qc: e.tensor_tensor(
                                    resid[:, mt, 256 * qc:256 * qc + 256], pb[7][:, 0:256],
                                    resid[:, mt, 256 * qc:256 * qc + 256], ALU.add),
                                    reads=[bk_, "resid"], writes=["resid"])
            p.barrier()

        def l1ffn():
            sub = contextlib.ExitStack()
            with sub:
                ln_stats()
                load_weight(wbig[:, :, 0:1024], "wrT", "wbig", 8)
                load_weight(wbig[:, :, 1024:2048], "wkfT", "wbig", 8)
                xnp = arena[:, :, 0:513]
                xr = arena[:, :, 513:1025]
                xk = arena[:, :, 1025:1537]
                krel = arena[:, :, 1537:2049]
                rbuf = fsraw[:, :].rearrange("p (k c) -> p k c", k=KT)
                wvg2 = sub.enter_context(nc.sbuf_tensor(uname("wvg2"), [128, 2, KT, 128], BF16))
                wvfT = dram["wvfT"]

                def fetch_wv(mt, pp):
                    srcv = wvfT[:, :, 128 * mt:128 * mt + 128].transpose([1, 0, 2])
                    for kt in range(KT):
                        p.emit("gd", lambda e, kt=kt, pp=pp: e.dma_start(
                            wvg2[:, pp, kt, :], srcv[:, kt, :]),
                            writes=["wvg2%d" % pp], inc=16)

                NCH = S // PC
                for c in range(NCH):
                    cs = c * PC
                    if c == 0:
                        p.emit("v", lambda e: e.memset(xnp[:, :, 0:1], 0.0), writes=["xnp0"])
                    else:
                        p.emit("v", lambda e: e.tensor_copy(xnp[:, :, 0:1], xnp[:, :, PC:PC + 1]),
                               reads=["xnp"], writes=["xnp0"])
                    bcast_mr(cs, PC)
                    xnorm_apply(lambda kt: xnp[:, kt, 1:1 + PC], "xnp", cs, PC)
                    for kt in range(KT):
                        p.emit("v", lambda e, kt=kt: e.tensor_tensor(
                            t1[:, 0:PC], xnp[:, kt, 1:1 + PC], xnp[:, kt, 0:PC], ALU.subtract),
                            reads=["xnp", "xnp0"], writes=["t1"])
                        p.emit("v", lambda e, kt=kt: e.scalar_tensor_tensor(
                            xr[:, kt, :], t1[:, 0:PC], bias["mur"][:, kt:kt + 1], xnp[:, kt, 0:PC],
                            ALU.mult, ALU.add),
                            reads=["t1", "bias", "xnp", "xnp0"], writes=["xr"])
                        p.emit("v", lambda e, kt=kt: e.scalar_tensor_tensor(
                            xk[:, kt, :], t1[:, 0:PC], bias["muk"][:, kt:kt + 1], xnp[:, kt, 0:PC],
                            ALU.mult, ALU.add),
                            reads=["t1", "bias", "xnp", "xnp0"], writes=["xk"])
                    for mt in range(KT):
                        bk_ = "pb%d" % (mt % 2)
                        for kt in range(KT):
                            p.emit("t", lambda e, mt=mt, kt=kt: e.matmul(
                                pb[mt % 2][:, 0:PC], wbig[:, kt, 128 * mt:128 * mt + 128],
                                xr[:, kt, :], start=(kt == 0), stop=(kt == KT - 1)),
                                reads=["wbig", "xr"], writes=[bk_])
                        p.emit("s", lambda e, mt=mt: e.activation(rbuf[:, mt, :], pb[mt % 2][:, 0:PC],
                                                                  AF.Sigmoid),
                               reads=[bk_], writes=["rbuf"])
                    for mt in range(KT):
                        bk_ = "pb%d" % (2 + mt % 2)
                        for kt in range(KT):
                            p.emit("t", lambda e, mt=mt, kt=kt: e.matmul(
                                pb[2 + mt % 2][:, 0:PC], wbig[:, kt, 1024 + 128 * mt:1024 + 128 * mt + 128],
                                xk[:, kt, :], start=(kt == 0), stop=(kt == KT - 1)),
                                reads=["wbig", "xk"], writes=[bk_])
                        p.emit("v", lambda e, mt=mt: e.tensor_scalar_max(
                            krel[:, mt, :], pb[2 + mt % 2][:, 0:PC], 0.0),
                            reads=[bk_], writes=["krel"])
                        p.emit("v", lambda e, mt=mt: e.tensor_tensor(
                            krel[:, mt, :], krel[:, mt, :], krel[:, mt, :], ALU.mult),
                            reads=["krel"], writes=["krel"])
                    fetch_wv(0, 0)
                    for mt in range(KT):
                        pp2 = mt % 2
                        if mt + 1 < KT:
                            fetch_wv(mt + 1, (mt + 1) % 2)
                        elif c + 1 < NCH:
                            pass
                        bk_ = "pb%d" % (4 + mt % 2)
                        for kt in range(KT):
                            p.emit("t", lambda e, mt=mt, kt=kt, pp2=pp2: e.matmul(
                                pb[4 + mt % 2][:, 0:PC], wvg2[:, pp2, kt, :],
                                krel[:, kt, :], start=(kt == 0), stop=(kt == KT - 1)),
                                reads=["wvg2%d" % pp2, "krel"], writes=[bk_])
                        p.emit("v", lambda e, mt=mt: e.tensor_tensor(
                            t2[:, 0:PC], pb[4 + mt % 2][:, 0:PC], rbuf[:, mt, :], ALU.mult),
                            reads=[bk_, "rbuf"], writes=["t2"])
                        p.emit("v", lambda e, mt=mt, cs=cs: e.scalar_tensor_tensor(
                            resid[:, mt, cs:cs + PC], t2[:, 0:PC], 0.5, resid[:, mt, cs:cs + PC],
                            ALU.mult, ALU.add),
                            reads=["t2", "resid"], writes=["resid"])
            p.barrier()

        def l1attn(stage=99):
            sub = contextlib.ExitStack()
            with sub:
                ln_stats()
                wmini = sub.enter_context(nc.sbuf_tensor(uname("wmini"), [128, 2, 2048], BF16))
                load_weight(wmini[:, :, 0:256], "bmT", "wmini", 2)
                load_weight(wmini[:, :, 256:512], "amat", "wmini", 2)
                load_weight(wmini[:, :, 512:768], "amatT", "wmini", 2)
                load_weight(wmini[:, :, 768:1792], "cmT", "wmini", 2)
                load_weight(wbig[:, :, 0:256], "winT", "wbig", 8)
                wpow = sub.enter_context(nc.sbuf_tensor(uname("wpow"), [128, 2, 1536], BF16))
                cb = sub.enter_context(nc.sbuf_tensor(uname("cbuf"), [128, 2, 128], F32))
                hh = sub.enter_context(nc.sbuf_tensor(uname("hhbuf"), [128, 2, 8], F32))
                hh_bf = sub.enter_context(nc.sbuf_tensor(uname("hhbf"), [128, 2, 8], BF16))

                amat = wmini[:, :, 256:512]
                amatT = wmini[:, :, 512:768]
                bmT = wmini[:, :, 0:256]
                cmT = wmini[:, :, 768:1792]
                uby = fsraw[:, 0:2048].bitcast(BF16).rearrange("p (k c) -> p k c", k=2)
                ubuf = arena[:, 0:2, 0:2048]
                kstg = arena[:, 2:4, 0:512]
                hst = arena[:, 4:6, 0:128]
                hst3 = arena[:, 6:8, 0:1]

                for c in range(S // LNC):
                    cs = c * LNC
                    bcast_mr(cs, LNC)
                    xnorm_apply(lambda kt: xstat[:, kt, :], "xstat", cs, LNC)
                    for mt in range(2):
                        bk_ = "pb%d" % (mt % 2)
                        for kt in range(KT):
                            p.emit("t", lambda e, mt=mt, kt=kt: e.matmul(
                                pb[mt % 2][:, 0:LNC], wbig[:, kt, 128 * mt:128 * mt + 128],
                                xstat[:, kt, :], start=(kt == 0), stop=(kt == KT - 1)),
                                reads=["wbig", "xstat"], writes=[bk_])
                        p.emit("s", lambda e, mt=mt, cs=cs: e.activation(
                            ubuf[:, mt, cs:cs + LNC], pb[mt % 2][:, 0:LNC], AF.Identity,
                            bias=bias["binb"][:, mt:mt + 1]),
                            reads=[bk_, "bias"], writes=["ubuf"])
                if stage < 20:
                    p.barrier()
                    return
                for c in range(4):
                    cs = c * UC
                    for mt in range(2):
                        bk_ = "pb%d" % (2 + mt % 2)
                        for kt in range(2):
                            p.emit("t", lambda e, mt=mt, kt=kt, cs=cs: e.matmul(
                                pb[2 + mt % 2][:, 0:UC], bmT[:, kt, 128 * mt:128 * mt + 128],
                                ubuf[:, kt, cs:cs + UC], start=(kt == 0), stop=(kt == 1)),
                                reads=["wmini", "ubuf"], writes=[bk_])
                        p.emit("s", lambda e, mt=mt, cs=cs: e.activation(
                            uby[:, mt, cs:cs + UC], pb[2 + mt % 2][:, 0:UC], AF.Copy),
                            reads=[bk_], writes=["uby"])

                def square2(src, srcT, skeys, dst, dkey, dstT=None, dkeyT=None):
                    for mt in range(2):
                        bk_ = "pb%d" % (mt % 2)
                        for kt in range(2):
                            p.emit("t", lambda e, mt=mt, kt=kt: e.matmul(
                                pb[mt % 2][:, 0:256], srcT[:, kt, 128 * mt:128 * mt + 128],
                                src[:, kt, :], start=(kt == 0), stop=(kt == 1)),
                                reads=list(skeys), writes=[bk_])
                        p.emit("s", lambda e, mt=mt: e.activation(dst[:, mt, :], pb[mt % 2][:, 0:256], AF.Copy),
                               reads=[bk_], writes=[dkey])
                    if dstT is None:
                        return
                    for mt in range(2):
                        bk_ = "pb%d" % (2 + mt % 2)
                        for kt in range(2):
                            p.emit("t", lambda e, mt=mt, kt=kt: e.matmul(
                                pb[2 + mt % 2][:, 0:256], src[:, kt, 128 * mt:128 * mt + 128],
                                srcT[:, kt, :], start=(kt == 0), stop=(kt == 1)),
                                reads=list(skeys), writes=[bk_])
                        p.emit("s", lambda e, mt=mt: e.activation(dstT[:, mt, :], pb[2 + mt % 2][:, 0:256], AF.Copy),
                               reads=[bk_], writes=[dkeyT])

                if stage < 30:
                    p.barrier()
                    return
                p0 = wpow[:, :, 0:256]
                p0T = wpow[:, :, 256:512]
                p1 = wpow[:, :, 512:768]
                p1T = wpow[:, :, 768:1024]
                b16 = wpow[:, :, 1024:1280]
                b256 = wpow[:, :, 1280:1536]
                square2(amat, amatT, ["wmini"], p0, "p0", p0T, "p0T")
                square2(p0, p0T, ["p0", "p0T"], p1, "p1", p1T, "p1T")
                square2(p1, p1T, ["p1", "p1T"], p0, "p0", p0T, "p0T")
                square2(p0, p0T, ["p0", "p0T"], b16, "b16", p1T, "p1T")
                square2(b16, p1T, ["b16", "p1T"], p0, "p0", p0T, "p0T")
                square2(p0, p0T, ["p0", "p0T"], p1, "p1", p1T, "p1T")
                square2(p1, p1T, ["p1", "p1T"], p0, "p0", p0T, "p0T")
                square2(p0, p0T, ["p0", "p0T"], b256, "b256")

                def scan_local(buf_ap, buf_key, Lc, Cc, m_ap, m_keys, hst_ap, hst_key, f32buf):
                    for mt in range(2):
                        p.emit("s", lambda e, mt=mt: e.activation(
                            hst_ap[:, mt, 0:Cc], buf_ap[:, mt, 0::Lc], AF.Copy),
                            reads=[buf_key], writes=[hst_key])
                    for j in range(1, Lc):
                        for mt in range(2):
                            for kt in range(2):
                                p.emit("t", lambda e, mt=mt, kt=kt: e.matmul(
                                    pb[6][:, 128 * mt:128 * mt + Cc] if Cc == 128 else pb[6][:, 8 * mt:8 * mt + Cc],
                                    m_ap[:, kt, 128 * mt:128 * mt + 128], hst_ap[:, kt, 0:Cc],
                                    start=(kt == 0), stop=(kt == 1)),
                                    reads=list(m_keys) + [hst_key], writes=["pb6"])
                        for mt in range(2):
                            src = pb[6][:, 128 * mt:128 * mt + Cc] if Cc == 128 else pb[6][:, 8 * mt:8 * mt + Cc]
                            p.emit("v", lambda e, mt=mt, j=j, src=src: e.tensor_tensor(
                                buf_ap[:, mt, j::Lc], src, buf_ap[:, mt, j::Lc], ALU.add),
                                reads=["pb6", buf_key], writes=[buf_key])
                        for mt in range(2):
                            p.emit("s", lambda e, mt=mt, j=j: e.activation(
                                hst_ap[:, mt, 0:Cc], buf_ap[:, mt, j::Lc], AF.Copy),
                                reads=[buf_key], writes=[hst_key])

                def scan_expand(buf_ap, buf_key, Lc, Cc, m_ap, m_keys, w_ap, w_key):
                    for j in range(Lc):
                        for mt in range(2):
                            for kt in range(2):
                                p.emit("t", lambda e, mt=mt, kt=kt: e.matmul(
                                    pb[7][:, 128 * mt:128 * mt + Cc] if Cc == 128 else pb[7][:, 8 * mt:8 * mt + Cc],
                                    m_ap[:, kt, 128 * mt:128 * mt + 128], w_ap[:, kt, 0:Cc],
                                    start=(kt == 0), stop=(kt == 1)),
                                    reads=list(m_keys) + [w_key], writes=["pb7"])
                        for mt in range(2):
                            src = pb[7][:, 128 * mt:128 * mt + Cc] if Cc == 128 else pb[7][:, 8 * mt:8 * mt + Cc]
                            p.emit("v", lambda e, mt=mt, j=j, src=src: e.tensor_tensor(
                                buf_ap[:, mt, j::Lc], src, buf_ap[:, mt, j::Lc], ALU.add),
                                reads=["pb7", buf_key], writes=[buf_key])
                        if j < Lc - 1:
                            for mt in range(2):
                                src = pb[7][:, 128 * mt:128 * mt + Cc] if Cc == 128 else pb[7][:, 8 * mt:8 * mt + Cc]
                                p.emit("v", lambda e, mt=mt, src=src: e.tensor_copy(
                                    w_ap[:, mt, 0:Cc], src),
                                    reads=["pb7"], writes=[w_key])

                if stage < 40:
                    p.barrier()
                    return
                scan_local(uby, "uby", 16, 128, amat, ["wmini"], hst, "hst", None)
                if stage < 50:
                    p.barrier()
                    return
                for mt in range(2):
                    p.emit("v", lambda e, mt=mt: e.tensor_copy(cb[:, mt, :], uby[:, mt, 15::16]),
                           reads=["uby"], writes=["cbuf"])
                if stage < 51:
                    p.barrier()
                    return
                scan_local(cb, "cbuf", 16, 8, b16, ["b16"], hst, "hst", None)
                if stage < 52:
                    p.barrier()
                    return
                p.emit("v", lambda e: e.memset(hh[:, :, 0:1], 0.0), writes=["hh"])
                p.emit("v", lambda e: e.tensor_copy(hh[:, :, 1:2], cb[:, :, 15:16]),
                       reads=["cbuf"], writes=["hh"])
                for g2 in range(2, 8):
                    for mt in range(2):
                        p.emit("s", lambda e, mt=mt, g2=g2: e.activation(
                            hst3[:, mt, :], hh[:, mt, g2 - 1:g2], AF.Copy),
                            reads=["hh"], writes=["hst3"])
                    for mt in range(2):
                        for kt in range(2):
                            p.emit("t", lambda e, mt=mt, kt=kt: e.matmul(
                                pb[6][:, 8 * mt:8 * mt + 1], b256[:, kt, 128 * mt:128 * mt + 128],
                                hst3[:, kt, :], start=(kt == 0), stop=(kt == 1)),
                                reads=["b256", "hst3"], writes=["pb6"])
                    for mt in range(2):
                        p.emit("v", lambda e, mt=mt, g2=g2: e.tensor_tensor(
                            hh[:, mt, g2:g2 + 1], pb[6][:, 8 * mt:8 * mt + 1],
                            cb[:, mt, 16 * (g2 - 1) + 15:16 * (g2 - 1) + 16], ALU.add),
                            reads=["pb6", "cbuf"], writes=["hh"])
                if stage < 53:
                    p.barrier()
                    return
                p.emit("s", lambda e: e.activation(hh_bf[:, :, :], hh[:, :, :], AF.Copy),
                       reads=["hh"], writes=["hhbf"])
                scan_expand(cb, "cbuf", 16, 8, b16, ["b16"], hh_bf, "hhbf")
                if stage < 60:
                    p.barrier()
                    return
                p.emit("v", lambda e: e.memset(hst[:, :, 0:1], 0.0), writes=["hst"])
                for mt in range(2):
                    p.emit("s", lambda e, mt=mt: e.activation(hst[:, mt, 1:128], cb[:, mt, 0:127], AF.Copy),
                           reads=["cbuf"], writes=["hst"])
                scan_expand(uby, "uby", 16, 128, amat, ["wmini"], hst, "hst")

                if stage < 70:
                    p.barrier()
                    return
                for c in range(4):
                    cs = c * UC
                    for mt in range(2):
                        p.emit("s", lambda e, mt=mt, cs=cs: e.activation(
                            kstg[:, mt, :], uby[:, mt, cs:cs + UC], AF.Copy),
                            reads=["uby"], writes=["kstg"])
                    for mt in range(KT):
                        bk_ = "pb%d" % (mt % 2)
                        for kt in range(2):
                            p.emit("t", lambda e, mt=mt, kt=kt: e.matmul(
                                pb[mt % 2][:, 0:UC], cmT[:, kt, 128 * mt:128 * mt + 128],
                                kstg[:, kt, :], start=(kt == 0), stop=(kt == 1)),
                                reads=["wmini", "kstg"], writes=[bk_])
                        p.emit("v", lambda e, mt=mt, cs=cs: e.scalar_tensor_tensor(
                            resid[:, mt, cs:cs + UC], pb[mt % 2][:, 0:UC],
                            bias["ddv"][:, mt:mt + 1], resid[:, mt, cs:cs + UC],
                            ALU.add, ALU.add),
                            reads=[bk_, "bias", "resid"], writes=["resid"])
            p.barrier()

        subs = [l0ffn, l0attn, l0ffn, l1ffn,
                lambda: l1attn(stage=l1a_stage), l1ffn]
        for i in range(min(n_sub, 6)):
            subs[i]()

        doutT = dout.transpose([1, 0, 2])
        for kt in range(KT):
            p.emit("d", lambda e, kt=kt: e.dma_start(doutT[:, kt, :], resid[:, kt, :]),
                   reads=["resid"], inc=16)
        p._wait("t", ("d", p.cnt["d"]))
    library_overlay.lower_extended_insts(nc)
    return nc


_CACHE = {}


def kernel(**inputs):
    n_sub = int(inputs.pop("_n_sub", 6))
    l1a_stage = int(inputs.pop("_l1a_stage", 99))
    ck = (n_sub, l1a_stage)
    if ck not in _CACHE:
        _CACHE[ck] = build_program(n_sub, l1a_stage)
    nc = _CACHE[ck]

    def f32(a):
        return np.ascontiguousarray(np.asarray(a), dtype=np.float32)

    g = {k: f32(v) for k, v in inputs.items()}
    shared = {
        "wqT": f32(g["L0_attn_wq"].T).reshape(8, 128, 1024),
        "wkT": f32(g["L0_attn_wk"].T).reshape(8, 128, 1024),
        "wvT": f32(g["L0_attn_wv"].T).reshape(8, 128, 1024),
        "woT": f32(g["L0_attn_wo"].T).reshape(8, 128, 1024),
        "bq": f32(g["L0_attn_bq"].reshape(8, 128).T),
        "bk": f32(g["L0_attn_bk"].reshape(8, 128).T),
        "bo": f32(g["L0_attn_bo"].reshape(8, 128).T),
        "bvrow": f32(g["L0_attn_bv"].reshape(1, 1024)),
        "cw1": f32(g["L0_ffn_cn_w1"]).reshape(8, 128, 1024),
        "cw2": f32(g["L0_ffn_cn_w2"]).reshape(8, 128, 512),
        "cw3": f32(g["L0_ffn_cn_w3"]).reshape(4, 128, 256),
        "cw4": f32(g["L0_ffn_cn_w4"].reshape(2, 128).T),
        "gw1": f32(g["L0_ffn_gc_w1"]).reshape(8, 128, 512),
        "gw2": f32(g["L0_ffn_gc_w2"]).reshape(4, 128, 256),
        "gw3": f32(g["L0_ffn_gc_w3"].reshape(2, 128).T),
        "cb1": f32(g["L0_ffn_cn_b1"].reshape(8, 128).T),
        "cb2": f32(g["L0_ffn_cn_b2"].reshape(4, 128).T),
        "cb3": f32(g["L0_ffn_cn_b3"].reshape(2, 128).T),
        "cb4": f32(g["L0_ffn_cn_b4"].reshape(1, 1)),
        "gb1": f32(g["L0_ffn_gc_b1"].reshape(4, 128).T),
        "gb2": f32(g["L0_ffn_gc_b2"].reshape(2, 128).T),
        "gb3": f32(g["L0_ffn_gc_b3"].reshape(1, 1)),
        "arr": f32(g["L0_ffn_adaptive_range"].reshape(1, 1)),
        "wrT": f32(g["L1_ffn_W_r"].T).reshape(8, 128, 1024),
        "wkfT": f32(g["L1_ffn_W_k"].T).reshape(8, 128, 1024),
        "wvfT": f32(g["L1_ffn_W_v"].T).reshape(8, 128, 1024),
        "mur": f32(g["L1_ffn_mu_r"].reshape(8, 128).T),
        "muk": f32(g["L1_ffn_mu_k"].reshape(8, 128).T),
        "winT": f32(g["L1_attn_W_in"].T).reshape(8, 128, 256),
        "bmT": f32(g["L1_attn_Bm"].T).reshape(2, 128, 256),
        "amat": f32(g["L1_attn_A"]).reshape(2, 128, 256),
        "amatT": f32(g["L1_attn_A"].T).reshape(2, 128, 256),
        "cmT": f32(g["L1_attn_C"].T).reshape(2, 128, 1024),
        "binb": f32(g["L1_attn_b_in"].reshape(2, 128).T),
        "ddv": f32(g["L1_attn_D"].reshape(8, 128).T),
    }
    x = g["L0_attn_x"]
    in_maps = []
    for b in range(8):
        m = dict(shared)
        m["xT"] = f32(x[b].T).reshape(8, 128, 2048)
        in_maps.append(m)
    res = run_bass_kernel_spmd(nc, in_maps, core_ids=list(range(8)))
    outs = []
    for r in res.results:
        o = np.asarray(r["out"], dtype=np.float32).reshape(1024, 2048)
        outs.append(o.T)
    return np.stack(outs, axis=0)


if __name__ == "__main__":
    import reference as _ref
    inp = {k: np.asarray(v) for k, v in _ref.setup_inputs(0).items()}
    out = kernel(**inp)
    print(out.shape, out.dtype)
